# version 19
# baseline (speedup 1.0000x reference)
import contextlib
import numpy as np
import ml_dtypes
import concourse.bass as bass
import concourse.mybir as mybir
from concourse.bass_utils import run_bass_kernel_spmd

F32 = mybir.dt.float32
F32R = mybir.dt.float32r
BF16 = mybir.dt.bfloat16
AF = mybir.ActivationFunctionType
ALU = mybir.AluOpType
AX = mybir.AxisListType

B, S, D = 8, 2048, 1024
NH, HD, RD = 16, 64, 32
NT = S // 128
NC = D // 128
NP = 8
NLEAF, NNODE = 16, 15
BIG = 1.0e9
EPS = 1e-5


def _chunkmajor(w):
    n = w.shape[1]
    return np.ascontiguousarray(
        w.reshape(NC, 128, n).transpose(1, 0, 2).reshape(128, NC * n))


def host_prep(inp):
    h = {}
    w_qkvT = np.ascontiguousarray(np.asarray(inp["L0_attn_w_qkv"]).T)
    qcols = np.concatenate([w_qkvT[:, mp*768:mp*768+256] for mp in range(4)], 1)
    vcols = np.concatenate([w_qkvT[:, mp*768+256:mp*768+512] for mp in range(4)], 1)
    kcols = np.concatenate([w_qkvT[:, mp*768+512:mp*768+768] for mp in range(4)], 1)
    h["w_q"] = _chunkmajor(qcols)
    h["w_k"] = _chunkmajor(kcols)
    h["w_v"] = _chunkmajor(vcols)
    h["w_o"] = _chunkmajor(
        np.ascontiguousarray(np.asarray(inp["L0_attn_w_out"]).T))

    pos = np.arange(S, dtype=np.float64)
    invf = 1.0 / 10000.0 ** (np.arange(0, RD, 2, dtype=np.float64) / RD)
    ang = np.outer(invf, pos)
    cos = np.ones((64, S), np.float64)
    sin = np.zeros((64, S), np.float64)
    for d in range(RD):
        cos[d] = np.cos(ang[d // 2])
        sin[d] = np.sin(ang[d // 2])
    h["cos"] = np.tile(cos, (2, 1)).astype(np.float32)
    h["sin"] = np.tile(sin, (2, 1)).astype(np.float32)
    P = np.zeros((128, 128), np.float32)
    for base in (0, 64):
        for j in range(RD // 2):
            P[base + 2*j + 1, base + 2*j] = -1.0
            P[base + 2*j, base + 2*j + 1] = 1.0
    h["prope"] = P
    h["ident"] = np.eye(128, dtype=np.float32)

    m = np.zeros((128, 4, 2, 512), np.float32)
    for r in range(4):
        for ck in range(128):
            m[ck, r, :, :] = (np.arange(512) >= 128*r + ck)[None, :]
    h["mask01"] = m.reshape(128, 4096)

    e = np.zeros((128, NP, 16), np.float32)
    for p in range(NP):
        e[0:64, p, 2*p] = 1.0
        e[64:128, p, 2*p+1] = 1.0
    h["enorm"] = e.reshape(128, NP * 16)

    A = np.zeros((NNODE, NLEAF), np.float32)
    thr = np.zeros((NLEAF, 1), np.float32)
    for leaf in range(NLEAF):
        bits = [(leaf >> (3 - t)) & 1 for t in range(4)]
        cur, nleft = 0, 0
        for t in range(4):
            A[cur, leaf] = 1.0 if bits[t] else -1.0
            nleft += 0 if bits[t] else 1
            cur = 2*cur + 1 + bits[t]
        thr[leaf, 0] = 3.5 - nleft
    h["aroute"] = A
    h["thr"] = thr

    node_wT = np.concatenate(
        [np.asarray(inp[f"L0_ffn_{f}_node_w"]).T for f in "qkv"], axis=1)
    h["node_w"] = _chunkmajor(node_wT)
    h["node_b"] = np.ascontiguousarray(np.stack(
        [np.asarray(inp[f"L0_ffn_{f}_node_b"]).astype(np.float32)
         for f in "qkv"], axis=1))

    bige = np.zeros((16, NP, 128), np.float32)
    for p in range(NP):
        bige[2*p, p, 0:64] = BIG
        bige[2*p+1, p, 64:128] = BIG
    h["bige"] = bige.reshape(16, NP * 128)
    half01 = np.zeros((2, 128), np.float32)
    half01[0, 0:64] = 1.0
    half01[1, 64:128] = 1.0
    h["half01"] = half01
    ebc = np.zeros((16, NP * 128), np.float32)
    for p in range(NP):
        ebc[p, p*128:p*128 + 64] = 1.0
        ebc[8 + p, p*128 + 64:p*128 + 128] = 1.0
    h["ebc"] = ebc
    h["onescol"] = np.ones((128, 1), np.float32)
    h["onesrow"] = np.ones((1, 128), np.float32)
    h["one11"] = np.ones((1, 1), np.float32)

    for f in "qkv":
        w1 = np.asarray(inp[f"L0_ffn_{f}_w1"])
        w1T = np.ascontiguousarray(w1.transpose(1, 0, 2).reshape(D, 1024))
        h[f"w1_{f}"] = _chunkmajor(w1T)
        w2 = np.asarray(inp[f"L0_ffn_{f}_w2"])
        h[f"w2_{f}"] = np.ascontiguousarray(
            w2.reshape(NP, 128, 1024).transpose(1, 0, 2).reshape(128, NP * 1024))
        b1 = np.asarray(inp[f"L0_ffn_{f}_b1"])
        h[f"b1_{f}"] = np.ascontiguousarray(
            b1.reshape(NP, 128).T).astype(np.float32)
        h[f"b2_{f}"] = np.asarray(inp[f"L0_ffn_{f}_b2"]).astype(np.float32)
    return h


HOSTSHAPES = {
    "w_q": [128, NC * 1024], "w_k": [128, NC * 1024],
    "w_v": [128, NC * 1024], "w_o": [128, NC * 1024],
    "cos": [128, S], "sin": [128, S], "prope": [128, 128], "ident": [128, 128],
    "mask01": [128, 4096], "enorm": [128, NP * 16],
    "aroute": [NNODE, NLEAF], "thr": [NLEAF, 1],
    "node_w": [128, NC * 45], "node_b": [NNODE, 3], "bige": [16, NP * 128],
    "half01": [2, 128], "onescol": [128, 1], "onesrow": [1, 128], "one11": [1, 1],
    "ebc": [16, 8 * 128],
}
for _f in "qkv":
    HOSTSHAPES[f"w1_{_f}"] = [128, 8192]
    HOSTSHAPES[f"w2_{_f}"] = [128, 8192]
    HOSTSHAPES[f"b1_{_f}"] = [128, 8]
    HOSTSHAPES[f"b2_{_f}"] = [16, 1024]

STRIPS = [(qc, kt) for qc in range(4) for kt in range(4 * qc + 4)]


def build_program(debug=False):
    nc = bass.Bass("TRN2", target_bir_lowering=False,
                   detect_race_conditions=False)

    def din(name, shape, dt=F32):
        return nc.dram_tensor(name, shape, dt, kind="ExternalInput").ap()

    x_h = din("x", [S, D])
    hin = {k: din(k, v) for k, v in HOSTSHAPES.items()}
    y_h = nc.dram_tensor("y", [S, D], F32, kind="ExternalOutput").ap()

    sk = "ExternalOutput" if debug else "Internal"
    attn_hbm = nc.dram_tensor("attn_sc", [S, D], F32, kind=sk).ap()
    q2_hbm = nc.dram_tensor("q2_sc", [128, NC * 2048], BF16, kind=sk).ap()
    k2_hbm = nc.dram_tensor("k2_sc", [128, NC * 2048], BF16, kind=sk).ap()
    v2_hbm = nc.dram_tensor("v2_sc", [128, NT * 1024], BF16, kind=sk).ap()
    arawA_hbm = nc.dram_tensor("arawA_sc", [64, NP * S], BF16, kind=sk).ap()
    arawB_hbm = nc.dram_tensor("arawB_sc", [64, NP * S], BF16, kind=sk).ap()
    sums_hbm = nc.dram_tensor("sums_sc", [2, NP, S], F32, kind=sk).ap()
    rsb_hbm = nc.dram_tensor("rsb_sc", [NNODE, 3 * S], BF16, kind="Internal").ap()

    def dbg(name, shape, dt=F32):
        if not debug:
            return None
        return nc.dram_tensor("dbg_" + name, shape, dt,
                              kind="ExternalOutput").ap()

    d_xnfm = dbg("xnfm", [128, NC * 2048], BF16)
    d_xnlo = dbg("xnlo", [128, NC * 2048], BF16)
    d_rsb = dbg("rsb", [NNODE, 3 * S], BF16)
    d_q = dbg("q", [128, NP * 2048], BF16)
    d_k = dbg("k", [128, NP * 2048], BF16)
    d_v = dbg("v", [128, NT * 1040], BF16)
    d_biasg = dbg("biasg", [128, 1])
    d_anorm = dbg("anorm", [128, NP * 2048], BF16)
    d_sel = dbg("sel", [16, 3 * S])
    d_hid = dbg("hid", [128, NP * 2048], BF16)
    d_bias2 = dbg("bias2", [128, 1])
    d_stats = dbg("stats", [128, 80])
    d_hstage = dbg("hstage", [128, 4096], BF16)
    d_xstage = dbg("xstage", [128, 4096])

    cnt = {}

    with contextlib.ExitStack() as _top:
        def _sb(name, shape, dt, side="left"):
            return _top.enter_context(nc.sbuf_tensor("sb_" + name, shape, dt, side=side))
        ident32 = _sb("ident32", [128, 128], F32)
        ident16 = _sb("ident16", [128, 128], BF16)
        prope = _sb("prope", [128, 128], BF16)
        enorm = _sb("enorm", [128, NP * 16], BF16)
        aroute = _sb("aroute", [NNODE, NLEAF], BF16)
        thr = _sb("thr", [NLEAF, 1], F32)
        nodeb = _sb("nodeb", [NNODE, 3], F32)
        bige = _sb("bige", [16, NP * 128], BF16)
        half01 = _sb("half01", [2, 128], F32R)
        onescol = _sb("onescol", [128, 1], BF16)
        onesrow = _sb("onesrow", [1, 128], BF16)
        one11 = _sb("one11", [1, 1], F32)
        nodewh = _sb("nodewh", [128, NC * 45], BF16)
        nodewl = _sb("nodewl", [128, NC * 45], BF16)
        biasg = _sb("biasg", [128, 1], F32)
        bias2 = _sb("bias2", [128, 1], F32)
        tiny = _sb("tiny", [128, 16], F32)
        tinyr = _sb("tinyr", [128, 4], BF16)
        pa = _top.enter_context(nc.psum_tensor("pa", [128, 2048], F32))
        pb = _top.enter_context(nc.psum_tensor("pb", [128, 2048], F32))
        sA = _top.enter_context(nc.semaphore("sA"))
        sB = _top.enter_context(nc.semaphore("sB"))
        sC = _top.enter_context(nc.semaphore("sC"))
        sD = _top.enter_context(nc.semaphore("sD"))
        sE = _top.enter_context(nc.semaphore("sE"))
        sF = _top.enter_context(nc.semaphore("sF"))
        sG = _top.enter_context(nc.semaphore("sG"))
        sH = _top.enter_context(nc.semaphore("sH"))
        s_w = _top.enter_context(nc.semaphore("s_w"))
        s_x = _top.enter_context(nc.semaphore("s_x"))
        s_tr = _top.enter_context(nc.semaphore("s_tr"))
        s_st = _top.enter_context(nc.semaphore("s_st"))
        s_out = _top.enter_context(nc.semaphore("s_out"))
        s_p = _top.enter_context(nc.semaphore("s_p"))
        SEM = {"sA": sA, "sB": sB, "sC": sC, "sD": sD, "sE": sE, "sF": sF,
               "sG": sG, "sH": sH, "s_w": s_w, "s_x": s_x, "s_tr": s_tr,
               "s_st": s_st, "s_out": s_out, "s_p": s_p}
        for k in SEM:
            cnt[k] = 0

        def INC(instr, name, k=1):
            instr.then_inc(SEM[name], k)
            cnt[name] += k
            return instr

        def wload(e, dst, src):
            INC(e.dma_start(out=dst, in_=src), "s_w", 16)

        def ostore(e, dst, src):
            INC(e.dma_start(out=dst, in_=src), "s_out", 16)

        with nc.sbuf_tensor("nodew32", [128, NC * 45], F32,
                            side="right") as nodew32:
            with nc.Block(no_gpsimd_drain=True) as blk:
                @blk.gpsimd
                def _(e):
                    for dst, key in (
                        (prope, "prope"), (enorm, "enorm"), (aroute, "aroute"),
                        (bige, "bige"), (onescol, "onescol"),
                        (ident32, "ident"), (ident16, "ident"), (thr, "thr"),
                        (nodeb, "node_b"), (half01, "half01"),
                        (onesrow, "onesrow"), (one11, "one11"),
                        (nodew32, "node_w"),
                    ):
                        wload(e, dst[:], hin[key])
                    e.wait_ge(s_w, 16 * (cnt["s_w"] // 16))

            with nc.Block(no_gpsimd_drain=True) as blk:
                @blk.scalar
                def _(e):
                    e.copy(nodewh[:], nodew32[:])
            with nc.Block(no_gpsimd_drain=True) as blk:
                @blk.vector
                def _(e):
                    e.tensor_sub(nodewl[:], nodew32[:], nodewh[:])
                    e.memset(tiny[:, 15:16], EPS)

        with nc.sbuf_tensor("xnfm", [128, NC * 2048], BF16, side="left") as xnfm:
            with contextlib.ExitStack() as stW:
                wqh = stW.enter_context(nc.sbuf_tensor(
                    "wqh", [128, NC * 1024], BF16, side="right"))
                wv = stW.enter_context(nc.sbuf_tensor(
                    "wv", [128, NC * 1024], BF16, side="right"))
                cossb = stW.enter_context(nc.sbuf_tensor(
                    "cossb", [128, S], BF16, side="right"))
                sinsb = stW.enter_context(nc.sbuf_tensor(
                    "sinsb", [128, S], BF16, side="right"))
                with (
                    nc.sbuf_tensor("rsb", [NNODE, 3 * S], BF16,
                                   side="right") as rsb,
                    nc.sbuf_tensor("xnlo", [128, NC * 2048], BF16,
                                   side="right") as xnlo,
                    nc.sbuf_tensor("xstage", [128, 8 * 1024], F32,
                                   side="right") as xstage,
                    nc.sbuf_tensor("xnf32", [128, 1024], F32,
                                   side="right") as xnf32,
                    nc.sbuf_tensor("hstage", [128, 8 * 1024], BF16,
                                   side="right") as hstage,
                    nc.sbuf_tensor("lstage", [128, 8 * 1024], BF16,
                                   side="right") as lstage,
                    nc.sbuf_tensor("stats", [128, 96], F32,
                                   side="right") as stats,
                ):
                    with nc.Block(no_gpsimd_drain=True) as blk:
                        @blk.gpsimd
                        def _(e):
                            wload(e, wqh[:], hin["w_q"])
                            wload(e, wv[:], hin["w_v"])
                            wload(e, cossb[:], hin["cos"])
                            wload(e, sinsb[:], hin["sin"])

                    bB0, bC0, bE0, bF0, bG0, bH0 = (
                        cnt["sB"], cnt["sC"], cnt["sE"], cnt["sF"],
                        cnt["sG"], cnt["sH"])
                    bX0 = cnt["s_x"]
                    bP0 = cnt.get("s_p", 0)
                    with nc.Block(no_gpsimd_drain=True) as blk:
                        @blk.sync
                        def _(e):
                            for g in range(4):
                                for i in range(4):
                                    t = g * 4 + i
                                    gs = (g % 2) * 4096
                                    if g >= 2 and i == 0:
                                        e.wait_ge(sB, bB0 + 4 * (g - 1))
                                        e.wait_ge(sC, bC0 + 4 * (g - 1))
                                    INC(e.dma_start(
                                        out=xstage[:, gs + i*1024:
                                                   gs + (i+1)*1024],
                                        in_=x_h[t*128:(t+1)*128, :]),
                                        "s_x", 16)

                        @blk.scalar
                        def _(e):
                            for g in range(4):
                                t0 = g * 4
                                gs = (g % 2) * 4096
                                sc = 64 + (g % 2) * 16
                                e.wait_ge(s_x, bX0 + 64 * (g + 1))
                                for i in range(4):
                                    t = t0 + i
                                    ins = e.activation(
                                        xnf32[:],
                                        xstage[:, gs + i*1024:gs + (i+1)*1024],
                                        AF.Square,
                                        accum_out=stats[:, 16+t:17+t])
                                    if i == 3:
                                        INC(ins, "sH")
                                e.wait_ge(sG, bG0 + 3 * g + 1)
                                INC(e.activation(stats[:, sc:sc+4],
                                                 stats[:, t0:t0+4],
                                                 AF.Identity,
                                                 scale=-1.0 / D), "sH")
                                e.wait_ge(sG, bG0 + 3 * g + 2)
                                INC(e.activation(stats[:, sc+8:sc+12],
                                                 stats[:, sc+12:sc+16],
                                                 AF.Sqrt,
                                                 bias=tiny[:, 15:16]), "sH")
                                e.wait_ge(sG, bG0 + 3 * g + 3)
                                e.wait_ge(s_p, bP0 + 2 * g + 2)
                                for i in range(4):
                                    t = t0 + i
                                    INC(e.activation(
                                        hstage[:, gs + i*1024:gs + (i+1)*1024],
                                        xstage[:, gs + i*1024:gs + (i+1)*1024],
                                        AF.Identity,
                                        bias=stats[:, 48+t:49+t],
                                        scale=stats[:, 32+t:33+t]), "sB")
                                for i in range(4):
                                    t = t0 + i
                                    gi4 = g * 4 + i
                                    ps = pa if gi4 % 2 == 0 else pb
                                    e.wait_ge(sE, bE0 + gi4 + 1)
                                    INC(e.copy(xnfm.ap().rearrange(
                                        "p (c n) -> p c n",
                                        c=NC)[:, :, t*128:(t+1)*128],
                                        ps.ap()[:, 0:1024].rearrange(
                                            "p (c n) -> p c n", c=NC)), "sF")

                        @blk.gpsimd
                        def _(e):
                            for g in range(4):
                                t0 = g * 4
                                sc = 64 + (g % 2) * 16
                                e.wait_ge(sH, bH0 + 3 * g + 2)
                                INC(e.tensor_mul(stats[:, sc+4:sc+8],
                                                 stats[:, sc:sc+4],
                                                 stats[:, sc:sc+4]), "s_p")
                                e.wait_ge(sG, bG0 + 3 * g + 3)
                                INC(e.tensor_mul(stats[:, 48+t0:52+t0],
                                                 stats[:, sc:sc+4],
                                                 stats[:, 32+t0:36+t0]),
                                    "s_p")

                        @blk.vector
                        def _(e):
                            for g in range(4):
                                t0 = g * 4
                                gs = (g % 2) * 4096
                                sc = 64 + (g % 2) * 16
                                r4 = stats[:, 32+t0:36+t0]
                                e.wait_ge(s_x, bX0 + 64 * (g + 1))
                                for i in range(4):
                                    t = t0 + i
                                    ins = e.reduce_sum(
                                        stats[:, t:t+1],
                                        xstage[:, gs + i*1024:gs + (i+1)*1024],
                                        axis=AX.X)
                                    if i == 3:
                                        INC(ins, "sG")
                                e.wait_ge(sH, bH0 + 3 * g + 1)
                                e.wait_ge(s_p, bP0 + 2 * g + 1)
                                INC(e.scalar_tensor_tensor(
                                    stats[:, sc+12:sc+16],
                                    stats[:, 16+t0:20+t0], 1.0 / D,
                                    stats[:, sc+4:sc+8],
                                    ALU.mult, ALU.subtract), "sG")
                                e.wait_ge(sH, bH0 + 3 * g + 3)
                                INC(e.reciprocal(r4, stats[:, sc+8:sc+12]),
                                    "sG")
                                e.wait_ge(s_p, bP0 + 2 * g + 2)
                                e.drain()
                                for i in range(4):
                                    t = t0 + i
                                    gi4 = g * 4 + i
                                    ps = pa if gi4 % 2 == 0 else pb
                                    e.tensor_scalar(
                                        xnf32[:],
                                        xstage[:, gs + i*1024:gs + (i+1)*1024],
                                        stats[:, 32+t:33+t],
                                        stats[:, 48+t:49+t],
                                        ALU.mult, ALU.add)
                                    e.wait_ge(sB, bB0 + 4 * g + i + 1)
                                    INC(e.tensor_sub(
                                        lstage[:, gs + i*1024:gs + (i+1)*1024],
                                        xnf32[:],
                                        hstage[:, gs + i*1024:
                                               gs + (i+1)*1024]), "sC")
                                    e.wait_ge(sE, bE0 + gi4 + 1)
                                    INC(e.tensor_copy(xnlo.ap().rearrange(
                                        "p (c n) -> p c n",
                                        c=NC)[:, :, t*128:(t+1)*128],
                                        ps.ap()[:, 1024:2048].rearrange(
                                            "p (c n) -> p c n", c=NC)), "sF")

                        @blk.tensor
                        def _(e):
                            for g in range(4):
                                t0 = g * 4
                                gs = (g % 2) * 4096
                                for i in range(4):
                                    gi4 = g * 4 + i
                                    ps = pa if gi4 % 2 == 0 else pb
                                    e.wait_ge(sB, bB0 + 4 * g + i + 1)
                                    e.wait_ge(sC, bC0 + 4 * g + i + 1)
                                    if gi4 >= 2:
                                        e.wait_ge(sF, bF0 + 2 * (gi4 - 1))
                                    last = None
                                    for half, stg in ((0, hstage),
                                                      (1, lstage)):
                                        for dc in range(NC):
                                            last = e.matmul(
                                                ps[:, half*1024 + dc*128:
                                                   half*1024 + (dc+1)*128],
                                                stg[:, gs + i*1024 + dc*128:
                                                    gs + i*1024 + (dc+1)*128],
                                                ident16[:],
                                                start=True, stop=True)
                                    INC(last, "sE")

                    bE, bF = cnt["sE"], cnt["sF"]
                    with nc.Block(no_gpsimd_drain=True) as blk:
                        @blk.tensor
                        def _(e, bE=bE, bF=bF):
                            for f in range(3):
                                if f > 0:
                                    e.wait_ge(sF, bF + f)
                                last = None
                                for ps, (wt, xt) in enumerate(
                                        ((nodewh, xnfm), (nodewl, xnfm),
                                         (nodewh, xnlo))):
                                    for dc in range(NC):
                                        st = (ps == 0 and dc == 0)
                                        sp = (ps == 2 and dc == NC - 1)
                                        for gq in range(4):
                                            last = e.matmul(
                                                pa[0:NNODE, gq*512:(gq+1)*512],
                                                wt[:, dc*45 + f*15:
                                                   dc*45 + (f+1)*15],
                                                xt[:, dc*2048 + gq*512:
                                                   dc*2048 + (gq+1)*512],
                                                start=st, stop=sp)
                                INC(last, "sE")

                        @blk.vector
                        def _(e, bE=bE, bF=bF):
                            for f in range(3):
                                e.wait_ge(sE, bE + f + 1)
                                INC(e.tensor_scalar(
                                    rsb[:, f*S:(f+1)*S], pa[0:NNODE, :],
                                    nodeb[:, f:f+1], 0.0,
                                    ALU.add, ALU.is_ge), "sF")

                    with nc.Block(no_gpsimd_drain=True) as blk:
                        @blk.gpsimd
                        def _(e):
                            wload(e, rsb_hbm, rsb[:])
                            if debug:
                                ostore(e, d_xnfm, xnfm[:])
                                ostore(e, d_xnlo, xnlo[:])
                                ostore(e, d_rsb, rsb[:])
                            e.wait_ge(s_w, cnt["s_w"])

                with (
                    nc.sbuf_tensor("qsb", [128, NP * 2048], BF16,
                                   side="left") as qsb,
                    nc.sbuf_tensor("ksb", [128, NP * 2048], BF16,
                                   side="left") as ksb,
                    nc.sbuf_tensor("vsb", [128, NT * 1040], BF16,
                                   side="left") as vsb,
                ):
                    bA, bB, bC, bD = cnt["sA"], cnt["sB"], cnt["sC"], cnt["sD"]
                    sw_q = cnt["s_w"]
                    NU = 32
                    SLOT = (pa.ap()[:, 0:1024], pa.ap()[:, 1024:2048],
                            pb.ap()[:, 0:1024])
                    ROT = pb.ap()[:, 1024:2048]
                    with nc.Block(no_gpsimd_drain=True) as blk:
                        @blk.gpsimd
                        def _(e, bA=bA):
                            e.wait_ge(sA, bA + 16)
                            wload(e, wqh[:], hin["w_k"])

                        @blk.tensor
                        def _(e, bA=bA, bB=bB, bC=bC, sw_q=sw_q,
                              sw_k=cnt["s_w"]):
                            e.wait_ge(s_w, sw_q)

                            def emit_mm(u):
                                if u == 16:
                                    e.wait_ge(s_w, sw_k)
                                cc, h = u // 2, u % 2
                                pcc = cc % 8
                                sp = SLOT[u % 3]
                                if u >= 3:
                                    e.wait_ge(sB, bB + u - 2)
                                last = None
                                for dc in range(NC):
                                    for g2 in range(2):
                                        last = e.matmul(
                                            sp[:, g2*512:(g2+1)*512],
                                            wqh[:, dc*1024 + pcc*128:
                                                dc*1024 + (pcc+1)*128],
                                            xnfm[:, dc*2048 + h*1024
                                                 + g2*512:
                                                 dc*2048 + h*1024
                                                 + (g2+1)*512],
                                            start=(dc == 0), stop=(dc == NC-1))
                                INC(last, "sA")

                            def emit_rot(u):
                                cc, h = u // 2, u % 2
                                dstsb = qsb if cc < 8 else ksb
                                pcc = cc % 8
                                base = pcc*2048 + h*1024
                                e.wait_ge(sB, bB + u + 1)
                                if u >= 1:
                                    e.wait_ge(sC, bC + 2 * u)
                                last = None
                                for j in range(2):
                                    last = e.matmul(
                                        ROT[:, j*512:(j+1)*512], prope[:],
                                        dstsb[:, base + j*512:
                                              base + (j+1)*512],
                                        start=True, stop=True)
                                INC(last, "sD")

                            for u in range(NU + 1):
                                if u < NU:
                                    emit_mm(u)
                                if u >= 1:
                                    emit_rot(u - 1)

                        @blk.scalar
                        def _(e, bA=bA):
                            for u in range(NU):
                                cc, h = u // 2, u % 2
                                dstsb = qsb if cc < 8 else ksb
                                pcc = cc % 8
                                base = pcc*2048 + h*1024
                                e.wait_ge(sA, bA + u + 1)
                                INC(e.copy(dstsb[:, base:base + 1024],
                                           SLOT[u % 3]), "sB")

                        @blk.vector
                        def _(e, bD=bD):
                            for u in range(NU):
                                cc, h = u // 2, u % 2
                                dstsb = qsb if cc < 8 else ksb
                                pcc = cc % 8
                                sl = dstsb[:, pcc*2048 + h*1024:
                                           pcc*2048 + (h+1)*1024]
                                e.wait_ge(sD, bD + u + 1)
                                INC(e.tensor_mul(
                                    ROT[:, :], ROT[:, :],
                                    sinsb[:, h*1024:(h+1)*1024]), "sC")
                                e.tensor_mul(sl, sl,
                                             cossb[:, h*1024:(h+1)*1024])
                                INC(e.tensor_add(sl, sl, ROT[:, :]), "sC")

                    with nc.sbuf_tensor("sqsb", [128, 2 * 2048], BF16,
                                        side="right") as sqsb:
                        bA, bB, bC, bD, bE = (cnt["sA"], cnt["sB"], cnt["sC"],
                                              cnt["sD"], cnt["sE"])
                        with nc.Block(no_gpsimd_drain=True) as blk:
                            @blk.vector
                            def _(e, bB=bB):
                                for tt in range(NT):
                                    vv = vsb.ap()[:, tt*1040:(tt+1)*1040]
                                    vv3 = vv.rearrange("p (h z) -> p h z", h=16)
                                    e.memset(vv3[:, :, 64:65], 1.0)
                                for cc in range(16):
                                    srcsb = qsb if cc < 8 else ksb
                                    pcc = cc % 8
                                    sl = srcsb[:, pcc*2048:(pcc+1)*2048]
                                    if cc >= 2:
                                        e.wait_ge(sB, bB + cc - 1)
                                    INC(e.tensor_mul(
                                        sqsb[:, (cc % 2)*2048:
                                             (cc % 2 + 1)*2048], sl, sl), "sA")
                                    if cc == 9:
                                        e.wait_ge(sB, bB + 8)
                                        INC(e.reduce_max(tiny[0:16, 0:1],
                                                         pb[0:16, :],
                                                         axis=AX.X), "sC")
                                e.wait_ge(sB, bB + 16)
                                INC(e.reduce_max(tiny[0:16, 1:2], pb[0:16, :],
                                                 axis=AX.X), "sC")

                            @blk.tensor
                            def _(e, bA=bA, bC=bC, bE=bE):
                                for tt in range(NT):
                                    half = (tt % 2) * 1024
                                    if tt >= 2:
                                        e.wait_ge(sE, bE + tt - 1)
                                    last = None
                                    for dc in range(NC):
                                        for gq in range(2):
                                            last = e.matmul(
                                                pa[:, half + gq*512:
                                                   half + (gq+1)*512],
                                                xnfm[:, dc*2048 + tt*128:
                                                     dc*2048 + (tt+1)*128],
                                                wv[:, dc*1024 + gq*512:
                                                   dc*1024 + (gq+1)*512],
                                                start=(dc == 0),
                                                stop=(dc == NC-1))
                                    INC(last, "sF")
                                    cc = tt
                                    if cc == 8:
                                        e.wait_ge(sC, bC + 1)
                                    e.wait_ge(sA, bA + cc + 1)
                                    last = None
                                    for gq in range(4):
                                        last = e.matmul(
                                            pb[0:16, gq*512:(gq+1)*512],
                                            enorm[:, (cc % 8)*16:
                                                  (cc % 8 + 1)*16],
                                            sqsb[:, (cc % 2)*2048 + gq*512:
                                                 (cc % 2)*2048 + (gq+1)*512],
                                            start=(cc % 8 == 0),
                                            stop=(cc % 8 == 7))
                                    INC(last, "sB")

                            @blk.scalar
                            def _(e):
                                bF = cnt["sF"] - NT
                                for tt in range(NT):
                                    half = (tt % 2) * 1024
                                    e.wait_ge(sF, bF + tt + 1)
                                    src = pa.ap()[:, half:half+1024].rearrange(
                                        "p (h z) -> p h z", h=16)
                                    vv = vsb.ap()[:, tt*1040:(tt+1)*1040]
                                    dst = vv.rearrange(
                                        "p (h z) -> p h z", h=16)[:, :, 0:64]
                                    INC(e.copy(dst, src), "sE")

                    with nc.Block(no_gpsimd_drain=True) as blk:
                        @blk.scalar
                        def _(e):
                            e.activation(tiny[0:16, 2:3], tiny[0:16, 0:1],
                                         AF.Sqrt)
                            e.activation(tiny[0:16, 3:4], tiny[0:16, 1:2],
                                         AF.Sqrt)
                    with nc.Block(no_gpsimd_drain=True) as blk:
                        @blk.vector
                        def _(e):
                            e.tensor_mul(tiny[0:16, 4:5], tiny[0:16, 2:3],
                                         tiny[0:16, 3:4])
                    with nc.Block(no_gpsimd_drain=True) as blk:
                        @blk.scalar
                        def _(e):
                            e.copy(tinyr[0:16, 0:1], tiny[0:16, 4:5])
                    with nc.Block(no_gpsimd_drain=True) as blk:
                        @blk.tensor
                        def _(e):
                            e.matmul(pa[0:1, 0:16], tinyr[0:16, 0:1],
                                     ident16[0:16, 0:16], start=True,
                                     stop=True)
                    with nc.Block(no_gpsimd_drain=True) as blk:
                        @blk.vector
                        def _(e):
                            e.reduce_max(tiny[0:1, 5:6], pa[0:1, 0:16],
                                         axis=AX.X)
                    with nc.Block(no_gpsimd_drain=True) as blk:
                        @blk.scalar
                        def _(e):
                            e.copy(tinyr[0:1, 1:2], tiny[0:1, 5:6])
                    with nc.Block(no_gpsimd_drain=True) as blk:
                        @blk.tensor
                        def _(e):
                            e.matmul(pa[:, 16:17], onesrow[:], tinyr[0:1, 1:2],
                                     start=True, stop=True)
                    with nc.Block(no_gpsimd_drain=True) as blk:
                        @blk.scalar
                        def _(e):
                            e.activation(biasg[:], pa[:, 16:17], AF.Copy,
                                         scale=-1.0 / 8.0)

                    if debug:
                        with nc.Block(no_gpsimd_drain=True) as blk:
                            @blk.gpsimd
                            def _(e):
                                ostore(e, d_q, qsb[:])
                                ostore(e, d_k, ksb[:])
                                ostore(e, d_v, vsb[:])
                                ostore(e, d_biasg, biasg[:])

                    stW.close()
                    with (
                        nc.sbuf_tensor("psb", [128, 6 * 1024], BF16,
                                       side="right") as psb,
                        nc.sbuf_tensor("mask01sb", [128, 4096], BF16,
                                       side="right") as mask01,
                        nc.sbuf_tensor("stgA", [64, 2 * S], BF16,
                                       side="right") as stgA,
                        nc.sbuf_tensor("stgB", [64, 2 * S], BF16,
                                       side="right") as stgB,
                        nc.sbuf_tensor("stgS", [128, 2048], F32,
                                       side="right") as stgS,
                    ):
                        with nc.Block(no_gpsimd_drain=True) as blk:
                            @blk.gpsimd
                            def _(e):
                                wload(e, mask01[:], hin["mask01"])
                                e.wait_ge(s_w, cnt["s_w"])

                        ndiag = []
                        nd = 0
                        for (qc, kt) in STRIPS:
                            ndiag.append(nd)
                            if kt >= 4 * qc:
                                nd += 1
                        nsp = len(STRIPS)

                        bA, bB, bC, bD, bE = (cnt["sA"], cnt["sB"], cnt["sC"],
                                              cnt["sD"], cnt["sE"])
                        with nc.Block(no_gpsimd_drain=True) as blk:
                            @blk.tensor
                            def _(e, bA=bA, bB=bB, bC=bC, bD=bD, bE=bE):
                                allst = [(p, qc, kt) for p in range(NP)
                                         for (qc, kt) in STRIPS]

                                def emit_st(gi):
                                    p, qc, kt = allst[gi]
                                    buf = gi % 2
                                    sp = pa[:, buf*1024:(buf+1)*1024]
                                    if gi >= 2:
                                        e.wait_ge(sB, bB + gi - 1)
                                    mA = e.matmul(
                                        sp[:, 0:512],
                                        ksb[0:64, p*2048 + kt*128:
                                            p*2048 + (kt+1)*128],
                                        qsb[0:64, p*2048 + qc*512:
                                            p*2048 + (qc+1)*512],
                                        start=True, stop=True)
                                    mB = e.matmul(
                                        sp[:, 512:1024],
                                        ksb[64:128, p*2048 + kt*128:
                                            p*2048 + (kt+1)*128],
                                        qsb[64:128, p*2048 + qc*512:
                                            p*2048 + (qc+1)*512],
                                        start=True, stop=True)
                                    INC(mB, "sA")

                                def emit_pv(gi):
                                    p, qc, kt = allst[gi]
                                    si = gi - p * nsp
                                    nkt = 4 * qc + 4
                                    buf = gi % 6
                                    m = p * 4 + qc
                                    acc = (m % 2) * 1024
                                    if kt == 0 and m >= 2:
                                        e.wait_ge(sD, bD + m - 1)
                                    diag = kt >= 4 * qc
                                    if diag:
                                        e.wait_ge(sC, bC + p * 16
                                                  + ndiag[si] + 1)
                                    else:
                                        e.wait_ge(sB, bB + gi + 1)
                                    pvA = e.matmul(
                                        pb[0:65, acc:acc + 512],
                                        vsb[:, kt*1040 + (2*p)*65:
                                            kt*1040 + (2*p)*65 + 65],
                                        psb[:, buf*1024:buf*1024 + 512],
                                        start=(kt == 0), stop=(kt == nkt - 1))
                                    pvB = e.matmul(
                                        pb[0:65, acc + 512:acc + 1024],
                                        vsb[:, kt*1040 + (2*p+1)*65:
                                            kt*1040 + (2*p+1)*65 + 65],
                                        psb[:, buf*1024 + 512:
                                            buf*1024 + 1024],
                                        start=(kt == 0), stop=(kt == nkt - 1))
                                    INC(pvB, "sE")

                                n = len(allst)
                                for gi in range(n + 2):
                                    if gi < n:
                                        emit_st(gi)
                                    if gi >= 2:
                                        emit_pv(gi - 2)

                            @blk.scalar
                            def _(e, bA=bA, bE=bE):
                                for p in range(NP):
                                    for si, (qc, kt) in enumerate(STRIPS):
                                        gi = p * nsp + si
                                        pbuf = gi % 2
                                        buf = gi % 6
                                        sp = pa[:, pbuf*1024:(pbuf+1)*1024]
                                        e.wait_ge(sA, bA + gi + 1)
                                        if gi >= 6:
                                            e.wait_ge(sE, bE + gi - 5)
                                        INC(e.activation(
                                            psb[:, buf*1024:(buf+1)*1024],
                                            sp,
                                            AF.Exp, bias=biasg[:, 0:1],
                                            scale=1.0 / 8.0), "sB")

                            @blk.vector
                            def _(e, bE=bE, bH=cnt["s_st"]):
                                bB = cnt["sB"] - NP * nsp
                                last_gi = {}
                                for si, (qc, kt) in enumerate(STRIPS):
                                    if kt == 4 * qc + 3:
                                        last_gi[qc] = si
                                for p in range(NP):
                                    if p >= 2:
                                        e.wait_ge(s_st, bH + 160 * (p - 1))
                                    for si, (qc, kt) in enumerate(STRIPS):
                                        gi = p * nsp + si
                                        buf = gi % 6
                                        if kt >= 4 * qc:
                                            r = kt - 4 * qc
                                            e.wait_ge(sB, bB + gi + 1)
                                            INC(e.tensor_mul(
                                                psb[:, buf*1024:(buf+1)*1024],
                                                psb[:, buf*1024:(buf+1)*1024],
                                                mask01[:, r*1024:(r+1)*1024]),
                                                "sC")
                                        if kt == 4 * qc + 3:
                                            m = p * 4 + qc
                                            acc = (m % 2) * 1024
                                            if m >= 2:
                                                p2, qc2 = ((m - 2) // 4,
                                                           (m - 2) % 4)
                                                e.wait_ge(s_st, bH + 160 * p2
                                                          + 32 * (qc2 + 1))
                                            e.wait_ge(sE, bE + p * nsp
                                                      + last_gi[qc] + 1)
                                            cA = (qc % 2) * 1024
                                            cB = cA + 512
                                            sa = (p % 2) * 2048
                                            e.tensor_copy(
                                                stgA[:, sa + qc*512:
                                                     sa + (qc+1)*512],
                                                pb[0:64, acc:acc+512])
                                            e.tensor_copy(
                                                stgS[64:65, cA:cA+512],
                                                pb[64:65, acc:acc+512])
                                            e.tensor_copy(
                                                stgB[:, sa + qc*512:
                                                     sa + (qc+1)*512],
                                                pb[0:64, acc+512:acc+1024])
                                            INC(e.tensor_copy(
                                                stgS[64:65, cB:cB+512],
                                                pb[64:65, acc+512:acc+1024]),
                                                "sD")

                            @blk.sync
                            def _(e):
                                bD = cnt["sD"] - NP * 4
                                for p in range(NP):
                                    for qc in range(4):
                                        e.wait_ge(sD, bD + p * 4 + qc + 1)
                                        cA = (qc % 2) * 1024
                                        cB = cA + 512
                                        INC(e.dma_start(
                                            out=sums_hbm[0, p,
                                                         qc*512:(qc+1)*512],
                                            in_=stgS[64:65, cA:cA+512]),
                                            "s_st", 16)
                                        INC(e.dma_start(
                                            out=sums_hbm[1, p,
                                                         qc*512:(qc+1)*512],
                                            in_=stgS[64:65, cB:cB+512]),
                                            "s_st", 16)
                                    sa = (p % 2) * 2048
                                    INC(e.dma_start(
                                        out=arawA_hbm[:, p*S:(p+1)*S],
                                        in_=stgA[:, sa:sa+2048]), "s_st", 16)
                                    INC(e.dma_start(
                                        out=arawB_hbm[:, p*S:(p+1)*S],
                                        in_=stgB[:, sa:sa+2048]), "s_st", 16)
                                e.wait_ge(s_st, cnt["s_st"])

            q2sb = _top.enter_context(nc.sbuf_tensor(
                "q2sb", [128, NC * 2048], BF16, side="right"))
            with (
                nc.sbuf_tensor("w1a", [128, 8192], BF16, side="right") as w1a,
                nc.sbuf_tensor("w2a", [128, 8192], BF16, side="right") as w2a,
                nc.sbuf_tensor("b2r", [16, 2 * 1024], F32R, side="right") as b2r,
                nc.sbuf_tensor("b1sb", [128, 24], F32, side="right") as b1sb,
                nc.sbuf_tensor("rsb2", [NNODE, 3 * S], BF16,
                               side="right") as rsb2,
            ):
                sw_after_rsb = [0]
                with nc.Block(no_gpsimd_drain=True) as blk:
                    @blk.gpsimd
                    def _(e):
                        wload(e, w1a[:], hin["w1_q"])
                        wload(e, w2a[:], hin["w2_q"])
                        wload(e, b2r[:, 0:1024], hin["b2_q"])
                        wload(e, b2r[:, 1024:2048], hin["b2_k"])
                        for i, f in enumerate("qkv"):
                            wload(e, b1sb[:, i*8:(i+1)*8], hin[f"b1_{f}"])
                        wload(e, rsb2[:], rsb_hbm)
                        sw_after_rsb[0] = cnt["s_w"]

                with (
                    nc.sbuf_tensor("asb", [128, NP * 2048], BF16,
                                   side="right") as asb,
                    nc.sbuf_tensor("wo", [128, NC * 1024], BF16,
                                   side="right") as wo,
                    nc.sbuf_tensor("rsrc", [16, S], F32, side="right") as rsrc,
                    nc.sbuf_tensor("rdst", [16, S], F32R, side="right") as rdst,
                    nc.sbuf_tensor("ostage", [128, 2 * 1024], F32,
                                   side="right") as ostage,
                    nc.sbuf_tensor("ebcsb", [16, 8 * 128], F32R,
                                   side="right") as ebc,
                ):
                    bXs = cnt["s_x"]
                    swB = [0]
                    swE = [0]
                    with nc.Block(no_gpsimd_drain=True) as blk:
                        @blk.gpsimd
                        def _(e, bSt=cnt["s_st"]):
                            wload(e, wo[:], hin["w_o"])
                            wload(e, ebc[:], hin["ebc"])
                            swE[0] = cnt["s_w"]
                            e.wait_ge(s_st, bSt)
                            INC(e.dma_start(
                                out=asb.ap()[64:128, :],
                                in_=arawB_hbm), "s_w", 16)
                            swB[0] = cnt["s_w"]

                        @blk.sync
                        def _(e):
                            INC(e.dma_start(
                                out=rsrc[:],
                                in_=sums_hbm.rearrange("h p c -> (h p) c")),
                                "s_x", 16)
                            INC(e.dma_start(
                                out=asb.ap()[0:64, :],
                                in_=arawA_hbm), "s_x", 16)

                        @blk.vector
                        def _(e, bXs=bXs):
                            e.wait_ge(s_x, bXs + 16)
                            with nc.allow_low_precision(
                                    reason="float32r out is 4-byte fp32"):
                                e.reciprocal(rdst[:], rsrc[:])

                    bB, bC, bD = cnt["sB"], cnt["sC"], cnt["sD"]
                    with nc.Block(no_gpsimd_drain=True) as blk:
                        @blk.tensor
                        def _(e, bD=bD, swE=swE[0]):
                            for p in range(NP):
                                ps = pa if p % 2 == 0 else pb
                                if p == 0:
                                    e.wait_ge(s_w, swE)
                                if p >= 2:
                                    e.wait_ge(sD, bD + p - 1)
                                last = None
                                for gq in range(4):
                                    last = e.matmul(
                                        ps[:, gq*512:(gq+1)*512],
                                        ebc[:, p*128:(p+1)*128],
                                        rdst[0:16, gq*512:(gq+1)*512],
                                        start=True, stop=True)
                                INC(last, "sC")

                        @blk.vector
                        def _(e, bC=bC, bXs=bXs, swB=swB[0]):
                            for p in range(NP):
                                ps = pa if p % 2 == 0 else pb
                                if p == 0:
                                    e.wait_ge(s_x, bXs + 32)
                                    e.wait_ge(s_w, swB)
                                e.wait_ge(sC, bC + p + 1)
                                INC(e.tensor_mul(
                                    asb[:, p*2048:(p+1)*2048],
                                    asb[:, p*2048:(p+1)*2048], ps[:, :]), "sD")

                    if debug:
                        with nc.Block(no_gpsimd_drain=True) as blk:
                            @blk.gpsimd
                            def _(e):
                                ostore(e, d_anorm, asb[:])

                    bE, bF, bG = cnt["sE"], cnt["sF"], cnt["sG"]
                    sw_wo = cnt["s_w"]
                    with nc.Block(no_gpsimd_drain=True) as blk:
                        @blk.tensor
                        def _(e, bE=bE, bG=bG, sw_wo=sw_wo):
                            e.wait_ge(s_w, sw_wo)
                            for qt in range(NT):
                                half = (qt % 2) * 1024
                                if qt >= 2:
                                    e.wait_ge(sE, bE + qt - 1)
                                last = None
                                for p in range(NP):
                                    for gq in range(2):
                                        last = e.matmul(
                                            pb[:, half + gq*512:
                                               half + (gq+1)*512],
                                            asb[:, p*2048 + qt*128:
                                                p*2048 + (qt+1)*128],
                                            wo[:, p*1024 + gq*512:
                                               p*1024 + (gq+1)*512],
                                            start=(p == 0), stop=(p == NP-1))
                                INC(last, "sF")

                        @blk.scalar
                        def _(e, bF=bF, bG=cnt["s_st"]):
                            for qt in range(NT):
                                half = (qt % 2) * 1024
                                e.wait_ge(sF, bF + qt + 1)
                                if qt >= 2:
                                    e.wait_ge(s_st, bG + 16 * (qt - 1))
                                INC(e.copy(
                                    ostage[:, half:half + 1024],
                                    pb[:, half:half + 1024]), "sE")

                        @blk.sync
                        def _(e, bG=cnt["s_st"]):
                            bE2 = cnt["sE"] - NT
                            for qt in range(NT):
                                half = (qt % 2) * 1024
                                e.wait_ge(sE, bE2 + qt + 1)
                                if qt > 0:
                                    e.wait_ge(s_st, bG + 16 * qt)
                                INC(e.dma_start(
                                    out=attn_hbm[qt*128:(qt+1)*128, :],
                                    in_=ostage[:, half:half + 1024]), "s_st",
                                    16)

                with (
                    nc.sbuf_tensor("w1b", [128, 8192], BF16,
                                   side="right") as w1b,
                    nc.sbuf_tensor("w2b", [128, 8192], BF16,
                                   side="right") as w2b,
                    nc.sbuf_tensor("hid", [128, NP * 2048], BF16,
                                   side="right") as hid,
                    nc.sbuf_tensor("self32r", [16, S], F32R,
                                   side="right") as self32r,
                    nc.sbuf_tensor("maskm1", [16, S], BF16,
                                   side="right") as maskm1,
                    nc.sbuf_tensor("fstage", [128, 2 * 2048], BF16,
                                   side="right") as fstage,
                ):
                    with nc.Block(no_gpsimd_drain=True) as blk:
                        @blk.gpsimd
                        def _(e):
                            wload(e, w1b[:], hin["w1_k"])
                            wload(e, w2b[:], hin["w2_k"])

                    for fi, f in enumerate("qkv"):
                        w1x = (w1a, w1b, w1a)[fi]
                        w2x = (w2a, w2b, w2a)[fi]
                        dsth = (q2_hbm, k2_hbm, v2_hbm)[fi]

                        bA, bB, bC, bD, bE, bG = (
                            cnt["sA"], cnt["sB"], cnt["sC"], cnt["sD"],
                            cnt["sE"], cnt["s_st"])
                        sw_need = sw_after_rsb[0] if fi == 0 else cnt["s_w"]
                        with nc.Block(no_gpsimd_drain=True) as blk:
                            @blk.tensor
                            def _(e, fi=fi, w1x=w1x, w2x=w2x, bA=bA, bB=bB,
                                  bC=bC, bD=bD, bE=bE, bG=bG,
                                  sw_need=sw_need):
                                e.wait_ge(s_w, sw_need)
                                last = None
                                for gq in range(4):
                                    last = e.matmul(
                                        pb[0:16, gq*512:(gq+1)*512], aroute[:],
                                        rsb2[:, fi*S + gq*512:
                                             fi*S + (gq+1)*512],
                                        start=True, stop=True)
                                INC(last, "sA")
                                for p in range(NP):
                                    ps = pa if p % 2 == 0 else pb
                                    if p == 1:
                                        e.wait_ge(sB, bB + 1)
                                    if p >= 2:
                                        e.wait_ge(sC, bC + p - 1)
                                    for dc in range(NC):
                                        for gq in range(4):
                                            e.matmul(
                                                ps[:, gq*512:(gq+1)*512],
                                                w1x[:, dc*1024 + p*128:
                                                    dc*1024 + (p+1)*128],
                                                xnfm[:, dc*2048 + gq*512:
                                                     dc*2048 + (gq+1)*512],
                                                start=(dc == 0), stop=False)
                                    e.wait_ge(sB, bB + 1)
                                    last = None
                                    for gq in range(4):
                                        last = e.matmul(
                                            ps[:, gq*512:(gq+1)*512],
                                            bige[:, p*128:(p+1)*128],
                                            maskm1[0:16, gq*512:(gq+1)*512],
                                            start=False, stop=True)
                                    INC(last, "sD")
                                if fi < 2:
                                    for oc in range(NC):
                                        ps = pa if oc % 2 == 0 else pb
                                        e.wait_ge(sC, bC + NP)
                                        if oc >= 2:
                                            e.wait_ge(sE, bE + oc - 1)
                                        for p in range(NP):
                                            for gq in range(4):
                                                e.matmul(
                                                    ps[:, gq*512:(gq+1)*512],
                                                    w2x[:, p*1024 + oc*128:
                                                        p*1024 + (oc+1)*128],
                                                    hid[:, p*2048 + gq*512:
                                                        p*2048 + (gq+1)*512],
                                                    start=(p == 0), stop=False)
                                        last = None
                                        for gq in range(4):
                                            last = e.matmul(
                                                ps[:, gq*512:(gq+1)*512],
                                                b2r[0:16,
                                                    (fi % 2)*1024 + oc*128:
                                                    (fi % 2)*1024
                                                    + (oc+1)*128],
                                                self32r[0:16,
                                                        gq*512:(gq+1)*512],
                                                start=False, stop=True)
                                        INC(last, "sF")
                                else:
                                    for tt in range(NT):
                                        ps = pa if tt % 2 == 0 else pb
                                        half = 0
                                        e.wait_ge(sC, bC + NP)
                                        if tt >= 2:
                                            e.wait_ge(sE, bE + tt - 1)
                                        for p in range(NP):
                                            for gq in range(2):
                                                e.matmul(
                                                    ps[:, gq*512:(gq+1)*512],
                                                    hid[:, p*2048 + tt*128:
                                                        p*2048 + (tt+1)*128],
                                                    w2x[:, p*1024 + gq*512:
                                                        p*1024 + (gq+1)*512],
                                                    start=(p == 0), stop=False)
                                        last = None
                                        for gq in range(2):
                                            last = e.matmul(
                                                ps[:, gq*512:(gq+1)*512],
                                                self32r[0:16,
                                                        tt*128:(tt+1)*128],
                                                b2r[0:16,
                                                    (fi % 2)*1024 + gq*512:
                                                    (fi % 2)*1024
                                                    + (gq+1)*512],
                                                start=False, stop=True)
                                        INC(last, "sF")

                            @blk.vector
                            def _(e, fi=fi, bA=bA):
                                e.wait_ge(sA, bA + 1)
                                e.tensor_scalar(
                                    self32r[0:16, :], pb[0:16, :],
                                    thr[:, 0:1], None, ALU.is_ge)
                                INC(e.tensor_scalar(
                                    maskm1[0:16, :], pb[0:16, :],
                                    thr[:, 0:1], -1.0, ALU.is_ge, ALU.add),
                                    "sB")

                            @blk.scalar
                            def _(e, fi=fi, bD=bD):
                                for p in range(NP):
                                    ps = pa if p % 2 == 0 else pb
                                    e.wait_ge(sD, bD + p + 1)
                                    INC(e.activation(
                                        hid[:, p*2048:(p+1)*2048], ps[:, :],
                                        AF.Relu,
                                        bias=b1sb[:, fi*8+p:fi*8+p+1]), "sC")
                                nsub = NC if fi < 2 else NT
                                w = 2048 if fi < 2 else 1024
                                bF = cnt["sF"] - nsub
                                for u in range(nsub):
                                    ps = pa if u % 2 == 0 else pb
                                    e.wait_ge(sF, bF + u + 1)
                                    if fi == 0:
                                        INC(e.copy(
                                            q2sb[:, u*2048:(u+1)*2048],
                                            ps[:, 0:w]), "sE")
                                        continue
                                    if u >= 2:
                                        e.wait_ge(s_st, bG + 16 * (u - 1))
                                    INC(e.copy(
                                        fstage[:, (u % 2)*2048:
                                               (u % 2)*2048 + w],
                                        ps[:, 0:w]), "sE")

                            @blk.sync
                            def _(e, fi=fi, dsth=dsth, bG=bG):
                                if fi == 0:
                                    return
                                nsub = NC if fi < 2 else NT
                                w = 2048 if fi < 2 else 1024
                                bE2 = cnt["sE"] - nsub
                                for u in range(nsub):
                                    e.wait_ge(sE, bE2 + u + 1)
                                    if u > 0:
                                        e.wait_ge(s_st, bG + 16 * u)
                                    INC(e.dma_start(
                                        out=dsth[:, u*w:(u+1)*w],
                                        in_=fstage[:, (u % 2)*2048:
                                                   (u % 2)*2048 + w]),
                                        "s_st", 16)

                            if fi == 1:
                                @blk.gpsimd
                                def _(e):
                                    wload(e, w1a[:], hin["w1_v"])
                                    wload(e, w2a[:], hin["w2_v"])
                                    wload(e, b2r[:, 0:1024], hin["b2_v"])

                        if debug and f == "q":
                            with nc.Block(no_gpsimd_drain=True) as blk:
                                @blk.gpsimd
                                def _(e, fi=fi):
                                    ostore(e, d_hid, hid[:])
                                    ostore(e, d_sel.rearrange(
                                        "p (f s) -> p f s", f=3)[:, fi, :],
                                        self32r[0:16, :])
                                    ostore(e, q2_hbm, q2sb[:])

        with (
            nc.sbuf_tensor("k2sb", [128, NC * 2048], BF16, side="right") as k2sb,
            nc.sbuf_tensor("v2sb", [128, NT * 1024], BF16, side="right") as v2sb,
        ):
            b_ld = {}
            with nc.Block(no_gpsimd_drain=True) as blk:
                @blk.scalar
                def _(e):
                    e.wait_ge(s_st, cnt["s_st"])
                    INC(e.dma_start(out=k2sb[:, 0:8192],
                                    in_=k2_hbm[:, 0:8192]), "s_tr", 16)
                    b_ld["k2"] = cnt["s_tr"]

                @blk.sync
                def _(e):
                    e.wait_ge(s_st, cnt["s_st"])
                    INC(e.dma_start(out=k2sb[:, 8192:16384],
                                    in_=k2_hbm[:, 8192:16384]), "s_x", 16)
                    b_ld["k2x"] = cnt["s_x"]

                @blk.gpsimd
                def _(e):
                    e.wait_ge(s_st, cnt["s_st"])
                    INC(e.dma_start(out=v2sb[:], in_=v2_hbm), "s_w", 16)
                    b_ld["v2"] = cnt["s_w"]

            with nc.sbuf_tensor("sq2", [128, 2 * 2048], BF16,
                                side="right") as sq2:
                bA, bB = cnt["sA"], cnt["sB"]
                with nc.Block(no_gpsimd_drain=True) as blk:
                    @blk.vector
                    def _(e, bA=bA, bB=bB):
                        for i, src in enumerate((q2sb, k2sb)):
                            for dc in range(NC):
                                j = i * NC + dc
                                if i == 1 and dc == 0:
                                    e.wait_ge(s_tr, b_ld["k2"])
                                if i == 1 and dc == 4:
                                    e.wait_ge(s_x, b_ld["k2x"])
                                if j >= 2:
                                    e.wait_ge(sB, bB + j - 1)
                                INC(e.tensor_mul(
                                    sq2[:, (j % 2)*2048:(j % 2 + 1)*2048],
                                    src[:, dc*2048:(dc+1)*2048],
                                    src[:, dc*2048:(dc+1)*2048]), "sA")

                    @blk.tensor
                    def _(e, bA=bA):
                        for i in range(2):
                            ps = pa if i == 0 else pb
                            for dc in range(NC):
                                j = i * NC + dc
                                e.wait_ge(sA, bA + j + 1)
                                last = None
                                for gq in range(4):
                                    last = e.matmul(
                                        ps[0:1, gq*512:(gq+1)*512], onescol[:],
                                        sq2[:, (j % 2)*2048 + gq*512:
                                            (j % 2)*2048 + (gq+1)*512],
                                        start=(dc == 0), stop=(dc == NC - 1))
                                INC(last, "sB")

                b2A, b2B, b2C = cnt["sA"], cnt["sB"], cnt["sC"]
                with nc.Block(no_gpsimd_drain=True) as blk:
                    @blk.vector
                    def _(e, b2B=b2B):
                        e.reduce_max(tiny[0:1, 6:7], pa[0:1, :], axis=AX.X)
                        INC(e.reduce_max(tiny[0:1, 7:8], pb[0:1, :],
                                         axis=AX.X), "sA")
                        e.wait_ge(sB, b2B + 1)
                        INC(e.tensor_mul(tiny[0:1, 10:11], tiny[0:1, 8:9],
                                         tiny[0:1, 9:10]), "sB")

                    @blk.scalar
                    def _(e, b2A=b2A, b2B=b2B, b2C=b2C):
                        e.wait_ge(sA, b2A + 1)
                        e.activation(tiny[0:1, 8:9], tiny[0:1, 6:7], AF.Sqrt)
                        INC(e.activation(tiny[0:1, 9:10], tiny[0:1, 7:8],
                                         AF.Sqrt), "sB")
                        e.wait_ge(sB, b2B + 2)
                        INC(e.copy(tinyr[0:1, 2:3], tiny[0:1, 10:11]), "sB")
                        e.wait_ge(sC, b2C + 1)
                        e.activation(bias2[:], pa[:, 16:17], AF.Copy,
                                     scale=-1.0 / 32.0)

                    @blk.tensor
                    def _(e, b2B=b2B):
                        e.wait_ge(sB, b2B + 3)
                        INC(e.matmul(pa[:, 16:17], onesrow[:],
                                     tinyr[0:1, 2:3],
                                     start=True, stop=True), "sC")
                if debug:
                    with nc.Block(no_gpsimd_drain=True) as blk:
                        @blk.gpsimd
                        def _(e):
                            ostore(e, d_bias2, bias2[:])

            with (
                nc.sbuf_tensor("p2sb", [128, 2 * 16 * 512], BF16,
                               side="right") as p2sb,
                nc.sbuf_tensor("o2sb", [128, 2 * NC * 512], F32,
                               side="right") as o2sb,
                nc.sbuf_tensor("r2row", [1, 512], F32, side="right") as r2row,
                nc.sbuf_tensor("r2col", [128, 2], F32, side="right") as r2col,
                nc.sbuf_tensor("xstg", [128, 2 * 1024], F32,
                               side="right") as xstg,
                nc.sbuf_tensor("astg", [128, 2 * 1024], F32,
                               side="right") as astg,
                nc.sbuf_tensor("ysb", [128, 2 * 1024], F32,
                               side="right") as ysb,
            ):
                bA, bB, bC, bD, bE, bF, bG, bH = (
                    cnt["sA"], cnt["sB"], cnt["sC"], cnt["sD"], cnt["sE"],
                    cnt["sF"], cnt["sG"], cnt["sH"])
                bX, bO, bP = cnt["s_x"], cnt["s_out"], cnt.get("s_p", 0)
                with nc.Block(no_gpsimd_drain=True) as blk:
                    @blk.tensor
                    def _(e):
                        for qc in range(4):
                            pslot = (qc % 2) * 16 * 512
                            oslot = (qc % 2) * NC * 512

                            def emit_sc(kt, qc=qc):
                                buf = kt % 2
                                if qc * 16 + kt >= 2:
                                    e.wait_ge(sB, bB + qc * 16 + kt - 1)
                                last = None
                                for dc in range(NC):
                                    last = e.matmul(
                                        pa[:, buf*512:(buf+1)*512],
                                        k2sb[:, dc*2048 + kt*128:
                                             dc*2048 + (kt+1)*128],
                                        q2sb[:, dc*2048 + qc*512:
                                             dc*2048 + (qc+1)*512],
                                        start=(dc == 0), stop=(dc == NC - 1))
                                INC(last, "sA")

                            def emit_sum(kt, qc=qc, pslot=pslot):
                                e.wait_ge(sB, bB + qc * 16 + kt + 1)
                                if kt == 0 and qc > 0:
                                    e.wait_ge(sG, bG + qc)
                                last = e.matmul(
                                    pa[0:1, 1024:1536], onescol[:],
                                    p2sb[:, pslot + kt*512:
                                         pslot + (kt+1)*512],
                                    start=(kt == 0), stop=(kt == NT - 1))
                                if kt == NT - 1:
                                    INC(last, "sC")

                            for kt in range(NT + 1):
                                if kt < NT:
                                    emit_sc(kt)
                                if kt >= 1:
                                    emit_sum(kt - 1)

                            for dc in range(NC):
                                ps2 = (pa if dc % 2 == 0 else pb)
                                if qc == 0 and dc == 0:
                                    e.wait_ge(s_w, b_ld["v2"])
                                if dc >= 2:
                                    e.wait_ge(sD, bD + qc * 8 + dc - 1)
                                last = None
                                for kt in range(NT):
                                    last = e.matmul(
                                        ps2[:, 1536:2048],
                                        v2sb[:, kt*1024 + dc*128:
                                             kt*1024 + (dc+1)*128],
                                        p2sb[:, pslot + kt*512:
                                             pslot + (kt+1)*512],
                                        start=(kt == 0), stop=(kt == NT - 1))
                                INC(last, "sE")

                            for qs in range(4):
                                if qc * 4 + qs >= 1:
                                    e.wait_ge(s_p, bP + qc * 4 + qs)
                                e.wait_ge(sD, bD + qc * 8 + 8)
                                for dc in range(NC):
                                    e.transpose(
                                        pb[:, dc*128:(dc+1)*128],
                                        o2sb[:, oslot + dc*512 + qs*128:
                                             oslot + dc*512 + (qs+1)*128],
                                        ident32[:])
                                e.wait_ge(sG, bG + qc + 1)
                                last = e.matmul(
                                    pb[:, 1024:1025],
                                    r2row[0:1, qs*128:(qs+1)*128],
                                    one11[:], start=True, stop=True)
                                INC(last, "sF")

                    @blk.scalar
                    def _(e):
                        for qc in range(4):
                            pslot = (qc % 2) * 16 * 512
                            oslot = (qc % 2) * NC * 512
                            for kt in range(NT):
                                buf = kt % 2
                                e.wait_ge(sA, bA + qc * 16 + kt + 1)
                                if qc >= 2:
                                    e.wait_ge(sE, bE + 8 * (qc - 1))
                                INC(e.activation(
                                    p2sb[:, pslot + kt*512:
                                         pslot + (kt+1)*512],
                                    pa[:, buf*512:(buf+1)*512],
                                    AF.Exp, bias=bias2[:, 0:1],
                                    scale=1.0 / 32.0), "sB")
                            for dc in range(NC):
                                ps2 = (pa if dc % 2 == 0 else pb)
                                e.wait_ge(sE, bE + qc * 8 + dc + 1)
                                INC(e.copy(
                                    o2sb[:, oslot + dc*512:
                                         oslot + (dc+1)*512],
                                    ps2[:, 1536:2048]), "sD")
                            for qs in range(4):
                                if qc * 4 + qs >= 2:
                                    e.wait_ge(s_p, bP + qc * 4 + qs - 1)
                                e.wait_ge(sF, bF + qc * 4 + qs + 1)
                                INC(e.copy(r2col[:, qs % 2:qs % 2 + 1],
                                           pb[:, 1024:1025]), "sC")

                    @blk.vector
                    def _(e):
                        for qc in range(4):
                            e.wait_ge(sC, bC + qc * 5 + 1)
                            INC(e.reciprocal(r2row[:], pa[0:1, 1024:1536]),
                                "sG")
                            for qs in range(4):
                                slot = (qs % 2) * 1024
                                if qc * 4 + qs >= 2:
                                    e.wait_ge(s_out,
                                              bO + 16 * (qc * 4 + qs - 1))
                                e.wait_ge(sC, bC + qc * 5 + qs + 2)
                                e.wait_ge(s_x, bX + 32 * (qc * 4 + qs + 1))
                                INC(e.scalar_tensor_tensor(
                                    ysb[:, slot:slot+1024], pb[:, 0:1024],
                                    r2col[:, qs % 2:qs % 2 + 1],
                                    astg[:, slot:slot+1024],
                                    ALU.mult, ALU.add), "s_p")
                                INC(e.tensor_add(
                                    ysb[:, slot:slot+1024],
                                    ysb[:, slot:slot+1024],
                                    xstg[:, slot:slot+1024]), "sH")

                    @blk.sync
                    def _(e):
                        for qc in range(4):
                            for qs in range(4):
                                qg = qc * 4 + qs
                                slot = (qs % 2) * 1024
                                if qg >= 2:
                                    e.wait_ge(sH, bH + qg - 1)
                                if qg > 0:
                                    e.wait_ge(s_x, bX + 32 * qg)
                                INC(e.dma_start(
                                    out=xstg[:, slot:slot+1024],
                                    in_=x_h[qg*128:(qg+1)*128, :]), "s_x", 16)
                                INC(e.dma_start(
                                    out=astg[:, slot:slot+1024],
                                    in_=attn_hbm[qg*128:(qg+1)*128, :]),
                                    "s_x", 16)

                    @blk.gpsimd
                    def _(e):
                        for qc in range(4):
                            for qs in range(4):
                                qg = qc * 4 + qs
                                slot = (qs % 2) * 1024
                                e.wait_ge(sH, bH + qg + 1)
                                if qg > 0:
                                    e.wait_ge(s_out, bO + 16 * qg)
                                INC(e.dma_start(
                                    out=y_h[qg*128:(qg+1)*128, :],
                                    in_=ysb[:, slot:slot+1024]), "s_out", 16)

        with nc.Block(no_gpsimd_drain=True) as blk:
            @blk.gpsimd
            def _(e):
                e.wait_ge(s_out, cnt["s_out"])

    return nc


def kernel(**inputs):
    hp = host_prep(inputs)
    x = np.ascontiguousarray(np.asarray(inputs["L0_attn_hidden_states"],
                                        dtype=np.float32))
    nc = build_program(debug=False)
    in_maps = []
    for c in range(B):
        m = {"x": np.ascontiguousarray(x[c])}
        m.update(hp)
        in_maps.append(m)
    res = run_bass_kernel_spmd(nc, in_maps, core_ids=list(range(8)))
    out = np.stack([np.asarray(res.results[c]["y"]) for c in range(B)], axis=0)
    return out.astype(np.float32)


# version 20
# speedup vs baseline: 1.0132x; 1.0132x over previous
import contextlib
import numpy as np
import ml_dtypes
import concourse.bass as bass
import concourse.mybir as mybir
from concourse.bass_utils import run_bass_kernel_spmd

F32 = mybir.dt.float32
F32R = mybir.dt.float32r
BF16 = mybir.dt.bfloat16
AF = mybir.ActivationFunctionType
ALU = mybir.AluOpType
AX = mybir.AxisListType

B, S, D = 8, 2048, 1024
NH, HD, RD = 16, 64, 32
NT = S // 128
NC = D // 128
NP = 8
NLEAF, NNODE = 16, 15
BIG = 1.0e9
EPS = 1e-5


def _chunkmajor(w):
    n = w.shape[1]
    return np.ascontiguousarray(
        w.reshape(NC, 128, n).transpose(1, 0, 2).reshape(128, NC * n))


def host_prep(inp):
    h = {}
    w_qkvT = np.ascontiguousarray(np.asarray(inp["L0_attn_w_qkv"]).T)
    qcols = np.concatenate([w_qkvT[:, mp*768:mp*768+256] for mp in range(4)], 1)
    vcols = np.concatenate([w_qkvT[:, mp*768+256:mp*768+512] for mp in range(4)], 1)
    kcols = np.concatenate([w_qkvT[:, mp*768+512:mp*768+768] for mp in range(4)], 1)
    h["w_q"] = _chunkmajor(qcols)
    h["w_k"] = _chunkmajor(kcols)
    h["w_v"] = _chunkmajor(vcols)
    h["w_o"] = _chunkmajor(
        np.ascontiguousarray(np.asarray(inp["L0_attn_w_out"]).T))

    pos = np.arange(S, dtype=np.float64)
    invf = 1.0 / 10000.0 ** (np.arange(0, RD, 2, dtype=np.float64) / RD)
    ang = np.outer(invf, pos)
    cos = np.ones((64, S), np.float64)
    sin = np.zeros((64, S), np.float64)
    for d in range(RD):
        cos[d] = np.cos(ang[d // 2])
        sin[d] = np.sin(ang[d // 2])
    h["cos"] = np.tile(cos, (2, 1)).astype(np.float32)
    h["sin"] = np.tile(sin, (2, 1)).astype(np.float32)
    P = np.zeros((128, 128), np.float32)
    for base in (0, 64):
        for j in range(RD // 2):
            P[base + 2*j + 1, base + 2*j] = -1.0
            P[base + 2*j, base + 2*j + 1] = 1.0
    h["prope"] = P
    h["ident"] = np.eye(128, dtype=np.float32)

    m = np.zeros((128, 4, 2, 512), np.float32)
    for r in range(4):
        for ck in range(128):
            m[ck, r, :, :] = (np.arange(512) >= 128*r + ck)[None, :]
    h["mask01"] = m.reshape(128, 4096)

    e = np.zeros((128, NP, 16), np.float32)
    for p in range(NP):
        e[0:64, p, 2*p] = 1.0
        e[64:128, p, 2*p+1] = 1.0
    h["enorm"] = e.reshape(128, NP * 16)

    A = np.zeros((NNODE, NLEAF), np.float32)
    thr = np.zeros((NLEAF, 1), np.float32)
    for leaf in range(NLEAF):
        bits = [(leaf >> (3 - t)) & 1 for t in range(4)]
        cur, nleft = 0, 0
        for t in range(4):
            A[cur, leaf] = 1.0 if bits[t] else -1.0
            nleft += 0 if bits[t] else 1
            cur = 2*cur + 1 + bits[t]
        thr[leaf, 0] = 3.5 - nleft
    h["aroute"] = A
    h["thr"] = thr

    node_wT = np.concatenate(
        [np.asarray(inp[f"L0_ffn_{f}_node_w"]).T for f in "qkv"], axis=1)
    h["node_w"] = _chunkmajor(node_wT)
    h["node_b"] = np.ascontiguousarray(np.stack(
        [np.asarray(inp[f"L0_ffn_{f}_node_b"]).astype(np.float32)
         for f in "qkv"], axis=1))

    bige = np.zeros((16, NP, 128), np.float32)
    for p in range(NP):
        bige[2*p, p, 0:64] = BIG
        bige[2*p+1, p, 64:128] = BIG
    h["bige"] = bige.reshape(16, NP * 128)
    half01 = np.zeros((2, 128), np.float32)
    half01[0, 0:64] = 1.0
    half01[1, 64:128] = 1.0
    h["half01"] = half01
    ebc = np.zeros((16, NP * 128), np.float32)
    for p in range(NP):
        ebc[p, p*128:p*128 + 64] = 1.0
        ebc[8 + p, p*128 + 64:p*128 + 128] = 1.0
    h["ebc"] = ebc
    h["onescol"] = np.ones((128, 1), np.float32)
    h["onesrow"] = np.ones((1, 128), np.float32)
    h["one11"] = np.ones((1, 1), np.float32)

    for f in "qkv":
        w1 = np.asarray(inp[f"L0_ffn_{f}_w1"])
        w1T = np.ascontiguousarray(w1.transpose(1, 0, 2).reshape(D, 1024))
        h[f"w1_{f}"] = _chunkmajor(w1T)
        w2 = np.asarray(inp[f"L0_ffn_{f}_w2"])
        h[f"w2_{f}"] = np.ascontiguousarray(
            w2.reshape(NP, 128, 1024).transpose(1, 0, 2).reshape(128, NP * 1024))
        b1 = np.asarray(inp[f"L0_ffn_{f}_b1"])
        h[f"b1_{f}"] = np.ascontiguousarray(
            b1.reshape(NP, 128).T).astype(np.float32)
        h[f"b2_{f}"] = np.asarray(inp[f"L0_ffn_{f}_b2"]).astype(np.float32)
    return h


HOSTSHAPES = {
    "w_q": [128, NC * 1024], "w_k": [128, NC * 1024],
    "w_v": [128, NC * 1024], "w_o": [128, NC * 1024],
    "cos": [128, S], "sin": [128, S], "prope": [128, 128], "ident": [128, 128],
    "mask01": [128, 4096], "enorm": [128, NP * 16],
    "aroute": [NNODE, NLEAF], "thr": [NLEAF, 1],
    "node_w": [128, NC * 45], "node_b": [NNODE, 3], "bige": [16, NP * 128],
    "half01": [2, 128], "onescol": [128, 1], "onesrow": [1, 128], "one11": [1, 1],
    "ebc": [16, 8 * 128],
}
for _f in "qkv":
    HOSTSHAPES[f"w1_{_f}"] = [128, 8192]
    HOSTSHAPES[f"w2_{_f}"] = [128, 8192]
    HOSTSHAPES[f"b1_{_f}"] = [128, 8]
    HOSTSHAPES[f"b2_{_f}"] = [16, 1024]

STRIPS = [(qc, kt) for qc in range(4) for kt in range(4 * qc + 4)]


def build_program(debug=False):
    nc = bass.Bass("TRN2", target_bir_lowering=False,
                   detect_race_conditions=False)

    def din(name, shape, dt=F32):
        return nc.dram_tensor(name, shape, dt, kind="ExternalInput").ap()

    x_h = din("x", [S, D])
    hin = {k: din(k, v) for k, v in HOSTSHAPES.items()}
    y_h = nc.dram_tensor("y", [S, D], F32, kind="ExternalOutput").ap()

    sk = "ExternalOutput" if debug else "Internal"
    attn_hbm = nc.dram_tensor("attn_sc", [S, D], F32, kind=sk).ap()
    q2_hbm = nc.dram_tensor("q2_sc", [128, NC * 2048], BF16, kind=sk).ap()
    k2_hbm = nc.dram_tensor("k2_sc", [128, NC * 2048], BF16, kind=sk).ap()
    v2_hbm = nc.dram_tensor("v2_sc", [128, NT * 1024], BF16, kind=sk).ap()
    arawA_hbm = nc.dram_tensor("arawA_sc", [64, NP * S], BF16, kind=sk).ap()
    arawB_hbm = nc.dram_tensor("arawB_sc", [64, NP * S], BF16, kind=sk).ap()
    sums_hbm = nc.dram_tensor("sums_sc", [2, NP, S], F32, kind=sk).ap()
    rsb_hbm = nc.dram_tensor("rsb_sc", [NNODE, 3 * S], BF16, kind="Internal").ap()

    def dbg(name, shape, dt=F32):
        if not debug:
            return None
        return nc.dram_tensor("dbg_" + name, shape, dt,
                              kind="ExternalOutput").ap()

    d_xnfm = dbg("xnfm", [128, NC * 2048], BF16)
    d_xnlo = dbg("xnlo", [128, NC * 2048], BF16)
    d_rsb = dbg("rsb", [NNODE, 3 * S], BF16)
    d_q = dbg("q", [128, NP * 2048], BF16)
    d_k = dbg("k", [128, NP * 2048], BF16)
    d_v = dbg("v", [128, NT * 1040], BF16)
    d_biasg = dbg("biasg", [128, 1])
    d_anorm = dbg("anorm", [128, NP * 2048], BF16)
    d_sel = dbg("sel", [16, 3 * S])
    d_hid = dbg("hid", [128, NP * 2048], BF16)
    d_bias2 = dbg("bias2", [128, 1])
    d_stats = dbg("stats", [128, 80])
    d_hstage = dbg("hstage", [128, 4096], BF16)
    d_xstage = dbg("xstage", [128, 4096])

    cnt = {}

    with contextlib.ExitStack() as _top:
        def _sb(name, shape, dt, side="left"):
            return _top.enter_context(nc.sbuf_tensor("sb_" + name, shape, dt, side=side))
        ident32 = _sb("ident32", [128, 128], F32)
        ident16 = _sb("ident16", [128, 128], BF16)
        prope = _sb("prope", [128, 128], BF16)
        enorm = _sb("enorm", [128, NP * 16], BF16)
        aroute = _sb("aroute", [NNODE, NLEAF], BF16)
        thr = _sb("thr", [NLEAF, 1], F32)
        nodeb = _sb("nodeb", [NNODE, 3], F32)
        bige = _sb("bige", [16, NP * 128], BF16)
        half01 = _sb("half01", [2, 128], F32R)
        onescol = _sb("onescol", [128, 1], BF16)
        onesrow = _sb("onesrow", [1, 128], BF16)
        one11 = _sb("one11", [1, 1], F32)
        nodewh = _sb("nodewh", [128, NC * 45], BF16)
        nodewl = _sb("nodewl", [128, NC * 45], BF16)
        biasg = _sb("biasg", [128, 1], F32)
        bias2 = _sb("bias2", [128, 1], F32)
        tiny = _sb("tiny", [128, 16], F32)
        tinyr = _sb("tinyr", [128, 4], BF16)
        pa = _top.enter_context(nc.psum_tensor("pa", [128, 2048], F32))
        pb = _top.enter_context(nc.psum_tensor("pb", [128, 2048], F32))
        sA = _top.enter_context(nc.semaphore("sA"))
        sB = _top.enter_context(nc.semaphore("sB"))
        sC = _top.enter_context(nc.semaphore("sC"))
        sD = _top.enter_context(nc.semaphore("sD"))
        sE = _top.enter_context(nc.semaphore("sE"))
        sF = _top.enter_context(nc.semaphore("sF"))
        sG = _top.enter_context(nc.semaphore("sG"))
        sH = _top.enter_context(nc.semaphore("sH"))
        s_w = _top.enter_context(nc.semaphore("s_w"))
        s_x = _top.enter_context(nc.semaphore("s_x"))
        s_tr = _top.enter_context(nc.semaphore("s_tr"))
        s_st = _top.enter_context(nc.semaphore("s_st"))
        s_out = _top.enter_context(nc.semaphore("s_out"))
        s_p = _top.enter_context(nc.semaphore("s_p"))
        SEM = {"sA": sA, "sB": sB, "sC": sC, "sD": sD, "sE": sE, "sF": sF,
               "sG": sG, "sH": sH, "s_w": s_w, "s_x": s_x, "s_tr": s_tr,
               "s_st": s_st, "s_out": s_out, "s_p": s_p}
        for k in SEM:
            cnt[k] = 0

        def INC(instr, name, k=1):
            instr.then_inc(SEM[name], k)
            cnt[name] += k
            return instr

        def wload(e, dst, src):
            INC(e.dma_start(out=dst, in_=src), "s_w", 16)

        def ostore(e, dst, src):
            INC(e.dma_start(out=dst, in_=src), "s_out", 16)

        with nc.sbuf_tensor("nodew32", [128, NC * 45], F32,
                            side="right") as nodew32:
            with nc.Block(no_gpsimd_drain=True) as blk:
                @blk.gpsimd
                def _(e):
                    for dst, key in (
                        (prope, "prope"), (enorm, "enorm"), (aroute, "aroute"),
                        (bige, "bige"), (onescol, "onescol"),
                        (ident32, "ident"), (ident16, "ident"), (thr, "thr"),
                        (nodeb, "node_b"), (half01, "half01"),
                        (onesrow, "onesrow"), (one11, "one11"),
                        (nodew32, "node_w"),
                    ):
                        wload(e, dst[:], hin[key])
                    e.wait_ge(s_w, 16 * (cnt["s_w"] // 16))

            with nc.Block(no_gpsimd_drain=True) as blk:
                @blk.scalar
                def _(e):
                    e.copy(nodewh[:], nodew32[:])
            with nc.Block(no_gpsimd_drain=True) as blk:
                @blk.vector
                def _(e):
                    e.tensor_sub(nodewl[:], nodew32[:], nodewh[:])

        with nc.sbuf_tensor("xnfm", [128, NC * 2048], BF16, side="left") as xnfm:
            with contextlib.ExitStack() as stW:
                wqh = stW.enter_context(nc.sbuf_tensor(
                    "wqh", [128, NC * 1024], BF16, side="right"))
                wv = stW.enter_context(nc.sbuf_tensor(
                    "wv", [128, NC * 1024], BF16, side="right"))
                cossb = stW.enter_context(nc.sbuf_tensor(
                    "cossb", [128, S], BF16, side="right"))
                sinsb = stW.enter_context(nc.sbuf_tensor(
                    "sinsb", [128, S], BF16, side="right"))
                with (
                    nc.sbuf_tensor("rsb", [NNODE, 3 * S], BF16,
                                   side="right") as rsb,
                    nc.sbuf_tensor("xnlo", [128, NC * 2048], BF16,
                                   side="right") as xnlo,
                    nc.sbuf_tensor("xstage", [128, 8 * 1024], F32,
                                   side="right") as xstage,
                    nc.sbuf_tensor("xnf32", [128, 1024], F32,
                                   side="right") as xnf32,
                    nc.sbuf_tensor("hstage", [128, 8 * 1024], BF16,
                                   side="right") as hstage,
                    nc.sbuf_tensor("lstage", [128, 8 * 1024], BF16,
                                   side="right") as lstage,
                    nc.sbuf_tensor("stats", [128, 96], F32,
                                   side="right") as stats,
                ):
                    with nc.Block(no_gpsimd_drain=True) as blk:
                        @blk.gpsimd
                        def _(e):
                            wload(e, wqh[:], hin["w_q"])
                            wload(e, wv[:], hin["w_v"])
                            wload(e, cossb[:], hin["cos"])
                            wload(e, sinsb[:], hin["sin"])

                    bB0, bC0, bE0, bF0, bG0, bH0 = (
                        cnt["sB"], cnt["sC"], cnt["sE"], cnt["sF"],
                        cnt["sG"], cnt["sH"])
                    bX0 = cnt["s_x"]
                    with nc.Block(no_gpsimd_drain=True) as blk:
                        @blk.sync
                        def _(e):
                            for g in range(4):
                                for i in range(4):
                                    t = g * 4 + i
                                    gs = (g % 2) * 4096
                                    if g >= 2 and i == 0:
                                        e.wait_ge(sB, bB0 + 4 * (g - 1))
                                        e.wait_ge(sC, bC0 + 4 * (g - 1))
                                    INC(e.dma_start(
                                        out=xstage[:, gs + i*1024:
                                                   gs + (i+1)*1024],
                                        in_=x_h[t*128:(t+1)*128, :]),
                                        "s_x", 16)

                        @blk.scalar
                        def _(e):
                            for g in range(4):
                                t0 = g * 4
                                gs = (g % 2) * 4096
                                sc = 64 + (g % 2) * 16
                                e.wait_ge(s_x, bX0 + 64 * (g + 1))
                                for i in range(4):
                                    t = t0 + i
                                    ins = e.activation(
                                        xnf32[:],
                                        xstage[:, gs + i*1024:gs + (i+1)*1024],
                                        AF.Square,
                                        accum_out=stats[:, 16+t:17+t])
                                    if i == 3:
                                        INC(ins, "sH")
                                e.wait_ge(sG, bG0 + 2 * g + 1)
                                INC(e.activation(stats[:, sc+8:sc+12],
                                                 stats[:, sc+4:sc+8],
                                                 AF.Sqrt), "sH")
                                e.wait_ge(sG, bG0 + 2 * g + 2)
                                for i in range(4):
                                    t = t0 + i
                                    INC(e.activation(
                                        hstage[:, gs + i*1024:gs + (i+1)*1024],
                                        xstage[:, gs + i*1024:gs + (i+1)*1024],
                                        AF.Identity,
                                        bias=stats[:, 48+t:49+t],
                                        scale=stats[:, 32+t:33+t]), "sB")
                                for i in range(4):
                                    t = t0 + i
                                    gi4 = g * 4 + i
                                    ps = pa if gi4 % 2 == 0 else pb
                                    e.wait_ge(sE, bE0 + gi4 + 1)
                                    INC(e.copy(xnfm.ap().rearrange(
                                        "p (c n) -> p c n",
                                        c=NC)[:, :, t*128:(t+1)*128],
                                        ps.ap()[:, 0:1024].rearrange(
                                            "p (c n) -> p c n", c=NC)), "sF")
                                    INC(e.copy(xnlo.ap().rearrange(
                                        "p (c n) -> p c n",
                                        c=NC)[:, :, t*128:(t+1)*128],
                                        ps.ap()[:, 1024:2048].rearrange(
                                            "p (c n) -> p c n", c=NC)), "sF")

                        @blk.vector
                        def _(e):
                            for g in range(4):
                                t0 = g * 4
                                gs = (g % 2) * 4096
                                sc = 64 + (g % 2) * 16
                                m4 = stats[:, sc:sc+4]
                                v4 = stats[:, sc+4:sc+8]
                                w4 = stats[:, sc+8:sc+12]
                                u4 = stats[:, sc+12:sc+16]
                                r4 = stats[:, 32+t0:36+t0]
                                e.wait_ge(s_x, bX0 + 64 * (g + 1))
                                for i in range(4):
                                    t = t0 + i
                                    e.reduce_sum(
                                        stats[:, t:t+1],
                                        xstage[:, gs + i*1024:gs + (i+1)*1024],
                                        axis=AX.X)
                                e.wait_ge(sH, bH0 + 2 * g + 1)
                                e.tensor_scalar(m4, stats[:, t0:t0+4],
                                                1.0 / D, None, ALU.mult)
                                e.tensor_scalar(v4, stats[:, 16+t0:20+t0],
                                                1.0 / D, None, ALU.mult)
                                e.drain()
                                e.tensor_mul(w4, m4, m4)
                                e.drain()
                                e.tensor_sub(v4, v4, w4)
                                e.drain()
                                INC(e.tensor_scalar(v4, v4, EPS, None,
                                                    ALU.add), "sG")
                                e.drain()
                                e.tensor_scalar(m4, m4, -1.0, None, ALU.mult)
                                e.wait_ge(sH, bH0 + 2 * g + 2)
                                e.reciprocal(r4, w4)
                                e.drain()
                                e.tensor_mul(u4, r4, r4)
                                e.drain()
                                e.tensor_mul(u4, u4, v4)
                                e.drain()
                                e.tensor_scalar(u4, u4, -0.5, 1.5,
                                                ALU.mult, ALU.add)
                                e.drain()
                                e.tensor_mul(r4, r4, u4)
                                e.drain()
                                INC(e.tensor_mul(stats[:, 48+t0:52+t0],
                                                 m4, r4), "sG")
                                for i in range(4):
                                    t = t0 + i
                                    e.tensor_scalar(
                                        xnf32[:],
                                        xstage[:, gs + i*1024:gs + (i+1)*1024],
                                        stats[:, 32+t:33+t],
                                        stats[:, 48+t:49+t],
                                        ALU.mult, ALU.add)
                                    e.wait_ge(sB, bB0 + 4 * g + i + 1)
                                    INC(e.tensor_sub(
                                        lstage[:, gs + i*1024:gs + (i+1)*1024],
                                        xnf32[:],
                                        hstage[:, gs + i*1024:
                                               gs + (i+1)*1024]), "sC")

                        @blk.tensor
                        def _(e):
                            for g in range(4):
                                t0 = g * 4
                                gs = (g % 2) * 4096
                                for i in range(4):
                                    gi4 = g * 4 + i
                                    ps = pa if gi4 % 2 == 0 else pb
                                    e.wait_ge(sB, bB0 + 4 * g + i + 1)
                                    e.wait_ge(sC, bC0 + 4 * g + i + 1)
                                    if gi4 >= 2:
                                        e.wait_ge(sF, bF0 + 2 * (gi4 - 1))
                                    last = None
                                    for half, stg in ((0, hstage),
                                                      (1, lstage)):
                                        for dc in range(NC):
                                            last = e.matmul(
                                                ps[:, half*1024 + dc*128:
                                                   half*1024 + (dc+1)*128],
                                                stg[:, gs + i*1024 + dc*128:
                                                    gs + i*1024 + (dc+1)*128],
                                                ident16[:],
                                                start=True, stop=True)
                                    INC(last, "sE")

                    bE, bF = cnt["sE"], cnt["sF"]
                    with nc.Block(no_gpsimd_drain=True) as blk:
                        @blk.tensor
                        def _(e, bE=bE, bF=bF):
                            for f in range(3):
                                if f > 0:
                                    e.wait_ge(sF, bF + f)
                                last = None
                                for ps, (wt, xt) in enumerate(
                                        ((nodewh, xnfm), (nodewl, xnfm),
                                         (nodewh, xnlo))):
                                    for dc in range(NC):
                                        st = (ps == 0 and dc == 0)
                                        sp = (ps == 2 and dc == NC - 1)
                                        for gq in range(4):
                                            last = e.matmul(
                                                pa[0:NNODE, gq*512:(gq+1)*512],
                                                wt[:, dc*45 + f*15:
                                                   dc*45 + (f+1)*15],
                                                xt[:, dc*2048 + gq*512:
                                                   dc*2048 + (gq+1)*512],
                                                start=st, stop=sp)
                                INC(last, "sE")

                        @blk.vector
                        def _(e, bE=bE, bF=bF):
                            for f in range(3):
                                e.wait_ge(sE, bE + f + 1)
                                INC(e.tensor_scalar(
                                    rsb[:, f*S:(f+1)*S], pa[0:NNODE, :],
                                    nodeb[:, f:f+1], 0.0,
                                    ALU.add, ALU.is_ge), "sF")

                    with nc.Block(no_gpsimd_drain=True) as blk:
                        @blk.gpsimd
                        def _(e):
                            wload(e, rsb_hbm, rsb[:])
                            if debug:
                                ostore(e, d_xnfm, xnfm[:])
                                ostore(e, d_xnlo, xnlo[:])
                                ostore(e, d_rsb, rsb[:])
                            e.wait_ge(s_w, cnt["s_w"])

                with (
                    nc.sbuf_tensor("qsb", [128, NP * 2048], BF16,
                                   side="left") as qsb,
                    nc.sbuf_tensor("ksb", [128, NP * 2048], BF16,
                                   side="left") as ksb,
                    nc.sbuf_tensor("vsb", [128, NT * 1040], BF16,
                                   side="left") as vsb,
                ):
                    bA, bB, bC, bD = cnt["sA"], cnt["sB"], cnt["sC"], cnt["sD"]
                    sw_q = cnt["s_w"]
                    NU = 32
                    SLOT = (pa.ap()[:, 0:1024], pa.ap()[:, 1024:2048],
                            pb.ap()[:, 0:1024])
                    ROT = pb.ap()[:, 1024:2048]
                    with nc.Block(no_gpsimd_drain=True) as blk:
                        @blk.gpsimd
                        def _(e, bA=bA):
                            e.wait_ge(sA, bA + 16)
                            wload(e, wqh[:], hin["w_k"])

                        @blk.tensor
                        def _(e, bA=bA, bB=bB, bC=bC, sw_q=sw_q,
                              sw_k=cnt["s_w"]):
                            e.wait_ge(s_w, sw_q)

                            def emit_mm(u):
                                if u == 16:
                                    e.wait_ge(s_w, sw_k)
                                cc, h = u // 2, u % 2
                                pcc = cc % 8
                                sp = SLOT[u % 3]
                                if u >= 3:
                                    e.wait_ge(sB, bB + u - 2)
                                last = None
                                for dc in range(NC):
                                    for g2 in range(2):
                                        last = e.matmul(
                                            sp[:, g2*512:(g2+1)*512],
                                            wqh[:, dc*1024 + pcc*128:
                                                dc*1024 + (pcc+1)*128],
                                            xnfm[:, dc*2048 + h*1024
                                                 + g2*512:
                                                 dc*2048 + h*1024
                                                 + (g2+1)*512],
                                            start=(dc == 0), stop=(dc == NC-1))
                                INC(last, "sA")

                            def emit_rot(u):
                                cc, h = u // 2, u % 2
                                dstsb = qsb if cc < 8 else ksb
                                pcc = cc % 8
                                base = pcc*2048 + h*1024
                                e.wait_ge(sB, bB + u + 1)
                                if u >= 1:
                                    e.wait_ge(sC, bC + 2 * u)
                                last = None
                                for j in range(2):
                                    last = e.matmul(
                                        ROT[:, j*512:(j+1)*512], prope[:],
                                        dstsb[:, base + j*512:
                                              base + (j+1)*512],
                                        start=True, stop=True)
                                INC(last, "sD")

                            for u in range(NU + 1):
                                if u < NU:
                                    emit_mm(u)
                                if u >= 1:
                                    emit_rot(u - 1)

                        @blk.scalar
                        def _(e, bA=bA):
                            for u in range(NU):
                                cc, h = u // 2, u % 2
                                dstsb = qsb if cc < 8 else ksb
                                pcc = cc % 8
                                base = pcc*2048 + h*1024
                                e.wait_ge(sA, bA + u + 1)
                                INC(e.copy(dstsb[:, base:base + 1024],
                                           SLOT[u % 3]), "sB")

                        @blk.vector
                        def _(e, bD=bD):
                            for u in range(NU):
                                cc, h = u // 2, u % 2
                                dstsb = qsb if cc < 8 else ksb
                                pcc = cc % 8
                                sl = dstsb[:, pcc*2048 + h*1024:
                                           pcc*2048 + (h+1)*1024]
                                e.wait_ge(sD, bD + u + 1)
                                INC(e.tensor_mul(
                                    ROT[:, :], ROT[:, :],
                                    sinsb[:, h*1024:(h+1)*1024]), "sC")
                                e.tensor_mul(sl, sl,
                                             cossb[:, h*1024:(h+1)*1024])
                                INC(e.tensor_add(sl, sl, ROT[:, :]), "sC")

                    with nc.sbuf_tensor("sqsb", [128, 2 * 2048], BF16,
                                        side="right") as sqsb:
                        bA, bB, bC, bD, bE = (cnt["sA"], cnt["sB"], cnt["sC"],
                                              cnt["sD"], cnt["sE"])
                        with nc.Block(no_gpsimd_drain=True) as blk:
                            @blk.vector
                            def _(e, bB=bB):
                                for tt in range(NT):
                                    vv = vsb.ap()[:, tt*1040:(tt+1)*1040]
                                    vv3 = vv.rearrange("p (h z) -> p h z", h=16)
                                    e.memset(vv3[:, :, 64:65], 1.0)
                                for cc in range(16):
                                    srcsb = qsb if cc < 8 else ksb
                                    pcc = cc % 8
                                    sl = srcsb[:, pcc*2048:(pcc+1)*2048]
                                    if cc >= 2:
                                        e.wait_ge(sB, bB + cc - 1)
                                    INC(e.tensor_mul(
                                        sqsb[:, (cc % 2)*2048:
                                             (cc % 2 + 1)*2048], sl, sl), "sA")
                                    if cc == 9:
                                        e.wait_ge(sB, bB + 8)
                                        INC(e.reduce_max(tiny[0:16, 0:1],
                                                         pb[0:16, :],
                                                         axis=AX.X), "sC")
                                e.wait_ge(sB, bB + 16)
                                INC(e.reduce_max(tiny[0:16, 1:2], pb[0:16, :],
                                                 axis=AX.X), "sC")

                            @blk.tensor
                            def _(e, bA=bA, bC=bC, bE=bE):
                                for tt in range(NT):
                                    half = (tt % 2) * 1024
                                    if tt >= 2:
                                        e.wait_ge(sE, bE + tt - 1)
                                    last = None
                                    for dc in range(NC):
                                        for gq in range(2):
                                            last = e.matmul(
                                                pa[:, half + gq*512:
                                                   half + (gq+1)*512],
                                                xnfm[:, dc*2048 + tt*128:
                                                     dc*2048 + (tt+1)*128],
                                                wv[:, dc*1024 + gq*512:
                                                   dc*1024 + (gq+1)*512],
                                                start=(dc == 0),
                                                stop=(dc == NC-1))
                                    INC(last, "sF")
                                    cc = tt
                                    if cc == 8:
                                        e.wait_ge(sC, bC + 1)
                                    e.wait_ge(sA, bA + cc + 1)
                                    last = None
                                    for gq in range(4):
                                        last = e.matmul(
                                            pb[0:16, gq*512:(gq+1)*512],
                                            enorm[:, (cc % 8)*16:
                                                  (cc % 8 + 1)*16],
                                            sqsb[:, (cc % 2)*2048 + gq*512:
                                                 (cc % 2)*2048 + (gq+1)*512],
                                            start=(cc % 8 == 0),
                                            stop=(cc % 8 == 7))
                                    INC(last, "sB")

                            @blk.scalar
                            def _(e):
                                bF = cnt["sF"] - NT
                                for tt in range(NT):
                                    half = (tt % 2) * 1024
                                    e.wait_ge(sF, bF + tt + 1)
                                    src = pa.ap()[:, half:half+1024].rearrange(
                                        "p (h z) -> p h z", h=16)
                                    vv = vsb.ap()[:, tt*1040:(tt+1)*1040]
                                    dst = vv.rearrange(
                                        "p (h z) -> p h z", h=16)[:, :, 0:64]
                                    INC(e.copy(dst, src), "sE")

                    g1A, g1B, g1C = cnt["sA"], cnt["sB"], cnt["sC"]
                    with nc.Block(no_gpsimd_drain=True) as blk:
                        @blk.scalar
                        def _(e, g1A=g1A, g1B=g1B, g1C=g1C):
                            e.activation(tiny[0:16, 2:3], tiny[0:16, 0:1],
                                         AF.Sqrt)
                            INC(e.activation(tiny[0:16, 3:4], tiny[0:16, 1:2],
                                             AF.Sqrt), "sA")
                            e.wait_ge(sB, g1B + 1)
                            INC(e.copy(tinyr[0:16, 0:1], tiny[0:16, 4:5]),
                                "sA")
                            e.wait_ge(sB, g1B + 2)
                            INC(e.copy(tinyr[0:1, 1:2], tiny[0:1, 5:6]), "sA")
                            e.wait_ge(sC, g1C + 2)
                            e.activation(biasg[:], pa[:, 16:17], AF.Copy,
                                         scale=-1.0 / 8.0)

                        @blk.vector
                        def _(e, g1A=g1A, g1B=g1B, g1C=g1C):
                            e.wait_ge(sA, g1A + 1)
                            INC(e.tensor_mul(tiny[0:16, 4:5], tiny[0:16, 2:3],
                                             tiny[0:16, 3:4]), "sB")
                            e.wait_ge(sC, g1C + 1)
                            INC(e.reduce_max(tiny[0:1, 5:6], pa[0:1, 0:16],
                                             axis=AX.X), "sB")

                        @blk.tensor
                        def _(e, g1A=g1A, g1C=g1C):
                            e.wait_ge(sA, g1A + 2)
                            INC(e.matmul(pa[0:1, 0:16], tinyr[0:16, 0:1],
                                         ident16[0:16, 0:16], start=True,
                                         stop=True), "sC")
                            e.wait_ge(sA, g1A + 3)
                            INC(e.matmul(pa[:, 16:17], onesrow[:],
                                         tinyr[0:1, 1:2],
                                         start=True, stop=True), "sC")

                    if debug:
                        with nc.Block(no_gpsimd_drain=True) as blk:
                            @blk.gpsimd
                            def _(e):
                                ostore(e, d_q, qsb[:])
                                ostore(e, d_k, ksb[:])
                                ostore(e, d_v, vsb[:])
                                ostore(e, d_biasg, biasg[:])

                    stW.close()
                    with (
                        nc.sbuf_tensor("psb", [128, 6 * 1024], BF16,
                                       side="right") as psb,
                        nc.sbuf_tensor("mask01sb", [128, 4096], BF16,
                                       side="right") as mask01,
                        nc.sbuf_tensor("stgA", [64, 2 * S], BF16,
                                       side="right") as stgA,
                        nc.sbuf_tensor("stgB", [64, 2 * S], BF16,
                                       side="right") as stgB,
                        nc.sbuf_tensor("stgS", [128, 2048], F32,
                                       side="right") as stgS,
                    ):
                        with nc.Block(no_gpsimd_drain=True) as blk:
                            @blk.gpsimd
                            def _(e):
                                wload(e, mask01[:], hin["mask01"])
                                e.wait_ge(s_w, cnt["s_w"])

                        ndiag = []
                        nd = 0
                        for (qc, kt) in STRIPS:
                            ndiag.append(nd)
                            if kt >= 4 * qc:
                                nd += 1
                        nsp = len(STRIPS)

                        bA, bB, bC, bD, bE = (cnt["sA"], cnt["sB"], cnt["sC"],
                                              cnt["sD"], cnt["sE"])
                        with nc.Block(no_gpsimd_drain=True) as blk:
                            @blk.tensor
                            def _(e, bA=bA, bB=bB, bC=bC, bD=bD, bE=bE):
                                allst = [(p, qc, kt) for p in range(NP)
                                         for (qc, kt) in STRIPS]

                                def emit_st(gi):
                                    p, qc, kt = allst[gi]
                                    buf = gi % 2
                                    sp = pa[:, buf*1024:(buf+1)*1024]
                                    if gi >= 2:
                                        e.wait_ge(sB, bB + gi - 1)
                                    mA = e.matmul(
                                        sp[:, 0:512],
                                        ksb[0:64, p*2048 + kt*128:
                                            p*2048 + (kt+1)*128],
                                        qsb[0:64, p*2048 + qc*512:
                                            p*2048 + (qc+1)*512],
                                        start=True, stop=True)
                                    mB = e.matmul(
                                        sp[:, 512:1024],
                                        ksb[64:128, p*2048 + kt*128:
                                            p*2048 + (kt+1)*128],
                                        qsb[64:128, p*2048 + qc*512:
                                            p*2048 + (qc+1)*512],
                                        start=True, stop=True)
                                    INC(mB, "sA")

                                def emit_pv(gi):
                                    p, qc, kt = allst[gi]
                                    si = gi - p * nsp
                                    nkt = 4 * qc + 4
                                    buf = gi % 6
                                    m = p * 4 + qc
                                    acc = (m % 2) * 1024
                                    if kt == 0 and m >= 2:
                                        e.wait_ge(sD, bD + m - 1)
                                    diag = kt >= 4 * qc
                                    if diag:
                                        e.wait_ge(sC, bC + p * 16
                                                  + ndiag[si] + 1)
                                    else:
                                        e.wait_ge(sB, bB + gi + 1)
                                    pvA = e.matmul(
                                        pb[0:65, acc:acc + 512],
                                        vsb[:, kt*1040 + (2*p)*65:
                                            kt*1040 + (2*p)*65 + 65],
                                        psb[:, buf*1024:buf*1024 + 512],
                                        start=(kt == 0), stop=(kt == nkt - 1))
                                    pvB = e.matmul(
                                        pb[0:65, acc + 512:acc + 1024],
                                        vsb[:, kt*1040 + (2*p+1)*65:
                                            kt*1040 + (2*p+1)*65 + 65],
                                        psb[:, buf*1024 + 512:
                                            buf*1024 + 1024],
                                        start=(kt == 0), stop=(kt == nkt - 1))
                                    INC(pvB, "sE")

                                n = len(allst)
                                for gi in range(n + 2):
                                    if gi < n:
                                        emit_st(gi)
                                    if gi >= 2:
                                        emit_pv(gi - 2)

                            @blk.scalar
                            def _(e, bA=bA, bE=bE):
                                for p in range(NP):
                                    for si, (qc, kt) in enumerate(STRIPS):
                                        gi = p * nsp + si
                                        pbuf = gi % 2
                                        buf = gi % 6
                                        sp = pa[:, pbuf*1024:(pbuf+1)*1024]
                                        e.wait_ge(sA, bA + gi + 1)
                                        if gi >= 6:
                                            e.wait_ge(sE, bE + gi - 5)
                                        INC(e.activation(
                                            psb[:, buf*1024:(buf+1)*1024],
                                            sp,
                                            AF.Exp, bias=biasg[:, 0:1],
                                            scale=1.0 / 8.0), "sB")

                            @blk.vector
                            def _(e, bE=bE, bH=cnt["s_st"]):
                                bB = cnt["sB"] - NP * nsp
                                last_gi = {}
                                for si, (qc, kt) in enumerate(STRIPS):
                                    if kt == 4 * qc + 3:
                                        last_gi[qc] = si
                                for p in range(NP):
                                    if p >= 2:
                                        e.wait_ge(s_st, bH + 160 * (p - 1))
                                    for si, (qc, kt) in enumerate(STRIPS):
                                        gi = p * nsp + si
                                        buf = gi % 6
                                        if kt >= 4 * qc:
                                            r = kt - 4 * qc
                                            e.wait_ge(sB, bB + gi + 1)
                                            INC(e.tensor_mul(
                                                psb[:, buf*1024:(buf+1)*1024],
                                                psb[:, buf*1024:(buf+1)*1024],
                                                mask01[:, r*1024:(r+1)*1024]),
                                                "sC")
                                        if kt == 4 * qc + 3:
                                            m = p * 4 + qc
                                            acc = (m % 2) * 1024
                                            if m >= 2:
                                                p2, qc2 = ((m - 2) // 4,
                                                           (m - 2) % 4)
                                                e.wait_ge(s_st, bH + 160 * p2
                                                          + 32 * (qc2 + 1))
                                            e.wait_ge(sE, bE + p * nsp
                                                      + last_gi[qc] + 1)
                                            cA = (qc % 2) * 1024
                                            cB = cA + 512
                                            sa = (p % 2) * 2048
                                            e.tensor_copy(
                                                stgA[:, sa + qc*512:
                                                     sa + (qc+1)*512],
                                                pb[0:64, acc:acc+512])
                                            e.tensor_copy(
                                                stgS[64:65, cA:cA+512],
                                                pb[64:65, acc:acc+512])
                                            e.tensor_copy(
                                                stgB[:, sa + qc*512:
                                                     sa + (qc+1)*512],
                                                pb[0:64, acc+512:acc+1024])
                                            INC(e.tensor_copy(
                                                stgS[64:65, cB:cB+512],
                                                pb[64:65, acc+512:acc+1024]),
                                                "sD")

                            @blk.sync
                            def _(e):
                                bD = cnt["sD"] - NP * 4
                                for p in range(NP):
                                    for qc in range(4):
                                        e.wait_ge(sD, bD + p * 4 + qc + 1)
                                        cA = (qc % 2) * 1024
                                        cB = cA + 512
                                        INC(e.dma_start(
                                            out=sums_hbm[0, p,
                                                         qc*512:(qc+1)*512],
                                            in_=stgS[64:65, cA:cA+512]),
                                            "s_st", 16)
                                        INC(e.dma_start(
                                            out=sums_hbm[1, p,
                                                         qc*512:(qc+1)*512],
                                            in_=stgS[64:65, cB:cB+512]),
                                            "s_st", 16)
                                    sa = (p % 2) * 2048
                                    INC(e.dma_start(
                                        out=arawA_hbm[:, p*S:(p+1)*S],
                                        in_=stgA[:, sa:sa+2048]), "s_st", 16)
                                    INC(e.dma_start(
                                        out=arawB_hbm[:, p*S:(p+1)*S],
                                        in_=stgB[:, sa:sa+2048]), "s_st", 16)
                                e.wait_ge(s_st, cnt["s_st"])

            q2sb = _top.enter_context(nc.sbuf_tensor(
                "q2sb", [128, NC * 2048], BF16, side="right"))
            with (
                nc.sbuf_tensor("w1a", [128, 8192], BF16, side="right") as w1a,
                nc.sbuf_tensor("w2a", [128, 8192], BF16, side="right") as w2a,
                nc.sbuf_tensor("b2r", [16, 2 * 1024], F32R, side="right") as b2r,
                nc.sbuf_tensor("b1sb", [128, 24], F32, side="right") as b1sb,
                nc.sbuf_tensor("rsb2", [NNODE, 3 * S], BF16,
                               side="right") as rsb2,
            ):
                sw_after_rsb = [0]
                with nc.Block(no_gpsimd_drain=True) as blk:
                    @blk.gpsimd
                    def _(e):
                        wload(e, w1a[:], hin["w1_q"])
                        wload(e, w2a[:], hin["w2_q"])
                        wload(e, b2r[:, 0:1024], hin["b2_q"])
                        wload(e, b2r[:, 1024:2048], hin["b2_k"])
                        for i, f in enumerate("qkv"):
                            wload(e, b1sb[:, i*8:(i+1)*8], hin[f"b1_{f}"])
                        wload(e, rsb2[:], rsb_hbm)
                        sw_after_rsb[0] = cnt["s_w"]

                with (
                    nc.sbuf_tensor("asb", [128, NP * 2048], BF16,
                                   side="right") as asb,
                    nc.sbuf_tensor("wo", [128, NC * 1024], BF16,
                                   side="right") as wo,
                    nc.sbuf_tensor("rsrc", [16, S], F32, side="right") as rsrc,
                    nc.sbuf_tensor("rdst", [16, S], F32R, side="right") as rdst,
                    nc.sbuf_tensor("ostage", [128, 2 * 1024], F32,
                                   side="right") as ostage,
                    nc.sbuf_tensor("ebcsb", [16, 8 * 128], F32R,
                                   side="right") as ebc,
                ):
                    bXs = cnt["s_x"]
                    swB = [0]
                    swE = [0]
                    with nc.Block(no_gpsimd_drain=True) as blk:
                        @blk.gpsimd
                        def _(e, bSt=cnt["s_st"]):
                            wload(e, wo[:], hin["w_o"])
                            wload(e, ebc[:], hin["ebc"])
                            swE[0] = cnt["s_w"]
                            e.wait_ge(s_st, bSt)
                            INC(e.dma_start(
                                out=asb.ap()[64:128, :],
                                in_=arawB_hbm), "s_w", 16)
                            swB[0] = cnt["s_w"]

                        @blk.sync
                        def _(e):
                            INC(e.dma_start(
                                out=rsrc[:],
                                in_=sums_hbm.rearrange("h p c -> (h p) c")),
                                "s_x", 16)
                            INC(e.dma_start(
                                out=asb.ap()[0:64, :],
                                in_=arawA_hbm), "s_x", 16)

                        @blk.vector
                        def _(e, bXs=bXs):
                            e.wait_ge(s_x, bXs + 16)
                            with nc.allow_low_precision(
                                    reason="float32r out is 4-byte fp32"):
                                e.reciprocal(rdst[:], rsrc[:])

                    bB, bC, bD = cnt["sB"], cnt["sC"], cnt["sD"]
                    with nc.Block(no_gpsimd_drain=True) as blk:
                        @blk.tensor
                        def _(e, bD=bD, swE=swE[0]):
                            for p in range(NP):
                                ps = pa if p % 2 == 0 else pb
                                if p == 0:
                                    e.wait_ge(s_w, swE)
                                if p >= 2:
                                    e.wait_ge(sD, bD + p - 1)
                                last = None
                                for gq in range(4):
                                    last = e.matmul(
                                        ps[:, gq*512:(gq+1)*512],
                                        ebc[:, p*128:(p+1)*128],
                                        rdst[0:16, gq*512:(gq+1)*512],
                                        start=True, stop=True)
                                INC(last, "sC")

                        @blk.vector
                        def _(e, bC=bC, bXs=bXs, swB=swB[0]):
                            for p in range(NP):
                                ps = pa if p % 2 == 0 else pb
                                if p == 0:
                                    e.wait_ge(s_x, bXs + 32)
                                    e.wait_ge(s_w, swB)
                                e.wait_ge(sC, bC + p + 1)
                                INC(e.tensor_mul(
                                    asb[:, p*2048:(p+1)*2048],
                                    asb[:, p*2048:(p+1)*2048], ps[:, :]), "sD")

                    if debug:
                        with nc.Block(no_gpsimd_drain=True) as blk:
                            @blk.gpsimd
                            def _(e):
                                ostore(e, d_anorm, asb[:])

                    bE, bF, bG = cnt["sE"], cnt["sF"], cnt["sG"]
                    sw_wo = cnt["s_w"]
                    with nc.Block(no_gpsimd_drain=True) as blk:
                        @blk.tensor
                        def _(e, bE=bE, bG=bG, sw_wo=sw_wo):
                            e.wait_ge(s_w, sw_wo)
                            for qt in range(NT):
                                half = (qt % 2) * 1024
                                if qt >= 2:
                                    e.wait_ge(sE, bE + qt - 1)
                                last = None
                                for p in range(NP):
                                    for gq in range(2):
                                        last = e.matmul(
                                            pb[:, half + gq*512:
                                               half + (gq+1)*512],
                                            asb[:, p*2048 + qt*128:
                                                p*2048 + (qt+1)*128],
                                            wo[:, p*1024 + gq*512:
                                               p*1024 + (gq+1)*512],
                                            start=(p == 0), stop=(p == NP-1))
                                INC(last, "sF")

                        @blk.scalar
                        def _(e, bF=bF, bG=cnt["s_st"]):
                            for qt in range(NT):
                                half = (qt % 2) * 1024
                                e.wait_ge(sF, bF + qt + 1)
                                if qt >= 2:
                                    e.wait_ge(s_st, bG + 16 * (qt - 1))
                                INC(e.copy(
                                    ostage[:, half:half + 1024],
                                    pb[:, half:half + 1024]), "sE")

                        @blk.sync
                        def _(e, bG=cnt["s_st"]):
                            bE2 = cnt["sE"] - NT
                            for qt in range(NT):
                                half = (qt % 2) * 1024
                                e.wait_ge(sE, bE2 + qt + 1)
                                if qt > 0:
                                    e.wait_ge(s_st, bG + 16 * qt)
                                INC(e.dma_start(
                                    out=attn_hbm[qt*128:(qt+1)*128, :],
                                    in_=ostage[:, half:half + 1024]), "s_st",
                                    16)

                with (
                    nc.sbuf_tensor("w1b", [128, 8192], BF16,
                                   side="right") as w1b,
                    nc.sbuf_tensor("w2b", [128, 8192], BF16,
                                   side="right") as w2b,
                    nc.sbuf_tensor("hid", [128, NP * 2048], BF16,
                                   side="right") as hid,
                    nc.sbuf_tensor("self32r", [16, S], F32R,
                                   side="right") as self32r,
                    nc.sbuf_tensor("maskm1", [16, S], BF16,
                                   side="right") as maskm1,
                    nc.sbuf_tensor("fstage", [128, 2 * 2048], BF16,
                                   side="right") as fstage,
                ):
                    with nc.Block(no_gpsimd_drain=True) as blk:
                        @blk.gpsimd
                        def _(e):
                            wload(e, w1b[:], hin["w1_k"])
                            wload(e, w2b[:], hin["w2_k"])

                    for fi, f in enumerate("qkv"):
                        w1x = (w1a, w1b, w1a)[fi]
                        w2x = (w2a, w2b, w2a)[fi]
                        dsth = (q2_hbm, k2_hbm, v2_hbm)[fi]

                        bA, bB, bC, bD, bE, bG = (
                            cnt["sA"], cnt["sB"], cnt["sC"], cnt["sD"],
                            cnt["sE"], cnt["s_st"])
                        sw_need = sw_after_rsb[0] if fi == 0 else cnt["s_w"]
                        with nc.Block(no_gpsimd_drain=True) as blk:
                            @blk.tensor
                            def _(e, fi=fi, w1x=w1x, w2x=w2x, bA=bA, bB=bB,
                                  bC=bC, bD=bD, bE=bE, bG=bG,
                                  sw_need=sw_need):
                                e.wait_ge(s_w, sw_need)
                                last = None
                                for gq in range(4):
                                    last = e.matmul(
                                        pb[0:16, gq*512:(gq+1)*512], aroute[:],
                                        rsb2[:, fi*S + gq*512:
                                             fi*S + (gq+1)*512],
                                        start=True, stop=True)
                                INC(last, "sA")
                                for p in range(NP):
                                    ps = pa if p % 2 == 0 else pb
                                    if p == 1:
                                        e.wait_ge(sB, bB + 1)
                                    if p >= 2:
                                        e.wait_ge(sC, bC + p - 1)
                                    for dc in range(NC):
                                        for gq in range(4):
                                            e.matmul(
                                                ps[:, gq*512:(gq+1)*512],
                                                w1x[:, dc*1024 + p*128:
                                                    dc*1024 + (p+1)*128],
                                                xnfm[:, dc*2048 + gq*512:
                                                     dc*2048 + (gq+1)*512],
                                                start=(dc == 0), stop=False)
                                    e.wait_ge(sB, bB + 1)
                                    last = None
                                    for gq in range(4):
                                        last = e.matmul(
                                            ps[:, gq*512:(gq+1)*512],
                                            bige[:, p*128:(p+1)*128],
                                            maskm1[0:16, gq*512:(gq+1)*512],
                                            start=False, stop=True)
                                    INC(last, "sD")
                                if fi < 2:
                                    for oc in range(NC):
                                        ps = pa if oc % 2 == 0 else pb
                                        e.wait_ge(sC, bC + NP)
                                        if oc >= 2:
                                            e.wait_ge(sE, bE + oc - 1)
                                        for p in range(NP):
                                            for gq in range(4):
                                                e.matmul(
                                                    ps[:, gq*512:(gq+1)*512],
                                                    w2x[:, p*1024 + oc*128:
                                                        p*1024 + (oc+1)*128],
                                                    hid[:, p*2048 + gq*512:
                                                        p*2048 + (gq+1)*512],
                                                    start=(p == 0), stop=False)
                                        last = None
                                        for gq in range(4):
                                            last = e.matmul(
                                                ps[:, gq*512:(gq+1)*512],
                                                b2r[0:16,
                                                    (fi % 2)*1024 + oc*128:
                                                    (fi % 2)*1024
                                                    + (oc+1)*128],
                                                self32r[0:16,
                                                        gq*512:(gq+1)*512],
                                                start=False, stop=True)
                                        INC(last, "sF")
                                else:
                                    for tt in range(NT):
                                        ps = pa if tt % 2 == 0 else pb
                                        half = 0
                                        e.wait_ge(sC, bC + NP)
                                        if tt >= 2:
                                            e.wait_ge(sE, bE + tt - 1)
                                        for p in range(NP):
                                            for gq in range(2):
                                                e.matmul(
                                                    ps[:, gq*512:(gq+1)*512],
                                                    hid[:, p*2048 + tt*128:
                                                        p*2048 + (tt+1)*128],
                                                    w2x[:, p*1024 + gq*512:
                                                        p*1024 + (gq+1)*512],
                                                    start=(p == 0), stop=False)
                                        last = None
                                        for gq in range(2):
                                            last = e.matmul(
                                                ps[:, gq*512:(gq+1)*512],
                                                self32r[0:16,
                                                        tt*128:(tt+1)*128],
                                                b2r[0:16,
                                                    (fi % 2)*1024 + gq*512:
                                                    (fi % 2)*1024
                                                    + (gq+1)*512],
                                                start=False, stop=True)
                                        INC(last, "sF")

                            @blk.vector
                            def _(e, fi=fi, bA=bA):
                                e.wait_ge(sA, bA + 1)
                                e.tensor_scalar(
                                    self32r[0:16, :], pb[0:16, :],
                                    thr[:, 0:1], None, ALU.is_ge)
                                INC(e.tensor_scalar(
                                    maskm1[0:16, :], pb[0:16, :],
                                    thr[:, 0:1], -1.0, ALU.is_ge, ALU.add),
                                    "sB")

                            @blk.scalar
                            def _(e, fi=fi, bD=bD):
                                for p in range(NP):
                                    ps = pa if p % 2 == 0 else pb
                                    e.wait_ge(sD, bD + p + 1)
                                    INC(e.activation(
                                        hid[:, p*2048:(p+1)*2048], ps[:, :],
                                        AF.Relu,
                                        bias=b1sb[:, fi*8+p:fi*8+p+1]), "sC")
                                nsub = NC if fi < 2 else NT
                                w = 2048 if fi < 2 else 1024
                                bF = cnt["sF"] - nsub
                                for u in range(nsub):
                                    ps = pa if u % 2 == 0 else pb
                                    e.wait_ge(sF, bF + u + 1)
                                    if fi == 0:
                                        INC(e.copy(
                                            q2sb[:, u*2048:(u+1)*2048],
                                            ps[:, 0:w]), "sE")
                                        continue
                                    if u >= 2:
                                        e.wait_ge(s_st, bG + 16 * (u - 1))
                                    INC(e.copy(
                                        fstage[:, (u % 2)*2048:
                                               (u % 2)*2048 + w],
                                        ps[:, 0:w]), "sE")

                            @blk.sync
                            def _(e, fi=fi, dsth=dsth, bG=bG):
                                if fi == 0:
                                    return
                                nsub = NC if fi < 2 else NT
                                w = 2048 if fi < 2 else 1024
                                bE2 = cnt["sE"] - nsub
                                for u in range(nsub):
                                    e.wait_ge(sE, bE2 + u + 1)
                                    if u > 0:
                                        e.wait_ge(s_st, bG + 16 * u)
                                    INC(e.dma_start(
                                        out=dsth[:, u*w:(u+1)*w],
                                        in_=fstage[:, (u % 2)*2048:
                                                   (u % 2)*2048 + w]),
                                        "s_st", 16)

                            if fi == 1:
                                @blk.gpsimd
                                def _(e):
                                    wload(e, w1a[:], hin["w1_v"])
                                    wload(e, w2a[:], hin["w2_v"])
                                    wload(e, b2r[:, 0:1024], hin["b2_v"])

                        if debug and f == "q":
                            with nc.Block(no_gpsimd_drain=True) as blk:
                                @blk.gpsimd
                                def _(e, fi=fi):
                                    ostore(e, d_hid, hid[:])
                                    ostore(e, d_sel.rearrange(
                                        "p (f s) -> p f s", f=3)[:, fi, :],
                                        self32r[0:16, :])
                                    ostore(e, q2_hbm, q2sb[:])

        with (
            nc.sbuf_tensor("k2sb", [128, NC * 2048], BF16, side="right") as k2sb,
            nc.sbuf_tensor("v2sb", [128, NT * 1024], BF16, side="right") as v2sb,
        ):
            b_ld = {}
            with nc.Block(no_gpsimd_drain=True) as blk:
                @blk.scalar
                def _(e):
                    e.wait_ge(s_st, cnt["s_st"])
                    INC(e.dma_start(out=k2sb[:, 0:8192],
                                    in_=k2_hbm[:, 0:8192]), "s_tr", 16)
                    b_ld["k2"] = cnt["s_tr"]

                @blk.sync
                def _(e):
                    e.wait_ge(s_st, cnt["s_st"])
                    INC(e.dma_start(out=k2sb[:, 8192:16384],
                                    in_=k2_hbm[:, 8192:16384]), "s_x", 16)
                    b_ld["k2x"] = cnt["s_x"]

                @blk.gpsimd
                def _(e):
                    e.wait_ge(s_st, cnt["s_st"])
                    INC(e.dma_start(out=v2sb[:], in_=v2_hbm), "s_w", 16)
                    b_ld["v2"] = cnt["s_w"]

            with nc.sbuf_tensor("sq2", [128, 2 * 2048], BF16,
                                side="right") as sq2:
                bA, bB = cnt["sA"], cnt["sB"]
                with nc.Block(no_gpsimd_drain=True) as blk:
                    @blk.vector
                    def _(e, bA=bA, bB=bB):
                        for i, src in enumerate((q2sb, k2sb)):
                            for dc in range(NC):
                                j = i * NC + dc
                                if i == 1 and dc == 0:
                                    e.wait_ge(s_tr, b_ld["k2"])
                                if i == 1 and dc == 4:
                                    e.wait_ge(s_x, b_ld["k2x"])
                                if j >= 2:
                                    e.wait_ge(sB, bB + j - 1)
                                INC(e.tensor_mul(
                                    sq2[:, (j % 2)*2048:(j % 2 + 1)*2048],
                                    src[:, dc*2048:(dc+1)*2048],
                                    src[:, dc*2048:(dc+1)*2048]), "sA")

                    @blk.tensor
                    def _(e, bA=bA):
                        for i in range(2):
                            ps = pa if i == 0 else pb
                            for dc in range(NC):
                                j = i * NC + dc
                                e.wait_ge(sA, bA + j + 1)
                                last = None
                                for gq in range(4):
                                    last = e.matmul(
                                        ps[0:1, gq*512:(gq+1)*512], onescol[:],
                                        sq2[:, (j % 2)*2048 + gq*512:
                                            (j % 2)*2048 + (gq+1)*512],
                                        start=(dc == 0), stop=(dc == NC - 1))
                                INC(last, "sB")

                b2A, b2B, b2C = cnt["sA"], cnt["sB"], cnt["sC"]
                with nc.Block(no_gpsimd_drain=True) as blk:
                    @blk.vector
                    def _(e, b2B=b2B):
                        e.reduce_max(tiny[0:1, 6:7], pa[0:1, :], axis=AX.X)
                        INC(e.reduce_max(tiny[0:1, 7:8], pb[0:1, :],
                                         axis=AX.X), "sA")
                        e.wait_ge(sB, b2B + 1)
                        INC(e.tensor_mul(tiny[0:1, 10:11], tiny[0:1, 8:9],
                                         tiny[0:1, 9:10]), "sB")

                    @blk.scalar
                    def _(e, b2A=b2A, b2B=b2B, b2C=b2C):
                        e.wait_ge(sA, b2A + 1)
                        e.activation(tiny[0:1, 8:9], tiny[0:1, 6:7], AF.Sqrt)
                        INC(e.activation(tiny[0:1, 9:10], tiny[0:1, 7:8],
                                         AF.Sqrt), "sB")
                        e.wait_ge(sB, b2B + 2)
                        INC(e.copy(tinyr[0:1, 2:3], tiny[0:1, 10:11]), "sB")
                        e.wait_ge(sC, b2C + 1)
                        e.activation(bias2[:], pa[:, 16:17], AF.Copy,
                                     scale=-1.0 / 32.0)

                    @blk.tensor
                    def _(e, b2B=b2B):
                        e.wait_ge(sB, b2B + 3)
                        INC(e.matmul(pa[:, 16:17], onesrow[:],
                                     tinyr[0:1, 2:3],
                                     start=True, stop=True), "sC")
                if debug:
                    with nc.Block(no_gpsimd_drain=True) as blk:
                        @blk.gpsimd
                        def _(e):
                            ostore(e, d_bias2, bias2[:])

            with (
                nc.sbuf_tensor("p2sb", [128, 2 * 16 * 512], BF16,
                               side="right") as p2sb,
                nc.sbuf_tensor("o2sb", [128, 2 * NC * 512], F32,
                               side="right") as o2sb,
                nc.sbuf_tensor("r2row", [1, 512], F32, side="right") as r2row,
                nc.sbuf_tensor("r2col", [128, 2], F32, side="right") as r2col,
                nc.sbuf_tensor("xstg", [128, 2 * 1024], F32,
                               side="right") as xstg,
                nc.sbuf_tensor("astg", [128, 2 * 1024], F32,
                               side="right") as astg,
                nc.sbuf_tensor("ysb", [128, 2 * 1024], F32,
                               side="right") as ysb,
            ):
                bA, bB, bC, bD, bE, bF, bG, bH = (
                    cnt["sA"], cnt["sB"], cnt["sC"], cnt["sD"], cnt["sE"],
                    cnt["sF"], cnt["sG"], cnt["sH"])
                bX, bO, bP = cnt["s_x"], cnt["s_out"], cnt.get("s_p", 0)
                with nc.Block(no_gpsimd_drain=True) as blk:
                    @blk.tensor
                    def _(e):
                        for qc in range(4):
                            pslot = (qc % 2) * 16 * 512
                            oslot = (qc % 2) * NC * 512

                            def emit_sc(kt, qc=qc):
                                buf = kt % 2
                                if qc * 16 + kt >= 2:
                                    e.wait_ge(sB, bB + qc * 16 + kt - 1)
                                last = None
                                for dc in range(NC):
                                    last = e.matmul(
                                        pa[:, buf*512:(buf+1)*512],
                                        k2sb[:, dc*2048 + kt*128:
                                             dc*2048 + (kt+1)*128],
                                        q2sb[:, dc*2048 + qc*512:
                                             dc*2048 + (qc+1)*512],
                                        start=(dc == 0), stop=(dc == NC - 1))
                                INC(last, "sA")

                            def emit_sum(kt, qc=qc, pslot=pslot):
                                e.wait_ge(sB, bB + qc * 16 + kt + 1)
                                if kt == 0 and qc > 0:
                                    e.wait_ge(sG, bG + qc)
                                last = e.matmul(
                                    pa[0:1, 1024:1536], onescol[:],
                                    p2sb[:, pslot + kt*512:
                                         pslot + (kt+1)*512],
                                    start=(kt == 0), stop=(kt == NT - 1))
                                if kt == NT - 1:
                                    INC(last, "sC")

                            for kt in range(NT + 1):
                                if kt < NT:
                                    emit_sc(kt)
                                if kt >= 1:
                                    emit_sum(kt - 1)

                            for dc in range(NC):
                                ps2 = (pa if dc % 2 == 0 else pb)
                                if qc == 0 and dc == 0:
                                    e.wait_ge(s_w, b_ld["v2"])
                                if dc >= 2:
                                    e.wait_ge(sD, bD + qc * 8 + dc - 1)
                                last = None
                                for kt in range(NT):
                                    last = e.matmul(
                                        ps2[:, 1536:2048],
                                        v2sb[:, kt*1024 + dc*128:
                                             kt*1024 + (dc+1)*128],
                                        p2sb[:, pslot + kt*512:
                                             pslot + (kt+1)*512],
                                        start=(kt == 0), stop=(kt == NT - 1))
                                INC(last, "sE")

                            for qs in range(4):
                                if qc * 4 + qs >= 1:
                                    e.wait_ge(s_p, bP + qc * 4 + qs)
                                e.wait_ge(sD, bD + qc * 8 + 8)
                                for dc in range(NC):
                                    e.transpose(
                                        pb[:, dc*128:(dc+1)*128],
                                        o2sb[:, oslot + dc*512 + qs*128:
                                             oslot + dc*512 + (qs+1)*128],
                                        ident32[:])
                                e.wait_ge(sG, bG + qc + 1)
                                last = e.matmul(
                                    pb[:, 1024:1025],
                                    r2row[0:1, qs*128:(qs+1)*128],
                                    one11[:], start=True, stop=True)
                                INC(last, "sF")

                    @blk.scalar
                    def _(e):
                        for qc in range(4):
                            pslot = (qc % 2) * 16 * 512
                            oslot = (qc % 2) * NC * 512
                            for kt in range(NT):
                                buf = kt % 2
                                e.wait_ge(sA, bA + qc * 16 + kt + 1)
                                if qc >= 2:
                                    e.wait_ge(sE, bE + 8 * (qc - 1))
                                INC(e.activation(
                                    p2sb[:, pslot + kt*512:
                                         pslot + (kt+1)*512],
                                    pa[:, buf*512:(buf+1)*512],
                                    AF.Exp, bias=bias2[:, 0:1],
                                    scale=1.0 / 32.0), "sB")
                            for dc in range(NC):
                                ps2 = (pa if dc % 2 == 0 else pb)
                                e.wait_ge(sE, bE + qc * 8 + dc + 1)
                                INC(e.copy(
                                    o2sb[:, oslot + dc*512:
                                         oslot + (dc+1)*512],
                                    ps2[:, 1536:2048]), "sD")
                            for qs in range(4):
                                if qc * 4 + qs >= 2:
                                    e.wait_ge(s_p, bP + qc * 4 + qs - 1)
                                e.wait_ge(sF, bF + qc * 4 + qs + 1)
                                INC(e.copy(r2col[:, qs % 2:qs % 2 + 1],
                                           pb[:, 1024:1025]), "sC")

                    @blk.vector
                    def _(e):
                        for qc in range(4):
                            e.wait_ge(sC, bC + qc * 5 + 1)
                            INC(e.reciprocal(r2row[:], pa[0:1, 1024:1536]),
                                "sG")
                            for qs in range(4):
                                slot = (qs % 2) * 1024
                                if qc * 4 + qs >= 2:
                                    e.wait_ge(s_out,
                                              bO + 16 * (qc * 4 + qs - 1))
                                e.wait_ge(sC, bC + qc * 5 + qs + 2)
                                e.wait_ge(s_x, bX + 32 * (qc * 4 + qs + 1))
                                INC(e.scalar_tensor_tensor(
                                    ysb[:, slot:slot+1024], pb[:, 0:1024],
                                    r2col[:, qs % 2:qs % 2 + 1],
                                    astg[:, slot:slot+1024],
                                    ALU.mult, ALU.add), "s_p")
                                INC(e.tensor_add(
                                    ysb[:, slot:slot+1024],
                                    ysb[:, slot:slot+1024],
                                    xstg[:, slot:slot+1024]), "sH")

                    @blk.sync
                    def _(e):
                        for qc in range(4):
                            for qs in range(4):
                                qg = qc * 4 + qs
                                slot = (qs % 2) * 1024
                                if qg >= 2:
                                    e.wait_ge(sH, bH + qg - 1)
                                if qg > 0:
                                    e.wait_ge(s_x, bX + 32 * qg)
                                INC(e.dma_start(
                                    out=xstg[:, slot:slot+1024],
                                    in_=x_h[qg*128:(qg+1)*128, :]), "s_x", 16)
                                INC(e.dma_start(
                                    out=astg[:, slot:slot+1024],
                                    in_=attn_hbm[qg*128:(qg+1)*128, :]),
                                    "s_x", 16)

                    @blk.gpsimd
                    def _(e):
                        for qc in range(4):
                            for qs in range(4):
                                qg = qc * 4 + qs
                                slot = (qs % 2) * 1024
                                e.wait_ge(sH, bH + qg + 1)
                                if qg > 0:
                                    e.wait_ge(s_out, bO + 16 * qg)
                                INC(e.dma_start(
                                    out=y_h[qg*128:(qg+1)*128, :],
                                    in_=ysb[:, slot:slot+1024]), "s_out", 16)

        with nc.Block(no_gpsimd_drain=True) as blk:
            @blk.gpsimd
            def _(e):
                e.wait_ge(s_out, cnt["s_out"])

    return nc


def kernel(**inputs):
    hp = host_prep(inputs)
    x = np.ascontiguousarray(np.asarray(inputs["L0_attn_hidden_states"],
                                        dtype=np.float32))
    nc = build_program(debug=False)
    in_maps = []
    for c in range(B):
        m = {"x": np.ascontiguousarray(x[c])}
        m.update(hp)
        in_maps.append(m)
    res = run_bass_kernel_spmd(nc, in_maps, core_ids=list(range(8)))
    out = np.stack([np.asarray(res.results[c]["y"]) for c in range(B)], axis=0)
    return out.astype(np.float32)


# version 22
# speedup vs baseline: 1.0156x; 1.0023x over previous
import contextlib
import numpy as np
import ml_dtypes
import concourse.bass as bass
import concourse.mybir as mybir
from concourse.bass_utils import run_bass_kernel_spmd

F32 = mybir.dt.float32
F32R = mybir.dt.float32r
BF16 = mybir.dt.bfloat16
AF = mybir.ActivationFunctionType
ALU = mybir.AluOpType
AX = mybir.AxisListType

B, S, D = 8, 2048, 1024
NH, HD, RD = 16, 64, 32
NT = S // 128
NC = D // 128
NP = 8
NLEAF, NNODE = 16, 15
BIG = 1.0e9
EPS = 1e-5


def _chunkmajor(w):
    n = w.shape[1]
    return np.ascontiguousarray(
        w.reshape(NC, 128, n).transpose(1, 0, 2).reshape(128, NC * n))


def host_prep(inp):
    h = {}
    w_qkvT = np.ascontiguousarray(np.asarray(inp["L0_attn_w_qkv"]).T)
    qcols = np.concatenate([w_qkvT[:, mp*768:mp*768+256] for mp in range(4)], 1)
    vcols = np.concatenate([w_qkvT[:, mp*768+256:mp*768+512] for mp in range(4)], 1)
    kcols = np.concatenate([w_qkvT[:, mp*768+512:mp*768+768] for mp in range(4)], 1)
    h["w_q"] = _chunkmajor(qcols)
    h["w_k"] = _chunkmajor(kcols)
    h["w_v"] = _chunkmajor(vcols)
    h["w_o"] = _chunkmajor(
        np.ascontiguousarray(np.asarray(inp["L0_attn_w_out"]).T))

    pos = np.arange(S, dtype=np.float64)
    invf = 1.0 / 10000.0 ** (np.arange(0, RD, 2, dtype=np.float64) / RD)
    ang = np.outer(invf, pos)
    cos = np.ones((64, S), np.float64)
    sin = np.zeros((64, S), np.float64)
    for d in range(RD):
        cos[d] = np.cos(ang[d // 2])
        sin[d] = np.sin(ang[d // 2])
    h["cos"] = np.tile(cos, (2, 1)).astype(np.float32)
    h["sin"] = np.tile(sin, (2, 1)).astype(np.float32)
    P = np.zeros((128, 128), np.float32)
    for base in (0, 64):
        for j in range(RD // 2):
            P[base + 2*j + 1, base + 2*j] = -1.0
            P[base + 2*j, base + 2*j + 1] = 1.0
    h["prope"] = P
    h["ident"] = np.eye(128, dtype=np.float32)

    m = np.zeros((128, 4, 2, 512), np.float32)
    for r in range(4):
        for ck in range(128):
            m[ck, r, :, :] = (np.arange(512) >= 128*r + ck)[None, :]
    h["mask01"] = m.reshape(128, 4096)

    e = np.zeros((128, NP, 16), np.float32)
    for p in range(NP):
        e[0:64, p, 2*p] = 1.0
        e[64:128, p, 2*p+1] = 1.0
    h["enorm"] = e.reshape(128, NP * 16)

    A = np.zeros((NNODE, NLEAF), np.float32)
    thr = np.zeros((NLEAF, 1), np.float32)
    for leaf in range(NLEAF):
        bits = [(leaf >> (3 - t)) & 1 for t in range(4)]
        cur, nleft = 0, 0
        for t in range(4):
            A[cur, leaf] = 1.0 if bits[t] else -1.0
            nleft += 0 if bits[t] else 1
            cur = 2*cur + 1 + bits[t]
        thr[leaf, 0] = 3.5 - nleft
    h["aroute"] = A
    h["thr"] = thr

    node_wT = np.concatenate(
        [np.asarray(inp[f"L0_ffn_{f}_node_w"]).T for f in "qkv"], axis=1)
    h["node_w"] = _chunkmajor(node_wT)
    h["node_b"] = np.ascontiguousarray(np.stack(
        [np.asarray(inp[f"L0_ffn_{f}_node_b"]).astype(np.float32)
         for f in "qkv"], axis=1))

    bige = np.zeros((16, NP, 128), np.float32)
    for p in range(NP):
        bige[2*p, p, 0:64] = BIG
        bige[2*p+1, p, 64:128] = BIG
    h["bige"] = bige.reshape(16, NP * 128)
    half01 = np.zeros((2, 128), np.float32)
    half01[0, 0:64] = 1.0
    half01[1, 64:128] = 1.0
    h["half01"] = half01
    ebc = np.zeros((16, NP * 128), np.float32)
    for p in range(NP):
        ebc[p, p*128:p*128 + 64] = 1.0
        ebc[8 + p, p*128 + 64:p*128 + 128] = 1.0
    h["ebc"] = ebc
    h["onescol"] = np.ones((128, 1), np.float32)
    h["onesrow"] = np.ones((1, 128), np.float32)
    h["one11"] = np.ones((1, 1), np.float32)

    for f in "qkv":
        w1 = np.asarray(inp[f"L0_ffn_{f}_w1"])
        w1T = np.ascontiguousarray(w1.transpose(1, 0, 2).reshape(D, 1024))
        h[f"w1_{f}"] = _chunkmajor(w1T)
        w2 = np.asarray(inp[f"L0_ffn_{f}_w2"])
        h[f"w2_{f}"] = np.ascontiguousarray(
            w2.reshape(NP, 128, 1024).transpose(1, 0, 2).reshape(128, NP * 1024))
        b1 = np.asarray(inp[f"L0_ffn_{f}_b1"])
        h[f"b1_{f}"] = np.ascontiguousarray(
            b1.reshape(NP, 128).T).astype(np.float32)
        h[f"b2_{f}"] = np.asarray(inp[f"L0_ffn_{f}_b2"]).astype(np.float32)
    return h


HOSTSHAPES = {
    "w_q": [128, NC * 1024], "w_k": [128, NC * 1024],
    "w_v": [128, NC * 1024], "w_o": [128, NC * 1024],
    "cos": [128, S], "sin": [128, S], "prope": [128, 128], "ident": [128, 128],
    "mask01": [128, 4096], "enorm": [128, NP * 16],
    "aroute": [NNODE, NLEAF], "thr": [NLEAF, 1],
    "node_w": [128, NC * 45], "node_b": [NNODE, 3], "bige": [16, NP * 128],
    "half01": [2, 128], "onescol": [128, 1], "onesrow": [1, 128], "one11": [1, 1],
    "ebc": [16, 8 * 128],
}
for _f in "qkv":
    HOSTSHAPES[f"w1_{_f}"] = [128, 8192]
    HOSTSHAPES[f"w2_{_f}"] = [128, 8192]
    HOSTSHAPES[f"b1_{_f}"] = [128, 8]
    HOSTSHAPES[f"b2_{_f}"] = [16, 1024]

STRIPS = [(qc, kt) for qc in range(4) for kt in range(4 * qc + 4)]


def build_program(debug=False):
    nc = bass.Bass("TRN2", target_bir_lowering=False,
                   detect_race_conditions=False)

    def din(name, shape, dt=F32):
        return nc.dram_tensor(name, shape, dt, kind="ExternalInput").ap()

    x_h = din("x", [S, D])
    hin = {k: din(k, v) for k, v in HOSTSHAPES.items()}
    y_h = nc.dram_tensor("y", [S, D], F32, kind="ExternalOutput").ap()

    sk = "ExternalOutput" if debug else "Internal"
    attn_hbm = nc.dram_tensor("attn_sc", [S, D], F32, kind=sk).ap()
    q2_hbm = nc.dram_tensor("q2_sc", [128, NC * 2048], BF16, kind=sk).ap()
    k2_hbm = nc.dram_tensor("k2_sc", [128, NC * 2048], BF16, kind=sk).ap()
    v2_hbm = nc.dram_tensor("v2_sc", [128, NT * 1024], BF16, kind=sk).ap()
    arawA_hbm = nc.dram_tensor("arawA_sc", [64, NP * S], BF16, kind=sk).ap()
    arawB_hbm = nc.dram_tensor("arawB_sc", [64, NP * S], BF16, kind=sk).ap()
    sums_hbm = nc.dram_tensor("sums_sc", [2, NP, S], F32, kind=sk).ap()
    rsb_hbm = nc.dram_tensor("rsb_sc", [NNODE, 3 * S], BF16, kind="Internal").ap()

    def dbg(name, shape, dt=F32):
        if not debug:
            return None
        return nc.dram_tensor("dbg_" + name, shape, dt,
                              kind="ExternalOutput").ap()

    d_xnfm = dbg("xnfm", [128, NC * 2048], BF16)
    d_xnlo = dbg("xnlo", [128, NC * 2048], BF16)
    d_rsb = dbg("rsb", [NNODE, 3 * S], BF16)
    d_q = dbg("q", [128, NP * 2048], BF16)
    d_k = dbg("k", [128, NP * 2048], BF16)
    d_v = dbg("v", [128, NT * 1040], BF16)
    d_biasg = dbg("biasg", [128, 1])
    d_anorm = dbg("anorm", [128, NP * 2048], BF16)
    d_sel = dbg("sel", [16, 3 * S])
    d_hid = dbg("hid", [128, NP * 2048], BF16)
    d_bias2 = dbg("bias2", [128, 1])
    d_stats = dbg("stats", [128, 80])
    d_hstage = dbg("hstage", [128, 4096], BF16)
    d_xstage = dbg("xstage", [128, 4096])

    cnt = {}

    with contextlib.ExitStack() as _top:
        def _sb(name, shape, dt, side="left"):
            return _top.enter_context(nc.sbuf_tensor("sb_" + name, shape, dt, side=side))
        ident32 = _sb("ident32", [128, 128], F32)
        ident16 = _sb("ident16", [128, 128], BF16)
        prope = _sb("prope", [128, 128], BF16)
        enorm = _sb("enorm", [128, NP * 16], BF16)
        aroute = _sb("aroute", [NNODE, NLEAF], BF16)
        thr = _sb("thr", [NLEAF, 1], F32)
        nodeb = _sb("nodeb", [NNODE, 3], F32)
        bige = _sb("bige", [16, NP * 128], BF16)
        half01 = _sb("half01", [2, 128], F32R)
        onescol = _sb("onescol", [128, 1], BF16)
        onesrow = _sb("onesrow", [1, 128], BF16)
        one11 = _sb("one11", [1, 1], F32)
        nodewh = _sb("nodewh", [128, NC * 45], BF16)
        nodewl = _sb("nodewl", [128, NC * 45], BF16)
        biasg = _sb("biasg", [128, 1], F32)
        bias2 = _sb("bias2", [128, 1], F32)
        tiny = _sb("tiny", [128, 16], F32)
        tinyr = _sb("tinyr", [128, 4], BF16)
        pa = _top.enter_context(nc.psum_tensor("pa", [128, 2048], F32))
        pb = _top.enter_context(nc.psum_tensor("pb", [128, 2048], F32))
        sA = _top.enter_context(nc.semaphore("sA"))
        sB = _top.enter_context(nc.semaphore("sB"))
        sC = _top.enter_context(nc.semaphore("sC"))
        sD = _top.enter_context(nc.semaphore("sD"))
        sE = _top.enter_context(nc.semaphore("sE"))
        sF = _top.enter_context(nc.semaphore("sF"))
        sG = _top.enter_context(nc.semaphore("sG"))
        sH = _top.enter_context(nc.semaphore("sH"))
        s_w = _top.enter_context(nc.semaphore("s_w"))
        s_x = _top.enter_context(nc.semaphore("s_x"))
        s_tr = _top.enter_context(nc.semaphore("s_tr"))
        s_st = _top.enter_context(nc.semaphore("s_st"))
        s_out = _top.enter_context(nc.semaphore("s_out"))
        s_p = _top.enter_context(nc.semaphore("s_p"))
        SEM = {"sA": sA, "sB": sB, "sC": sC, "sD": sD, "sE": sE, "sF": sF,
               "sG": sG, "sH": sH, "s_w": s_w, "s_x": s_x, "s_tr": s_tr,
               "s_st": s_st, "s_out": s_out, "s_p": s_p}
        for k in SEM:
            cnt[k] = 0

        def INC(instr, name, k=1):
            instr.then_inc(SEM[name], k)
            cnt[name] += k
            return instr

        def wload(e, dst, src):
            INC(e.dma_start(out=dst, in_=src), "s_w", 16)

        def ostore(e, dst, src):
            INC(e.dma_start(out=dst, in_=src), "s_out", 16)

        with nc.sbuf_tensor("nodew32", [128, NC * 45], F32,
                            side="right") as nodew32:
            with nc.Block(no_gpsimd_drain=True) as blk:
                @blk.gpsimd
                def _(e):
                    for dst, key in (
                        (prope, "prope"), (enorm, "enorm"), (aroute, "aroute"),
                        (bige, "bige"), (onescol, "onescol"),
                        (ident32, "ident"), (ident16, "ident"), (thr, "thr"),
                        (nodeb, "node_b"), (half01, "half01"),
                        (onesrow, "onesrow"), (one11, "one11"),
                        (nodew32, "node_w"),
                    ):
                        wload(e, dst[:], hin[key])
                    e.wait_ge(s_w, 16 * (cnt["s_w"] // 16))

            with nc.Block(no_gpsimd_drain=True) as blk:
                @blk.scalar
                def _(e):
                    e.copy(nodewh[:], nodew32[:])
            with nc.Block(no_gpsimd_drain=True) as blk:
                @blk.vector
                def _(e):
                    e.tensor_sub(nodewl[:], nodew32[:], nodewh[:])

        with nc.sbuf_tensor("xnfm", [128, NC * 2048], BF16, side="left") as xnfm:
            with contextlib.ExitStack() as stW:
                wqh = stW.enter_context(nc.sbuf_tensor(
                    "wqh", [128, NC * 1024], BF16, side="right"))
                wv = stW.enter_context(nc.sbuf_tensor(
                    "wv", [128, NC * 1024], BF16, side="right"))
                cossb = stW.enter_context(nc.sbuf_tensor(
                    "cossb", [128, S], BF16, side="right"))
                sinsb = stW.enter_context(nc.sbuf_tensor(
                    "sinsb", [128, S], BF16, side="right"))
                with (
                    nc.sbuf_tensor("rsb", [NNODE, 3 * S], BF16,
                                   side="right") as rsb,
                    nc.sbuf_tensor("xnlo", [128, NC * 2048], BF16,
                                   side="right") as xnlo,
                    nc.sbuf_tensor("xstage", [128, 8 * 1024], F32,
                                   side="right") as xstage,
                    nc.sbuf_tensor("xnf32", [128, 1024], F32,
                                   side="right") as xnf32,
                    nc.sbuf_tensor("hstage", [128, 8 * 1024], BF16,
                                   side="right") as hstage,
                    nc.sbuf_tensor("lstage", [128, 8 * 1024], BF16,
                                   side="right") as lstage,
                    nc.sbuf_tensor("stats", [128, 96], F32,
                                   side="right") as stats,
                ):
                    with nc.Block(no_gpsimd_drain=True) as blk:
                        @blk.gpsimd
                        def _(e):
                            wload(e, wqh[:], hin["w_q"])
                            wload(e, wv[:], hin["w_v"])
                            wload(e, cossb[:], hin["cos"])
                            wload(e, sinsb[:], hin["sin"])

                    bB0, bC0, bE0, bF0, bG0, bH0 = (
                        cnt["sB"], cnt["sC"], cnt["sE"], cnt["sF"],
                        cnt["sG"], cnt["sH"])
                    bX0 = cnt["s_x"]
                    with nc.Block(no_gpsimd_drain=True) as blk:
                        @blk.sync
                        def _(e):
                            for g in range(4):
                                for i in range(4):
                                    t = g * 4 + i
                                    gs = (g % 2) * 4096
                                    if g >= 2 and i == 0:
                                        e.wait_ge(sB, bB0 + 4 * (g - 1))
                                        e.wait_ge(sC, bC0 + 4 * (g - 1))
                                    INC(e.dma_start(
                                        out=xstage[:, gs + i*1024:
                                                   gs + (i+1)*1024],
                                        in_=x_h[t*128:(t+1)*128, :]),
                                        "s_x", 16)

                        @blk.scalar
                        def _(e):
                            for g in range(4):
                                t0 = g * 4
                                gs = (g % 2) * 4096
                                sc = 64 + (g % 2) * 16
                                e.wait_ge(s_x, bX0 + 64 * (g + 1))
                                for i in range(4):
                                    t = t0 + i
                                    ins = e.activation(
                                        xnf32[:],
                                        xstage[:, gs + i*1024:gs + (i+1)*1024],
                                        AF.Square,
                                        accum_out=stats[:, 16+t:17+t])
                                    if i == 3:
                                        INC(ins, "sH")
                                e.wait_ge(sG, bG0 + 2 * g + 1)
                                INC(e.activation(stats[:, sc+8:sc+12],
                                                 stats[:, sc+4:sc+8],
                                                 AF.Sqrt), "sH")
                                e.wait_ge(sG, bG0 + 2 * g + 2)
                                for i in range(4):
                                    t = t0 + i
                                    INC(e.activation(
                                        hstage[:, gs + i*1024:gs + (i+1)*1024],
                                        xstage[:, gs + i*1024:gs + (i+1)*1024],
                                        AF.Identity,
                                        bias=stats[:, 48+t:49+t],
                                        scale=stats[:, 32+t:33+t]), "sB")
                                for i in range(4):
                                    t = t0 + i
                                    gi4 = g * 4 + i
                                    ps = pa if gi4 % 2 == 0 else pb
                                    e.wait_ge(sE, bE0 + gi4 + 1)
                                    INC(e.copy(xnfm.ap().rearrange(
                                        "p (c n) -> p c n",
                                        c=NC)[:, :, t*128:(t+1)*128],
                                        ps.ap()[:, 0:1024].rearrange(
                                            "p (c n) -> p c n", c=NC)), "sF")
                                    INC(e.copy(xnlo.ap().rearrange(
                                        "p (c n) -> p c n",
                                        c=NC)[:, :, t*128:(t+1)*128],
                                        ps.ap()[:, 1024:2048].rearrange(
                                            "p (c n) -> p c n", c=NC)), "sF")

                        @blk.vector
                        def _(e):
                            for g in range(4):
                                t0 = g * 4
                                gs = (g % 2) * 4096
                                sc = 64 + (g % 2) * 16
                                m4 = stats[:, sc:sc+4]
                                v4 = stats[:, sc+4:sc+8]
                                w4 = stats[:, sc+8:sc+12]
                                u4 = stats[:, sc+12:sc+16]
                                r4 = stats[:, 32+t0:36+t0]
                                e.wait_ge(s_x, bX0 + 64 * (g + 1))
                                for i in range(4):
                                    t = t0 + i
                                    e.reduce_sum(
                                        stats[:, t:t+1],
                                        xstage[:, gs + i*1024:gs + (i+1)*1024],
                                        axis=AX.X)
                                e.wait_ge(sH, bH0 + 2 * g + 1)
                                e.tensor_scalar(m4, stats[:, t0:t0+4],
                                                1.0 / D, None, ALU.mult)
                                e.tensor_scalar(v4, stats[:, 16+t0:20+t0],
                                                1.0 / D, None, ALU.mult)
                                e.drain()
                                e.tensor_mul(w4, m4, m4)
                                e.drain()
                                e.tensor_sub(v4, v4, w4)
                                e.drain()
                                INC(e.tensor_scalar(v4, v4, EPS, None,
                                                    ALU.add), "sG")
                                e.drain()
                                e.tensor_scalar(m4, m4, -1.0, None, ALU.mult)
                                e.wait_ge(sH, bH0 + 2 * g + 2)
                                e.reciprocal(r4, w4)
                                e.drain()
                                INC(e.tensor_mul(stats[:, 48+t0:52+t0],
                                                 m4, r4), "sG")
                                for i in range(4):
                                    t = t0 + i
                                    e.tensor_scalar(
                                        xnf32[:],
                                        xstage[:, gs + i*1024:gs + (i+1)*1024],
                                        stats[:, 32+t:33+t],
                                        stats[:, 48+t:49+t],
                                        ALU.mult, ALU.add)
                                    e.wait_ge(sB, bB0 + 4 * g + i + 1)
                                    INC(e.tensor_sub(
                                        lstage[:, gs + i*1024:gs + (i+1)*1024],
                                        xnf32[:],
                                        hstage[:, gs + i*1024:
                                               gs + (i+1)*1024]), "sC")

                        @blk.tensor
                        def _(e):
                            for g in range(4):
                                t0 = g * 4
                                gs = (g % 2) * 4096
                                for i in range(4):
                                    gi4 = g * 4 + i
                                    ps = pa if gi4 % 2 == 0 else pb
                                    e.wait_ge(sB, bB0 + 4 * g + i + 1)
                                    e.wait_ge(sC, bC0 + 4 * g + i + 1)
                                    if gi4 >= 2:
                                        e.wait_ge(sF, bF0 + 2 * (gi4 - 1))
                                    last = None
                                    for half, stg in ((0, hstage),
                                                      (1, lstage)):
                                        for dc in range(NC):
                                            last = e.matmul(
                                                ps[:, half*1024 + dc*128:
                                                   half*1024 + (dc+1)*128],
                                                stg[:, gs + i*1024 + dc*128:
                                                    gs + i*1024 + (dc+1)*128],
                                                ident16[:],
                                                start=True, stop=True)
                                    INC(last, "sE")

                    bE, bF = cnt["sE"], cnt["sF"]
                    with nc.Block(no_gpsimd_drain=True) as blk:
                        @blk.tensor
                        def _(e, bE=bE, bF=bF):
                            for f in range(3):
                                if f > 0:
                                    e.wait_ge(sF, bF + f)
                                last = None
                                for ps, (wt, xt) in enumerate(
                                        ((nodewh, xnfm), (nodewl, xnfm),
                                         (nodewh, xnlo))):
                                    for dc in range(NC):
                                        st = (ps == 0 and dc == 0)
                                        sp = (ps == 2 and dc == NC - 1)
                                        for gq in range(4):
                                            last = e.matmul(
                                                pa[0:NNODE, gq*512:(gq+1)*512],
                                                wt[:, dc*45 + f*15:
                                                   dc*45 + (f+1)*15],
                                                xt[:, dc*2048 + gq*512:
                                                   dc*2048 + (gq+1)*512],
                                                start=st, stop=sp)
                                INC(last, "sE")

                        @blk.vector
                        def _(e, bE=bE, bF=bF):
                            for f in range(3):
                                e.wait_ge(sE, bE + f + 1)
                                INC(e.tensor_scalar(
                                    rsb[:, f*S:(f+1)*S], pa[0:NNODE, :],
                                    nodeb[:, f:f+1], 0.0,
                                    ALU.add, ALU.is_ge), "sF")

                    with nc.Block(no_gpsimd_drain=True) as blk:
                        @blk.gpsimd
                        def _(e):
                            wload(e, rsb_hbm, rsb[:])
                            if debug:
                                ostore(e, d_xnfm, xnfm[:])
                                ostore(e, d_xnlo, xnlo[:])
                                ostore(e, d_rsb, rsb[:])
                            e.wait_ge(s_w, cnt["s_w"])

                with (
                    nc.sbuf_tensor("qsb", [128, NP * 2048], BF16,
                                   side="left") as qsb,
                    nc.sbuf_tensor("ksb", [128, NP * 2048], BF16,
                                   side="left") as ksb,
                    nc.sbuf_tensor("vsb", [128, NT * 1040], BF16,
                                   side="left") as vsb,
                ):
                    bA, bB, bC, bD = cnt["sA"], cnt["sB"], cnt["sC"], cnt["sD"]
                    sw_q = cnt["s_w"]
                    NU = 32
                    SLOT = (pa.ap()[:, 0:1024], pa.ap()[:, 1024:2048],
                            pb.ap()[:, 0:1024])
                    ROT = pb.ap()[:, 1024:2048]
                    with nc.Block(no_gpsimd_drain=True) as blk:
                        @blk.gpsimd
                        def _(e, bA=bA):
                            e.wait_ge(sA, bA + 16)
                            wload(e, wqh[:], hin["w_k"])

                        @blk.tensor
                        def _(e, bA=bA, bB=bB, bC=bC, sw_q=sw_q,
                              sw_k=cnt["s_w"]):
                            e.wait_ge(s_w, sw_q)

                            def emit_mm(u):
                                if u == 16:
                                    e.wait_ge(s_w, sw_k)
                                cc, h = u // 2, u % 2
                                pcc = cc % 8
                                sp = SLOT[u % 3]
                                if u >= 3:
                                    e.wait_ge(sB, bB + u - 2)
                                last = None
                                for dc in range(NC):
                                    for g2 in range(2):
                                        last = e.matmul(
                                            sp[:, g2*512:(g2+1)*512],
                                            wqh[:, dc*1024 + pcc*128:
                                                dc*1024 + (pcc+1)*128],
                                            xnfm[:, dc*2048 + h*1024
                                                 + g2*512:
                                                 dc*2048 + h*1024
                                                 + (g2+1)*512],
                                            start=(dc == 0), stop=(dc == NC-1))
                                INC(last, "sA")

                            def emit_rot(u):
                                cc, h = u // 2, u % 2
                                dstsb = qsb if cc < 8 else ksb
                                pcc = cc % 8
                                base = pcc*2048 + h*1024
                                e.wait_ge(sB, bB + u + 1)
                                if u >= 1:
                                    e.wait_ge(sC, bC + 2 * u)
                                last = None
                                for j in range(2):
                                    last = e.matmul(
                                        ROT[:, j*512:(j+1)*512], prope[:],
                                        dstsb[:, base + j*512:
                                              base + (j+1)*512],
                                        start=True, stop=True)
                                INC(last, "sD")

                            for u in range(NU + 1):
                                if u < NU:
                                    emit_mm(u)
                                if u >= 1:
                                    emit_rot(u - 1)

                        @blk.scalar
                        def _(e, bA=bA):
                            for u in range(NU):
                                cc, h = u // 2, u % 2
                                dstsb = qsb if cc < 8 else ksb
                                pcc = cc % 8
                                base = pcc*2048 + h*1024
                                e.wait_ge(sA, bA + u + 1)
                                INC(e.copy(dstsb[:, base:base + 1024],
                                           SLOT[u % 3]), "sB")

                        @blk.vector
                        def _(e, bD=bD):
                            for u in range(NU):
                                cc, h = u // 2, u % 2
                                dstsb = qsb if cc < 8 else ksb
                                pcc = cc % 8
                                sl = dstsb[:, pcc*2048 + h*1024:
                                           pcc*2048 + (h+1)*1024]
                                e.wait_ge(sD, bD + u + 1)
                                INC(e.tensor_mul(
                                    ROT[:, :], ROT[:, :],
                                    sinsb[:, h*1024:(h+1)*1024]), "sC")
                                e.tensor_mul(sl, sl,
                                             cossb[:, h*1024:(h+1)*1024])
                                INC(e.tensor_add(sl, sl, ROT[:, :]), "sC")

                    with nc.sbuf_tensor("sqsb", [128, 2 * 2048], BF16,
                                        side="right") as sqsb:
                        bA, bB, bC, bD, bE = (cnt["sA"], cnt["sB"], cnt["sC"],
                                              cnt["sD"], cnt["sE"])
                        with nc.Block(no_gpsimd_drain=True) as blk:
                            @blk.vector
                            def _(e, bB=bB):
                                for tt in range(NT):
                                    vv = vsb.ap()[:, tt*1040:(tt+1)*1040]
                                    vv3 = vv.rearrange("p (h z) -> p h z", h=16)
                                    e.memset(vv3[:, :, 64:65], 1.0)
                                for cc in range(16):
                                    srcsb = qsb if cc < 8 else ksb
                                    pcc = cc % 8
                                    sl = srcsb[:, pcc*2048:(pcc+1)*2048]
                                    if cc >= 2:
                                        e.wait_ge(sB, bB + cc - 1)
                                    INC(e.tensor_mul(
                                        sqsb[:, (cc % 2)*2048:
                                             (cc % 2 + 1)*2048], sl, sl), "sA")
                                    if cc == 9:
                                        e.wait_ge(sB, bB + 8)
                                        INC(e.reduce_max(tiny[0:16, 0:1],
                                                         pb[0:16, :],
                                                         axis=AX.X), "sC")
                                e.wait_ge(sB, bB + 16)
                                INC(e.reduce_max(tiny[0:16, 1:2], pb[0:16, :],
                                                 axis=AX.X), "sC")

                            @blk.tensor
                            def _(e, bA=bA, bC=bC, bE=bE):
                                for tt in range(NT):
                                    half = (tt % 2) * 1024
                                    if tt >= 2:
                                        e.wait_ge(sE, bE + tt - 1)
                                    last = None
                                    for dc in range(NC):
                                        for gq in range(2):
                                            last = e.matmul(
                                                pa[:, half + gq*512:
                                                   half + (gq+1)*512],
                                                xnfm[:, dc*2048 + tt*128:
                                                     dc*2048 + (tt+1)*128],
                                                wv[:, dc*1024 + gq*512:
                                                   dc*1024 + (gq+1)*512],
                                                start=(dc == 0),
                                                stop=(dc == NC-1))
                                    INC(last, "sF")
                                    cc = tt
                                    if cc == 8:
                                        e.wait_ge(sC, bC + 1)
                                    e.wait_ge(sA, bA + cc + 1)
                                    last = None
                                    for gq in range(4):
                                        last = e.matmul(
                                            pb[0:16, gq*512:(gq+1)*512],
                                            enorm[:, (cc % 8)*16:
                                                  (cc % 8 + 1)*16],
                                            sqsb[:, (cc % 2)*2048 + gq*512:
                                                 (cc % 2)*2048 + (gq+1)*512],
                                            start=(cc % 8 == 0),
                                            stop=(cc % 8 == 7))
                                    INC(last, "sB")

                            @blk.scalar
                            def _(e):
                                bF = cnt["sF"] - NT
                                for tt in range(NT):
                                    half = (tt % 2) * 1024
                                    e.wait_ge(sF, bF + tt + 1)
                                    src = pa.ap()[:, half:half+1024].rearrange(
                                        "p (h z) -> p h z", h=16)
                                    vv = vsb.ap()[:, tt*1040:(tt+1)*1040]
                                    dst = vv.rearrange(
                                        "p (h z) -> p h z", h=16)[:, :, 0:64]
                                    INC(e.copy(dst, src), "sE")

                    g1A, g1B, g1C = cnt["sA"], cnt["sB"], cnt["sC"]
                    with nc.Block(no_gpsimd_drain=True) as blk:
                        @blk.scalar
                        def _(e, g1A=g1A, g1B=g1B, g1C=g1C):
                            e.activation(tiny[0:16, 2:3], tiny[0:16, 0:1],
                                         AF.Sqrt)
                            INC(e.activation(tiny[0:16, 3:4], tiny[0:16, 1:2],
                                             AF.Sqrt), "sA")
                            e.wait_ge(sB, g1B + 1)
                            INC(e.copy(tinyr[0:16, 0:1], tiny[0:16, 4:5]),
                                "sA")
                            e.wait_ge(sB, g1B + 2)
                            INC(e.copy(tinyr[0:1, 1:2], tiny[0:1, 5:6]), "sA")
                            e.wait_ge(sC, g1C + 2)
                            e.activation(biasg[:], pa[:, 16:17], AF.Copy,
                                         scale=-1.0 / 8.0)

                        @blk.vector
                        def _(e, g1A=g1A, g1B=g1B, g1C=g1C):
                            e.wait_ge(sA, g1A + 1)
                            INC(e.tensor_mul(tiny[0:16, 4:5], tiny[0:16, 2:3],
                                             tiny[0:16, 3:4]), "sB")
                            e.wait_ge(sC, g1C + 1)
                            INC(e.reduce_max(tiny[0:1, 5:6], pa[0:1, 0:16],
                                             axis=AX.X), "sB")

                        @blk.tensor
                        def _(e, g1A=g1A, g1C=g1C):
                            e.wait_ge(sA, g1A + 2)
                            INC(e.matmul(pa[0:1, 0:16], tinyr[0:16, 0:1],
                                         ident16[0:16, 0:16], start=True,
                                         stop=True), "sC")
                            e.wait_ge(sA, g1A + 3)
                            INC(e.matmul(pa[:, 16:17], onesrow[:],
                                         tinyr[0:1, 1:2],
                                         start=True, stop=True), "sC")

                    if debug:
                        with nc.Block(no_gpsimd_drain=True) as blk:
                            @blk.gpsimd
                            def _(e):
                                ostore(e, d_q, qsb[:])
                                ostore(e, d_k, ksb[:])
                                ostore(e, d_v, vsb[:])
                                ostore(e, d_biasg, biasg[:])

                    stW.close()
                    with (
                        nc.sbuf_tensor("psb", [128, 6 * 1024], BF16,
                                       side="right") as psb,
                        nc.sbuf_tensor("mask01sb", [128, 4096], BF16,
                                       side="right") as mask01,
                        nc.sbuf_tensor("stgA", [64, 2 * S], BF16,
                                       side="right") as stgA,
                        nc.sbuf_tensor("stgB", [64, 2 * S], BF16,
                                       side="right") as stgB,
                        nc.sbuf_tensor("stgS", [128, 2048], F32,
                                       side="right") as stgS,
                    ):
                        with nc.Block(no_gpsimd_drain=True) as blk:
                            @blk.gpsimd
                            def _(e):
                                wload(e, mask01[:], hin["mask01"])
                                e.wait_ge(s_w, cnt["s_w"])

                        ndiag = []
                        nd = 0
                        for (qc, kt) in STRIPS:
                            ndiag.append(nd)
                            if kt >= 4 * qc:
                                nd += 1
                        nsp = len(STRIPS)

                        bA, bB, bC, bD, bE = (cnt["sA"], cnt["sB"], cnt["sC"],
                                              cnt["sD"], cnt["sE"])
                        with nc.Block(no_gpsimd_drain=True) as blk:
                            @blk.tensor
                            def _(e, bA=bA, bB=bB, bC=bC, bD=bD, bE=bE):
                                allst = [(p, qc, kt) for p in range(NP)
                                         for (qc, kt) in STRIPS]

                                def emit_st(gi):
                                    p, qc, kt = allst[gi]
                                    buf = gi % 2
                                    sp = pa[:, buf*1024:(buf+1)*1024]
                                    if gi >= 2:
                                        e.wait_ge(sB, bB + gi - 1)
                                    mA = e.matmul(
                                        sp[:, 0:512],
                                        ksb[0:64, p*2048 + kt*128:
                                            p*2048 + (kt+1)*128],
                                        qsb[0:64, p*2048 + qc*512:
                                            p*2048 + (qc+1)*512],
                                        start=True, stop=True)
                                    mB = e.matmul(
                                        sp[:, 512:1024],
                                        ksb[64:128, p*2048 + kt*128:
                                            p*2048 + (kt+1)*128],
                                        qsb[64:128, p*2048 + qc*512:
                                            p*2048 + (qc+1)*512],
                                        start=True, stop=True)
                                    INC(mB, "sA")

                                def emit_pv(gi):
                                    p, qc, kt = allst[gi]
                                    si = gi - p * nsp
                                    nkt = 4 * qc + 4
                                    buf = gi % 6
                                    m = p * 4 + qc
                                    acc = (m % 2) * 1024
                                    if kt == 0 and m >= 2:
                                        e.wait_ge(sD, bD + m - 1)
                                    diag = kt >= 4 * qc
                                    if diag:
                                        e.wait_ge(sC, bC + p * 16
                                                  + ndiag[si] + 1)
                                    else:
                                        e.wait_ge(sB, bB + gi + 1)
                                    pvA = e.matmul(
                                        pb[0:65, acc:acc + 512],
                                        vsb[:, kt*1040 + (2*p)*65:
                                            kt*1040 + (2*p)*65 + 65],
                                        psb[:, buf*1024:buf*1024 + 512],
                                        start=(kt == 0), stop=(kt == nkt - 1))
                                    pvB = e.matmul(
                                        pb[0:65, acc + 512:acc + 1024],
                                        vsb[:, kt*1040 + (2*p+1)*65:
                                            kt*1040 + (2*p+1)*65 + 65],
                                        psb[:, buf*1024 + 512:
                                            buf*1024 + 1024],
                                        start=(kt == 0), stop=(kt == nkt - 1))
                                    INC(pvB, "sE")

                                n = len(allst)
                                for gi in range(n + 2):
                                    if gi < n:
                                        emit_st(gi)
                                    if gi >= 2:
                                        emit_pv(gi - 2)

                            @blk.scalar
                            def _(e, bA=bA, bE=bE):
                                for p in range(NP):
                                    for si, (qc, kt) in enumerate(STRIPS):
                                        gi = p * nsp + si
                                        pbuf = gi % 2
                                        buf = gi % 6
                                        sp = pa[:, pbuf*1024:(pbuf+1)*1024]
                                        e.wait_ge(sA, bA + gi + 1)
                                        if gi >= 6:
                                            e.wait_ge(sE, bE + gi - 5)
                                        INC(e.activation(
                                            psb[:, buf*1024:(buf+1)*1024],
                                            sp,
                                            AF.Exp, bias=biasg[:, 0:1],
                                            scale=1.0 / 8.0), "sB")

                            @blk.vector
                            def _(e, bE=bE, bH=cnt["s_st"]):
                                bB = cnt["sB"] - NP * nsp
                                last_gi = {}
                                for si, (qc, kt) in enumerate(STRIPS):
                                    if kt == 4 * qc + 3:
                                        last_gi[qc] = si
                                for p in range(NP):
                                    if p >= 2:
                                        e.wait_ge(s_st, bH + 160 * (p - 1))
                                    for si, (qc, kt) in enumerate(STRIPS):
                                        gi = p * nsp + si
                                        buf = gi % 6
                                        if kt >= 4 * qc:
                                            r = kt - 4 * qc
                                            e.wait_ge(sB, bB + gi + 1)
                                            INC(e.tensor_mul(
                                                psb[:, buf*1024:(buf+1)*1024],
                                                psb[:, buf*1024:(buf+1)*1024],
                                                mask01[:, r*1024:(r+1)*1024]),
                                                "sC")
                                        if kt == 4 * qc + 3:
                                            m = p * 4 + qc
                                            acc = (m % 2) * 1024
                                            if m >= 2:
                                                p2, qc2 = ((m - 2) // 4,
                                                           (m - 2) % 4)
                                                e.wait_ge(s_st, bH + 160 * p2
                                                          + 32 * (qc2 + 1))
                                            e.wait_ge(sE, bE + p * nsp
                                                      + last_gi[qc] + 1)
                                            cA = (qc % 2) * 1024
                                            cB = cA + 512
                                            sa = (p % 2) * 2048
                                            e.tensor_copy(
                                                stgA[:, sa + qc*512:
                                                     sa + (qc+1)*512],
                                                pb[0:64, acc:acc+512])
                                            e.tensor_copy(
                                                stgS[64:65, cA:cA+512],
                                                pb[64:65, acc:acc+512])
                                            e.tensor_copy(
                                                stgB[:, sa + qc*512:
                                                     sa + (qc+1)*512],
                                                pb[0:64, acc+512:acc+1024])
                                            INC(e.tensor_copy(
                                                stgS[64:65, cB:cB+512],
                                                pb[64:65, acc+512:acc+1024]),
                                                "sD")

                            @blk.sync
                            def _(e):
                                bD = cnt["sD"] - NP * 4
                                for p in range(NP):
                                    for qc in range(4):
                                        e.wait_ge(sD, bD + p * 4 + qc + 1)
                                        cA = (qc % 2) * 1024
                                        cB = cA + 512
                                        INC(e.dma_start(
                                            out=sums_hbm[0, p,
                                                         qc*512:(qc+1)*512],
                                            in_=stgS[64:65, cA:cA+512]),
                                            "s_st", 16)
                                        INC(e.dma_start(
                                            out=sums_hbm[1, p,
                                                         qc*512:(qc+1)*512],
                                            in_=stgS[64:65, cB:cB+512]),
                                            "s_st", 16)
                                    sa = (p % 2) * 2048
                                    INC(e.dma_start(
                                        out=arawA_hbm[:, p*S:(p+1)*S],
                                        in_=stgA[:, sa:sa+2048]), "s_st", 16)
                                    INC(e.dma_start(
                                        out=arawB_hbm[:, p*S:(p+1)*S],
                                        in_=stgB[:, sa:sa+2048]), "s_st", 16)
                                e.wait_ge(s_st, cnt["s_st"])

            q2sb = _top.enter_context(nc.sbuf_tensor(
                "q2sb", [128, NC * 2048], BF16, side="right"))
            with (
                nc.sbuf_tensor("w1a", [128, 8192], BF16, side="right") as w1a,
                nc.sbuf_tensor("w2a", [128, 8192], BF16, side="right") as w2a,
                nc.sbuf_tensor("b2r", [16, 2 * 1024], F32R, side="right") as b2r,
                nc.sbuf_tensor("b1sb", [128, 24], F32, side="right") as b1sb,
                nc.sbuf_tensor("rsb2", [NNODE, 3 * S], BF16,
                               side="right") as rsb2,
            ):
                sw_after_rsb = [0]
                with nc.Block(no_gpsimd_drain=True) as blk:
                    @blk.gpsimd
                    def _(e):
                        wload(e, w1a[:], hin["w1_q"])
                        wload(e, w2a[:], hin["w2_q"])
                        wload(e, b2r[:, 0:1024], hin["b2_q"])
                        wload(e, b2r[:, 1024:2048], hin["b2_k"])
                        for i, f in enumerate("qkv"):
                            wload(e, b1sb[:, i*8:(i+1)*8], hin[f"b1_{f}"])
                        wload(e, rsb2[:], rsb_hbm)
                        sw_after_rsb[0] = cnt["s_w"]

                with (
                    nc.sbuf_tensor("asb", [128, NP * 2048], BF16,
                                   side="right") as asb,
                    nc.sbuf_tensor("wo", [128, NC * 1024], BF16,
                                   side="right") as wo,
                    nc.sbuf_tensor("rsrc", [16, S], F32, side="right") as rsrc,
                    nc.sbuf_tensor("rdst", [16, S], F32R, side="right") as rdst,
                    nc.sbuf_tensor("ostage", [128, 2 * 1024], F32,
                                   side="right") as ostage,
                    nc.sbuf_tensor("ebcsb", [16, 8 * 128], F32R,
                                   side="right") as ebc,
                ):
                    bXs = cnt["s_x"]
                    swB = [0]
                    swE = [0]
                    with nc.Block(no_gpsimd_drain=True) as blk:
                        @blk.gpsimd
                        def _(e, bSt=cnt["s_st"]):
                            wload(e, wo[:], hin["w_o"])
                            wload(e, ebc[:], hin["ebc"])
                            swE[0] = cnt["s_w"]
                            e.wait_ge(s_st, bSt)
                            INC(e.dma_start(
                                out=asb.ap()[64:128, :],
                                in_=arawB_hbm), "s_w", 16)
                            swB[0] = cnt["s_w"]

                        @blk.sync
                        def _(e):
                            INC(e.dma_start(
                                out=rsrc[:],
                                in_=sums_hbm.rearrange("h p c -> (h p) c")),
                                "s_x", 16)
                            INC(e.dma_start(
                                out=asb.ap()[0:64, :],
                                in_=arawA_hbm), "s_x", 16)

                        @blk.vector
                        def _(e, bXs=bXs):
                            e.wait_ge(s_x, bXs + 16)
                            with nc.allow_low_precision(
                                    reason="float32r out is 4-byte fp32"):
                                e.reciprocal(rdst[:], rsrc[:])

                    bB, bC, bD = cnt["sB"], cnt["sC"], cnt["sD"]
                    with nc.Block(no_gpsimd_drain=True) as blk:
                        @blk.tensor
                        def _(e, bD=bD, swE=swE[0]):
                            for p in range(NP):
                                ps = pa if p % 2 == 0 else pb
                                if p == 0:
                                    e.wait_ge(s_w, swE)
                                if p >= 2:
                                    e.wait_ge(sD, bD + p - 1)
                                last = None
                                for gq in range(4):
                                    last = e.matmul(
                                        ps[:, gq*512:(gq+1)*512],
                                        ebc[:, p*128:(p+1)*128],
                                        rdst[0:16, gq*512:(gq+1)*512],
                                        start=True, stop=True)
                                INC(last, "sC")

                        @blk.vector
                        def _(e, bC=bC, bXs=bXs, swB=swB[0]):
                            for p in range(NP):
                                ps = pa if p % 2 == 0 else pb
                                if p == 0:
                                    e.wait_ge(s_x, bXs + 32)
                                    e.wait_ge(s_w, swB)
                                e.wait_ge(sC, bC + p + 1)
                                INC(e.tensor_mul(
                                    asb[:, p*2048:(p+1)*2048],
                                    asb[:, p*2048:(p+1)*2048], ps[:, :]), "sD")

                    if debug:
                        with nc.Block(no_gpsimd_drain=True) as blk:
                            @blk.gpsimd
                            def _(e):
                                ostore(e, d_anorm, asb[:])

                    bE, bF, bG = cnt["sE"], cnt["sF"], cnt["sG"]
                    sw_wo = cnt["s_w"]
                    with nc.Block(no_gpsimd_drain=True) as blk:
                        @blk.tensor
                        def _(e, bE=bE, bG=bG, sw_wo=sw_wo):
                            e.wait_ge(s_w, sw_wo)
                            for qt in range(NT):
                                half = (qt % 2) * 1024
                                if qt >= 2:
                                    e.wait_ge(sE, bE + qt - 1)
                                last = None
                                for p in range(NP):
                                    for gq in range(2):
                                        last = e.matmul(
                                            pb[:, half + gq*512:
                                               half + (gq+1)*512],
                                            asb[:, p*2048 + qt*128:
                                                p*2048 + (qt+1)*128],
                                            wo[:, p*1024 + gq*512:
                                               p*1024 + (gq+1)*512],
                                            start=(p == 0), stop=(p == NP-1))
                                INC(last, "sF")

                        @blk.scalar
                        def _(e, bF=bF, bG=cnt["s_st"]):
                            for qt in range(NT):
                                half = (qt % 2) * 1024
                                e.wait_ge(sF, bF + qt + 1)
                                if qt >= 2:
                                    e.wait_ge(s_st, bG + 16 * (qt - 1))
                                INC(e.copy(
                                    ostage[:, half:half + 1024],
                                    pb[:, half:half + 1024]), "sE")

                        @blk.sync
                        def _(e, bG=cnt["s_st"]):
                            bE2 = cnt["sE"] - NT
                            for qt in range(NT):
                                half = (qt % 2) * 1024
                                e.wait_ge(sE, bE2 + qt + 1)
                                if qt > 0:
                                    e.wait_ge(s_st, bG + 16 * qt)
                                INC(e.dma_start(
                                    out=attn_hbm[qt*128:(qt+1)*128, :],
                                    in_=ostage[:, half:half + 1024]), "s_st",
                                    16)

                with (
                    nc.sbuf_tensor("w1b", [128, 8192], BF16,
                                   side="right") as w1b,
                    nc.sbuf_tensor("w2b", [128, 8192], BF16,
                                   side="right") as w2b,
                    nc.sbuf_tensor("hid", [128, NP * 2048], BF16,
                                   side="right") as hid,
                    nc.sbuf_tensor("self32r", [16, S], F32R,
                                   side="right") as self32r,
                    nc.sbuf_tensor("maskm1", [16, S], BF16,
                                   side="right") as maskm1,
                    nc.sbuf_tensor("fstage", [128, 2 * 2048], BF16,
                                   side="right") as fstage,
                ):
                    with nc.Block(no_gpsimd_drain=True) as blk:
                        @blk.gpsimd
                        def _(e):
                            wload(e, w1b[:], hin["w1_k"])
                            wload(e, w2b[:], hin["w2_k"])

                    for fi, f in enumerate("qkv"):
                        w1x = (w1a, w1b, w1a)[fi]
                        w2x = (w2a, w2b, w2a)[fi]
                        dsth = (q2_hbm, k2_hbm, v2_hbm)[fi]

                        bA, bB, bC, bD, bE, bG = (
                            cnt["sA"], cnt["sB"], cnt["sC"], cnt["sD"],
                            cnt["sE"], cnt["s_st"])
                        sw_need = sw_after_rsb[0] if fi == 0 else cnt["s_w"]
                        with nc.Block(no_gpsimd_drain=True) as blk:
                            @blk.tensor
                            def _(e, fi=fi, w1x=w1x, w2x=w2x, bA=bA, bB=bB,
                                  bC=bC, bD=bD, bE=bE, bG=bG,
                                  sw_need=sw_need):
                                e.wait_ge(s_w, sw_need)
                                last = None
                                for gq in range(4):
                                    last = e.matmul(
                                        pb[0:16, gq*512:(gq+1)*512], aroute[:],
                                        rsb2[:, fi*S + gq*512:
                                             fi*S + (gq+1)*512],
                                        start=True, stop=True)
                                INC(last, "sA")
                                for p in range(NP):
                                    ps = pa if p % 2 == 0 else pb
                                    if p == 1:
                                        e.wait_ge(sB, bB + 1)
                                    if p >= 2:
                                        e.wait_ge(sC, bC + p - 1)
                                    for dc in range(NC):
                                        for gq in range(4):
                                            e.matmul(
                                                ps[:, gq*512:(gq+1)*512],
                                                w1x[:, dc*1024 + p*128:
                                                    dc*1024 + (p+1)*128],
                                                xnfm[:, dc*2048 + gq*512:
                                                     dc*2048 + (gq+1)*512],
                                                start=(dc == 0), stop=False)
                                    e.wait_ge(sB, bB + 1)
                                    last = None
                                    for gq in range(4):
                                        last = e.matmul(
                                            ps[:, gq*512:(gq+1)*512],
                                            bige[:, p*128:(p+1)*128],
                                            maskm1[0:16, gq*512:(gq+1)*512],
                                            start=False, stop=True)
                                    INC(last, "sD")
                                if fi < 2:
                                    for oc in range(NC):
                                        ps = pa if oc % 2 == 0 else pb
                                        e.wait_ge(sC, bC + NP)
                                        if oc >= 2:
                                            e.wait_ge(sE, bE + oc - 1)
                                        for p in range(NP):
                                            for gq in range(4):
                                                e.matmul(
                                                    ps[:, gq*512:(gq+1)*512],
                                                    w2x[:, p*1024 + oc*128:
                                                        p*1024 + (oc+1)*128],
                                                    hid[:, p*2048 + gq*512:
                                                        p*2048 + (gq+1)*512],
                                                    start=(p == 0), stop=False)
                                        last = None
                                        for gq in range(4):
                                            last = e.matmul(
                                                ps[:, gq*512:(gq+1)*512],
                                                b2r[0:16,
                                                    (fi % 2)*1024 + oc*128:
                                                    (fi % 2)*1024
                                                    + (oc+1)*128],
                                                self32r[0:16,
                                                        gq*512:(gq+1)*512],
                                                start=False, stop=True)
                                        INC(last, "sF")
                                else:
                                    for tt in range(NT):
                                        ps = pa if tt % 2 == 0 else pb
                                        half = 0
                                        e.wait_ge(sC, bC + NP)
                                        if tt >= 2:
                                            e.wait_ge(sE, bE + tt - 1)
                                        for p in range(NP):
                                            for gq in range(2):
                                                e.matmul(
                                                    ps[:, gq*512:(gq+1)*512],
                                                    hid[:, p*2048 + tt*128:
                                                        p*2048 + (tt+1)*128],
                                                    w2x[:, p*1024 + gq*512:
                                                        p*1024 + (gq+1)*512],
                                                    start=(p == 0), stop=False)
                                        last = None
                                        for gq in range(2):
                                            last = e.matmul(
                                                ps[:, gq*512:(gq+1)*512],
                                                self32r[0:16,
                                                        tt*128:(tt+1)*128],
                                                b2r[0:16,
                                                    (fi % 2)*1024 + gq*512:
                                                    (fi % 2)*1024
                                                    + (gq+1)*512],
                                                start=False, stop=True)
                                        INC(last, "sF")

                            @blk.vector
                            def _(e, fi=fi, bA=bA):
                                e.wait_ge(sA, bA + 1)
                                e.tensor_scalar(
                                    self32r[0:16, :], pb[0:16, :],
                                    thr[:, 0:1], None, ALU.is_ge)
                                INC(e.tensor_scalar(
                                    maskm1[0:16, :], pb[0:16, :],
                                    thr[:, 0:1], -1.0, ALU.is_ge, ALU.add),
                                    "sB")

                            @blk.scalar
                            def _(e, fi=fi, bD=bD):
                                for p in range(NP):
                                    ps = pa if p % 2 == 0 else pb
                                    e.wait_ge(sD, bD + p + 1)
                                    INC(e.activation(
                                        hid[:, p*2048:(p+1)*2048], ps[:, :],
                                        AF.Relu,
                                        bias=b1sb[:, fi*8+p:fi*8+p+1]), "sC")
                                nsub = NC if fi < 2 else NT
                                w = 2048 if fi < 2 else 1024
                                bF = cnt["sF"] - nsub
                                for u in range(nsub):
                                    ps = pa if u % 2 == 0 else pb
                                    e.wait_ge(sF, bF + u + 1)
                                    if fi == 0:
                                        INC(e.copy(
                                            q2sb[:, u*2048:(u+1)*2048],
                                            ps[:, 0:w]), "sE")
                                        continue
                                    if u >= 2:
                                        e.wait_ge(s_st, bG + 16 * (u - 1))
                                    INC(e.copy(
                                        fstage[:, (u % 2)*2048:
                                               (u % 2)*2048 + w],
                                        ps[:, 0:w]), "sE")

                            @blk.sync
                            def _(e, fi=fi, dsth=dsth, bG=bG):
                                if fi == 0:
                                    return
                                nsub = NC if fi < 2 else NT
                                w = 2048 if fi < 2 else 1024
                                bE2 = cnt["sE"] - nsub
                                for u in range(nsub):
                                    e.wait_ge(sE, bE2 + u + 1)
                                    if u > 0:
                                        e.wait_ge(s_st, bG + 16 * u)
                                    INC(e.dma_start(
                                        out=dsth[:, u*w:(u+1)*w],
                                        in_=fstage[:, (u % 2)*2048:
                                                   (u % 2)*2048 + w]),
                                        "s_st", 16)

                            if fi == 1:
                                @blk.gpsimd
                                def _(e):
                                    wload(e, w1a[:], hin["w1_v"])
                                    wload(e, w2a[:], hin["w2_v"])
                                    wload(e, b2r[:, 0:1024], hin["b2_v"])

                        if debug and f == "q":
                            with nc.Block(no_gpsimd_drain=True) as blk:
                                @blk.gpsimd
                                def _(e, fi=fi):
                                    ostore(e, d_hid, hid[:])
                                    ostore(e, d_sel.rearrange(
                                        "p (f s) -> p f s", f=3)[:, fi, :],
                                        self32r[0:16, :])
                                    ostore(e, q2_hbm, q2sb[:])

        with (
            nc.sbuf_tensor("k2sb", [128, NC * 2048], BF16, side="right") as k2sb,
            nc.sbuf_tensor("v2sb", [128, NT * 1024], BF16, side="right") as v2sb,
        ):
            b_ld = {}
            with nc.Block(no_gpsimd_drain=True) as blk:
                @blk.scalar
                def _(e):
                    e.wait_ge(s_st, cnt["s_st"])
                    INC(e.dma_start(out=k2sb[:, 0:8192],
                                    in_=k2_hbm[:, 0:8192]), "s_tr", 16)
                    b_ld["k2"] = cnt["s_tr"]

                @blk.sync
                def _(e):
                    e.wait_ge(s_st, cnt["s_st"])
                    INC(e.dma_start(out=k2sb[:, 8192:16384],
                                    in_=k2_hbm[:, 8192:16384]), "s_x", 16)
                    b_ld["k2x"] = cnt["s_x"]

                @blk.gpsimd
                def _(e):
                    e.wait_ge(s_st, cnt["s_st"])
                    INC(e.dma_start(out=v2sb[:], in_=v2_hbm), "s_w", 16)
                    b_ld["v2"] = cnt["s_w"]

            with nc.sbuf_tensor("sq2", [128, 2 * 2048], BF16,
                                side="right") as sq2:
                bA, bB = cnt["sA"], cnt["sB"]
                with nc.Block(no_gpsimd_drain=True) as blk:
                    @blk.vector
                    def _(e, bA=bA, bB=bB):
                        for i, src in enumerate((q2sb, k2sb)):
                            for dc in range(NC):
                                j = i * NC + dc
                                if i == 1 and dc == 0:
                                    e.wait_ge(s_tr, b_ld["k2"])
                                if i == 1 and dc == 4:
                                    e.wait_ge(s_x, b_ld["k2x"])
                                if j >= 2:
                                    e.wait_ge(sB, bB + j - 1)
                                INC(e.tensor_mul(
                                    sq2[:, (j % 2)*2048:(j % 2 + 1)*2048],
                                    src[:, dc*2048:(dc+1)*2048],
                                    src[:, dc*2048:(dc+1)*2048]), "sA")

                    @blk.tensor
                    def _(e, bA=bA):
                        for i in range(2):
                            ps = pa if i == 0 else pb
                            for dc in range(NC):
                                j = i * NC + dc
                                e.wait_ge(sA, bA + j + 1)
                                last = None
                                for gq in range(4):
                                    last = e.matmul(
                                        ps[0:1, gq*512:(gq+1)*512], onescol[:],
                                        sq2[:, (j % 2)*2048 + gq*512:
                                            (j % 2)*2048 + (gq+1)*512],
                                        start=(dc == 0), stop=(dc == NC - 1))
                                INC(last, "sB")

                b2A, b2B, b2C = cnt["sA"], cnt["sB"], cnt["sC"]
                with nc.Block(no_gpsimd_drain=True) as blk:
                    @blk.vector
                    def _(e, b2B=b2B):
                        e.reduce_max(tiny[0:1, 6:7], pa[0:1, :], axis=AX.X)
                        INC(e.reduce_max(tiny[0:1, 7:8], pb[0:1, :],
                                         axis=AX.X), "sA")
                        e.wait_ge(sB, b2B + 1)
                        INC(e.tensor_mul(tiny[0:1, 10:11], tiny[0:1, 8:9],
                                         tiny[0:1, 9:10]), "sB")

                    @blk.scalar
                    def _(e, b2A=b2A, b2B=b2B, b2C=b2C):
                        e.wait_ge(sA, b2A + 1)
                        e.activation(tiny[0:1, 8:9], tiny[0:1, 6:7], AF.Sqrt)
                        INC(e.activation(tiny[0:1, 9:10], tiny[0:1, 7:8],
                                         AF.Sqrt), "sB")
                        e.wait_ge(sB, b2B + 2)
                        INC(e.copy(tinyr[0:1, 2:3], tiny[0:1, 10:11]), "sB")
                        e.wait_ge(sC, b2C + 1)
                        e.activation(bias2[:], pa[:, 16:17], AF.Copy,
                                     scale=-1.0 / 32.0)

                    @blk.tensor
                    def _(e, b2B=b2B):
                        e.wait_ge(sB, b2B + 3)
                        INC(e.matmul(pa[:, 16:17], onesrow[:],
                                     tinyr[0:1, 2:3],
                                     start=True, stop=True), "sC")
                if debug:
                    with nc.Block(no_gpsimd_drain=True) as blk:
                        @blk.gpsimd
                        def _(e):
                            ostore(e, d_bias2, bias2[:])

            with (
                nc.sbuf_tensor("p2sb", [128, 2 * 16 * 512], BF16,
                               side="right") as p2sb,
                nc.sbuf_tensor("o2sb", [128, 2 * NC * 512], F32,
                               side="right") as o2sb,
                nc.sbuf_tensor("r2row", [1, 512], F32, side="right") as r2row,
                nc.sbuf_tensor("r2col", [128, 2], F32, side="right") as r2col,
                nc.sbuf_tensor("xstg", [128, 2 * 1024], F32,
                               side="right") as xstg,
                nc.sbuf_tensor("astg", [128, 2 * 1024], F32,
                               side="right") as astg,
                nc.sbuf_tensor("ysb", [128, 2 * 1024], F32,
                               side="right") as ysb,
            ):
                bA, bB, bC, bD, bE, bF, bG, bH = (
                    cnt["sA"], cnt["sB"], cnt["sC"], cnt["sD"], cnt["sE"],
                    cnt["sF"], cnt["sG"], cnt["sH"])
                bX, bO, bP = cnt["s_x"], cnt["s_out"], cnt.get("s_p", 0)
                with nc.Block(no_gpsimd_drain=True) as blk:
                    @blk.tensor
                    def _(e):
                        for qc in range(4):
                            pslot = (qc % 2) * 16 * 512
                            oslot = (qc % 2) * NC * 512

                            def emit_sc(kt, qc=qc):
                                buf = kt % 2
                                if qc * 16 + kt >= 2:
                                    e.wait_ge(sB, bB + qc * 16 + kt - 1)
                                last = None
                                for dc in range(NC):
                                    last = e.matmul(
                                        pa[:, buf*512:(buf+1)*512],
                                        k2sb[:, dc*2048 + kt*128:
                                             dc*2048 + (kt+1)*128],
                                        q2sb[:, dc*2048 + qc*512:
                                             dc*2048 + (qc+1)*512],
                                        start=(dc == 0), stop=(dc == NC - 1))
                                INC(last, "sA")

                            def emit_sum(kt, qc=qc, pslot=pslot):
                                e.wait_ge(sB, bB + qc * 16 + kt + 1)
                                if kt == 0 and qc > 0:
                                    e.wait_ge(sG, bG + qc)
                                last = e.matmul(
                                    pa[0:1, 1024:1536], onescol[:],
                                    p2sb[:, pslot + kt*512:
                                         pslot + (kt+1)*512],
                                    start=(kt == 0), stop=(kt == NT - 1))
                                if kt == NT - 1:
                                    INC(last, "sC")

                            for kt in range(NT + 1):
                                if kt < NT:
                                    emit_sc(kt)
                                if kt >= 1:
                                    emit_sum(kt - 1)

                            for dc in range(NC):
                                ps2 = (pa if dc % 2 == 0 else pb)
                                if qc == 0 and dc == 0:
                                    e.wait_ge(s_w, b_ld["v2"])
                                if dc >= 2:
                                    e.wait_ge(sD, bD + qc * 8 + dc - 1)
                                last = None
                                for kt in range(NT):
                                    last = e.matmul(
                                        ps2[:, 1536:2048],
                                        v2sb[:, kt*1024 + dc*128:
                                             kt*1024 + (dc+1)*128],
                                        p2sb[:, pslot + kt*512:
                                             pslot + (kt+1)*512],
                                        start=(kt == 0), stop=(kt == NT - 1))
                                INC(last, "sE")

                            for qs in range(4):
                                if qc * 4 + qs >= 1:
                                    e.wait_ge(s_p, bP + qc * 4 + qs)
                                e.wait_ge(sD, bD + qc * 8 + 8)
                                for dc in range(NC):
                                    e.transpose(
                                        pb[:, dc*128:(dc+1)*128],
                                        o2sb[:, oslot + dc*512 + qs*128:
                                             oslot + dc*512 + (qs+1)*128],
                                        ident32[:])
                                e.wait_ge(sG, bG + qc + 1)
                                last = e.matmul(
                                    pb[:, 1024:1025],
                                    r2row[0:1, qs*128:(qs+1)*128],
                                    one11[:], start=True, stop=True)
                                INC(last, "sF")

                    @blk.scalar
                    def _(e):
                        for qc in range(4):
                            pslot = (qc % 2) * 16 * 512
                            oslot = (qc % 2) * NC * 512
                            for kt in range(NT):
                                buf = kt % 2
                                e.wait_ge(sA, bA + qc * 16 + kt + 1)
                                if qc >= 2:
                                    e.wait_ge(sE, bE + 8 * (qc - 1))
                                INC(e.activation(
                                    p2sb[:, pslot + kt*512:
                                         pslot + (kt+1)*512],
                                    pa[:, buf*512:(buf+1)*512],
                                    AF.Exp, bias=bias2[:, 0:1],
                                    scale=1.0 / 32.0), "sB")
                            for dc in range(NC):
                                ps2 = (pa if dc % 2 == 0 else pb)
                                e.wait_ge(sE, bE + qc * 8 + dc + 1)
                                INC(e.copy(
                                    o2sb[:, oslot + dc*512:
                                         oslot + (dc+1)*512],
                                    ps2[:, 1536:2048]), "sD")
                            for qs in range(4):
                                if qc * 4 + qs >= 2:
                                    e.wait_ge(s_p, bP + qc * 4 + qs - 1)
                                e.wait_ge(sF, bF + qc * 4 + qs + 1)
                                INC(e.copy(r2col[:, qs % 2:qs % 2 + 1],
                                           pb[:, 1024:1025]), "sC")

                    @blk.vector
                    def _(e):
                        for qc in range(4):
                            e.wait_ge(sC, bC + qc * 5 + 1)
                            INC(e.reciprocal(r2row[:], pa[0:1, 1024:1536]),
                                "sG")
                            for qs in range(4):
                                slot = (qs % 2) * 1024
                                if qc * 4 + qs >= 2:
                                    e.wait_ge(s_out,
                                              bO + 16 * (qc * 4 + qs - 1))
                                e.wait_ge(sC, bC + qc * 5 + qs + 2)
                                e.wait_ge(s_x, bX + 32 * (qc * 4 + qs + 1))
                                INC(e.scalar_tensor_tensor(
                                    ysb[:, slot:slot+1024], pb[:, 0:1024],
                                    r2col[:, qs % 2:qs % 2 + 1],
                                    astg[:, slot:slot+1024],
                                    ALU.mult, ALU.add), "s_p")
                                INC(e.tensor_add(
                                    ysb[:, slot:slot+1024],
                                    ysb[:, slot:slot+1024],
                                    xstg[:, slot:slot+1024]), "sH")

                    @blk.sync
                    def _(e):
                        for qc in range(4):
                            for qs in range(4):
                                qg = qc * 4 + qs
                                slot = (qs % 2) * 1024
                                if qg >= 2:
                                    e.wait_ge(sH, bH + qg - 1)
                                if qg > 0:
                                    e.wait_ge(s_x, bX + 32 * qg)
                                INC(e.dma_start(
                                    out=xstg[:, slot:slot+1024],
                                    in_=x_h[qg*128:(qg+1)*128, :]), "s_x", 16)
                                INC(e.dma_start(
                                    out=astg[:, slot:slot+1024],
                                    in_=attn_hbm[qg*128:(qg+1)*128, :]),
                                    "s_x", 16)

                    @blk.gpsimd
                    def _(e):
                        for qc in range(4):
                            for qs in range(4):
                                qg = qc * 4 + qs
                                slot = (qs % 2) * 1024
                                e.wait_ge(sH, bH + qg + 1)
                                if qg > 0:
                                    e.wait_ge(s_out, bO + 16 * qg)
                                INC(e.dma_start(
                                    out=y_h[qg*128:(qg+1)*128, :],
                                    in_=ysb[:, slot:slot+1024]), "s_out", 16)

        with nc.Block(no_gpsimd_drain=True) as blk:
            @blk.gpsimd
            def _(e):
                e.wait_ge(s_out, cnt["s_out"])

    return nc


def kernel(**inputs):
    hp = host_prep(inputs)
    x = np.ascontiguousarray(np.asarray(inputs["L0_attn_hidden_states"],
                                        dtype=np.float32))
    nc = build_program(debug=False)
    in_maps = []
    for c in range(B):
        m = {"x": np.ascontiguousarray(x[c])}
        m.update(hp)
        in_maps.append(m)
    res = run_bass_kernel_spmd(nc, in_maps, core_ids=list(range(8)))
    out = np.stack([np.asarray(res.results[c]["y"]) for c in range(B)], axis=0)
    return out.astype(np.float32)


# version 24
# speedup vs baseline: 1.0188x; 1.0031x over previous
import contextlib
import numpy as np
import ml_dtypes
import concourse.bass as bass
import concourse.mybir as mybir
from concourse.bass_utils import run_bass_kernel_spmd

F32 = mybir.dt.float32
F32R = mybir.dt.float32r
BF16 = mybir.dt.bfloat16
AF = mybir.ActivationFunctionType
ALU = mybir.AluOpType
AX = mybir.AxisListType

B, S, D = 8, 2048, 1024
NH, HD, RD = 16, 64, 32
NT = S // 128
NC = D // 128
NP = 8
NLEAF, NNODE = 16, 15
BIG = 1.0e9
EPS = 1e-5


def _chunkmajor(w):
    n = w.shape[1]
    return np.ascontiguousarray(
        w.reshape(NC, 128, n).transpose(1, 0, 2).reshape(128, NC * n))


def host_prep(inp):
    h = {}
    w_qkvT = np.ascontiguousarray(np.asarray(inp["L0_attn_w_qkv"]).T)
    qcols = np.concatenate([w_qkvT[:, mp*768:mp*768+256] for mp in range(4)], 1)
    vcols = np.concatenate([w_qkvT[:, mp*768+256:mp*768+512] for mp in range(4)], 1)
    kcols = np.concatenate([w_qkvT[:, mp*768+512:mp*768+768] for mp in range(4)], 1)
    h["w_q"] = _chunkmajor(qcols)
    h["w_k"] = _chunkmajor(kcols)
    h["w_v"] = _chunkmajor(vcols)
    h["w_o"] = _chunkmajor(
        np.ascontiguousarray(np.asarray(inp["L0_attn_w_out"]).T))

    pos = np.arange(S, dtype=np.float64)
    invf = 1.0 / 10000.0 ** (np.arange(0, RD, 2, dtype=np.float64) / RD)
    ang = np.outer(invf, pos)
    cos = np.ones((64, S), np.float64)
    sin = np.zeros((64, S), np.float64)
    for d in range(RD):
        cos[d] = np.cos(ang[d // 2])
        sin[d] = np.sin(ang[d // 2])
    h["cos"] = np.tile(cos, (2, 1)).astype(np.float32)
    h["sin"] = np.tile(sin, (2, 1)).astype(np.float32)
    P = np.zeros((128, 128), np.float32)
    for base in (0, 64):
        for j in range(RD // 2):
            P[base + 2*j + 1, base + 2*j] = -1.0
            P[base + 2*j, base + 2*j + 1] = 1.0
    h["prope"] = P
    h["ident"] = np.eye(128, dtype=np.float32)

    m = np.zeros((128, 4, 2, 512), np.float32)
    for r in range(4):
        for ck in range(128):
            m[ck, r, :, :] = (np.arange(512) >= 128*r + ck)[None, :]
    h["mask01"] = m.reshape(128, 4096)

    e = np.zeros((128, NP, 16), np.float32)
    for p in range(NP):
        e[0:64, p, 2*p] = 1.0
        e[64:128, p, 2*p+1] = 1.0
    h["enorm"] = e.reshape(128, NP * 16)

    A = np.zeros((NNODE, NLEAF), np.float32)
    thr = np.zeros((NLEAF, 1), np.float32)
    for leaf in range(NLEAF):
        bits = [(leaf >> (3 - t)) & 1 for t in range(4)]
        cur, nleft = 0, 0
        for t in range(4):
            A[cur, leaf] = 1.0 if bits[t] else -1.0
            nleft += 0 if bits[t] else 1
            cur = 2*cur + 1 + bits[t]
        thr[leaf, 0] = 3.5 - nleft
    h["aroute"] = A
    h["thr"] = thr

    node_wT = np.concatenate(
        [np.asarray(inp[f"L0_ffn_{f}_node_w"]).T for f in "qkv"], axis=1)
    h["node_w"] = _chunkmajor(node_wT)
    h["node_b"] = np.ascontiguousarray(np.stack(
        [np.asarray(inp[f"L0_ffn_{f}_node_b"]).astype(np.float32)
         for f in "qkv"], axis=1))

    bige = np.zeros((16, NP, 128), np.float32)
    for p in range(NP):
        bige[2*p, p, 0:64] = BIG
        bige[2*p+1, p, 64:128] = BIG
    h["bige"] = bige.reshape(16, NP * 128)
    half01 = np.zeros((2, 128), np.float32)
    half01[0, 0:64] = 1.0
    half01[1, 64:128] = 1.0
    h["half01"] = half01
    ebc = np.zeros((16, NP * 128), np.float32)
    for p in range(NP):
        ebc[p, p*128:p*128 + 64] = 1.0
        ebc[8 + p, p*128 + 64:p*128 + 128] = 1.0
    h["ebc"] = ebc
    h["onescol"] = np.ones((128, 1), np.float32)
    h["onesrow"] = np.ones((1, 128), np.float32)
    h["one11"] = np.ones((1, 1), np.float32)

    for f in "qkv":
        w1 = np.asarray(inp[f"L0_ffn_{f}_w1"])
        w1T = np.ascontiguousarray(w1.transpose(1, 0, 2).reshape(D, 1024))
        h[f"w1_{f}"] = _chunkmajor(w1T)
        w2 = np.asarray(inp[f"L0_ffn_{f}_w2"])
        h[f"w2_{f}"] = np.ascontiguousarray(
            w2.reshape(NP, 128, 1024).transpose(1, 0, 2).reshape(128, NP * 1024))
        b1 = np.asarray(inp[f"L0_ffn_{f}_b1"])
        h[f"b1_{f}"] = np.ascontiguousarray(
            b1.reshape(NP, 128).T).astype(np.float32)
        h[f"b2_{f}"] = np.asarray(inp[f"L0_ffn_{f}_b2"]).astype(np.float32)
    return h


HOSTSHAPES = {
    "w_q": [128, NC * 1024], "w_k": [128, NC * 1024],
    "w_v": [128, NC * 1024], "w_o": [128, NC * 1024],
    "cos": [128, S], "sin": [128, S], "prope": [128, 128], "ident": [128, 128],
    "mask01": [128, 4096], "enorm": [128, NP * 16],
    "aroute": [NNODE, NLEAF], "thr": [NLEAF, 1],
    "node_w": [128, NC * 45], "node_b": [NNODE, 3], "bige": [16, NP * 128],
    "half01": [2, 128], "onescol": [128, 1], "onesrow": [1, 128], "one11": [1, 1],
    "ebc": [16, 8 * 128],
}
for _f in "qkv":
    HOSTSHAPES[f"w1_{_f}"] = [128, 8192]
    HOSTSHAPES[f"w2_{_f}"] = [128, 8192]
    HOSTSHAPES[f"b1_{_f}"] = [128, 8]
    HOSTSHAPES[f"b2_{_f}"] = [16, 1024]

STRIPS = [(qc, kt) for qc in range(4) for kt in range(4 * qc + 4)]


def build_program(debug=False):
    nc = bass.Bass("TRN2", target_bir_lowering=False,
                   detect_race_conditions=False)

    def din(name, shape, dt=F32):
        return nc.dram_tensor(name, shape, dt, kind="ExternalInput").ap()

    x_h = din("x", [S, D])
    hin = {k: din(k, v) for k, v in HOSTSHAPES.items()}
    y_h = nc.dram_tensor("y", [S, D], F32, kind="ExternalOutput").ap()

    sk = "ExternalOutput" if debug else "Internal"
    attn_hbm = nc.dram_tensor("attn_sc", [S, D], F32, kind=sk).ap()
    q2_hbm = nc.dram_tensor("q2_sc", [128, NC * 2048], BF16, kind=sk).ap()
    k2_hbm = nc.dram_tensor("k2_sc", [128, NC * 2048], BF16, kind=sk).ap()
    v2_hbm = nc.dram_tensor("v2_sc", [128, NT * 1024], BF16, kind=sk).ap()
    arawA_hbm = nc.dram_tensor("arawA_sc", [64, NP * S], BF16, kind=sk).ap()
    arawB_hbm = nc.dram_tensor("arawB_sc", [64, NP * S], BF16, kind=sk).ap()
    sums_hbm = nc.dram_tensor("sums_sc", [2, NP, S], F32, kind=sk).ap()
    rsb_hbm = nc.dram_tensor("rsb_sc", [NNODE, 3 * S], BF16, kind="Internal").ap()

    def dbg(name, shape, dt=F32):
        if not debug:
            return None
        return nc.dram_tensor("dbg_" + name, shape, dt,
                              kind="ExternalOutput").ap()

    d_xnfm = dbg("xnfm", [128, NC * 2048], BF16)
    d_xnlo = dbg("xnlo", [128, NC * 2048], BF16)
    d_rsb = dbg("rsb", [NNODE, 3 * S], BF16)
    d_q = dbg("q", [128, NP * 2048], BF16)
    d_k = dbg("k", [128, NP * 2048], BF16)
    d_v = dbg("v", [128, NT * 1040], BF16)
    d_biasg = dbg("biasg", [128, 1])
    d_anorm = dbg("anorm", [128, NP * 2048], BF16)
    d_sel = dbg("sel", [16, 3 * S])
    d_hid = dbg("hid", [128, NP * 2048], BF16)
    d_bias2 = dbg("bias2", [128, 1])
    d_stats = dbg("stats", [128, 80])
    d_hstage = dbg("hstage", [128, 4096], BF16)
    d_xstage = dbg("xstage", [128, 4096])

    cnt = {}

    with contextlib.ExitStack() as _top:
        def _sb(name, shape, dt, side="left"):
            return _top.enter_context(nc.sbuf_tensor("sb_" + name, shape, dt, side=side))
        ident32 = _sb("ident32", [128, 128], F32)
        ident16 = _sb("ident16", [128, 128], BF16)
        prope = _sb("prope", [128, 128], BF16)
        enorm = _sb("enorm", [128, NP * 16], BF16)
        aroute = _sb("aroute", [NNODE, NLEAF], BF16)
        thr = _sb("thr", [NLEAF, 1], F32)
        nodeb = _sb("nodeb", [NNODE, 3], F32)
        bige = _sb("bige", [16, NP * 128], BF16)
        half01 = _sb("half01", [2, 128], F32R)
        onescol = _sb("onescol", [128, 1], BF16)
        onesrow = _sb("onesrow", [1, 128], BF16)
        one11 = _sb("one11", [1, 1], F32)
        nodewh = _sb("nodewh", [128, NC * 45], BF16)
        nodewl = _sb("nodewl", [128, NC * 45], BF16)
        biasg = _sb("biasg", [128, 1], F32)
        bias2 = _sb("bias2", [128, 1], F32)
        tiny = _sb("tiny", [128, 16], F32)
        tinyr = _sb("tinyr", [128, 4], BF16)
        pa = _top.enter_context(nc.psum_tensor("pa", [128, 2048], F32))
        pb = _top.enter_context(nc.psum_tensor("pb", [128, 2048], F32))
        sA = _top.enter_context(nc.semaphore("sA"))
        sB = _top.enter_context(nc.semaphore("sB"))
        sC = _top.enter_context(nc.semaphore("sC"))
        sD = _top.enter_context(nc.semaphore("sD"))
        sE = _top.enter_context(nc.semaphore("sE"))
        sF = _top.enter_context(nc.semaphore("sF"))
        sG = _top.enter_context(nc.semaphore("sG"))
        sH = _top.enter_context(nc.semaphore("sH"))
        s_w = _top.enter_context(nc.semaphore("s_w"))
        s_x = _top.enter_context(nc.semaphore("s_x"))
        s_tr = _top.enter_context(nc.semaphore("s_tr"))
        s_st = _top.enter_context(nc.semaphore("s_st"))
        s_out = _top.enter_context(nc.semaphore("s_out"))
        s_p = _top.enter_context(nc.semaphore("s_p"))
        SEM = {"sA": sA, "sB": sB, "sC": sC, "sD": sD, "sE": sE, "sF": sF,
               "sG": sG, "sH": sH, "s_w": s_w, "s_x": s_x, "s_tr": s_tr,
               "s_st": s_st, "s_out": s_out, "s_p": s_p}
        for k in SEM:
            cnt[k] = 0

        def INC(instr, name, k=1):
            instr.then_inc(SEM[name], k)
            cnt[name] += k
            return instr

        def wload(e, dst, src):
            INC(e.dma_start(out=dst, in_=src), "s_w", 16)

        def ostore(e, dst, src):
            INC(e.dma_start(out=dst, in_=src), "s_out", 16)

        with nc.sbuf_tensor("nodew32", [128, NC * 45], F32,
                            side="right") as nodew32:
            with nc.Block(no_gpsimd_drain=True) as blk:
                @blk.gpsimd
                def _(e):
                    for dst, key in (
                        (prope, "prope"), (enorm, "enorm"), (aroute, "aroute"),
                        (bige, "bige"), (onescol, "onescol"),
                        (ident32, "ident"), (ident16, "ident"), (thr, "thr"),
                        (nodeb, "node_b"), (half01, "half01"),
                        (onesrow, "onesrow"), (one11, "one11"),
                        (nodew32, "node_w"),
                    ):
                        wload(e, dst[:], hin[key])
                    e.wait_ge(s_w, 16 * (cnt["s_w"] // 16))

            with nc.Block(no_gpsimd_drain=True) as blk:
                @blk.scalar
                def _(e):
                    e.copy(nodewh[:], nodew32[:])
            with nc.Block(no_gpsimd_drain=True) as blk:
                @blk.vector
                def _(e):
                    e.tensor_sub(nodewl[:], nodew32[:], nodewh[:])
                    e.memset(tiny[:, 15:16], EPS)

        with nc.sbuf_tensor("xnfm", [128, NC * 2048], BF16, side="left") as xnfm:
            with contextlib.ExitStack() as stW:
                wqh = stW.enter_context(nc.sbuf_tensor(
                    "wqh", [128, NC * 1024], BF16, side="right"))
                wv = stW.enter_context(nc.sbuf_tensor(
                    "wv", [128, NC * 1024], BF16, side="right"))
                cossb = stW.enter_context(nc.sbuf_tensor(
                    "cossb", [128, S], BF16, side="right"))
                sinsb = stW.enter_context(nc.sbuf_tensor(
                    "sinsb", [128, S], BF16, side="right"))
                with (
                    nc.sbuf_tensor("rsb", [NNODE, 3 * S], BF16,
                                   side="right") as rsb,
                    nc.sbuf_tensor("xnlo", [128, NC * 2048], BF16,
                                   side="right") as xnlo,
                    nc.sbuf_tensor("xstage", [128, 8 * 1024], F32,
                                   side="right") as xstage,
                    nc.sbuf_tensor("xnf32", [128, 1024], F32,
                                   side="right") as xnf32,
                    nc.sbuf_tensor("hstage", [128, 8 * 1024], BF16,
                                   side="right") as hstage,
                    nc.sbuf_tensor("lstage", [128, 8 * 1024], BF16,
                                   side="right") as lstage,
                    nc.sbuf_tensor("stats", [128, 96], F32,
                                   side="right") as stats,
                ):
                    with nc.Block(no_gpsimd_drain=True) as blk:
                        @blk.gpsimd
                        def _(e):
                            wload(e, wqh[:], hin["w_q"])
                            wload(e, wv[:], hin["w_v"])
                            wload(e, cossb[:], hin["cos"])
                            wload(e, sinsb[:], hin["sin"])

                    bB0, bC0, bE0, bF0, bG0, bH0 = (
                        cnt["sB"], cnt["sC"], cnt["sE"], cnt["sF"],
                        cnt["sG"], cnt["sH"])
                    bX0 = cnt["s_x"]
                    with nc.Block(no_gpsimd_drain=True) as blk:
                        @blk.sync
                        def _(e):
                            for g in range(4):
                                for i in range(4):
                                    t = g * 4 + i
                                    gs = (g % 2) * 4096
                                    if g >= 2 and i == 0:
                                        e.wait_ge(sB, bB0 + 4 * (g - 1))
                                        e.wait_ge(sC, bC0 + 4 * (g - 1))
                                    INC(e.dma_start(
                                        out=xstage[:, gs + i*1024:
                                                   gs + (i+1)*1024],
                                        in_=x_h[t*128:(t+1)*128, :]),
                                        "s_x", 16)

                        @blk.scalar
                        def _(e):
                            for g in range(4):
                                t0 = g * 4
                                gs = (g % 2) * 4096
                                sc = 64 + (g % 2) * 16
                                e.wait_ge(s_x, bX0 + 64 * (g + 1))
                                for i in range(4):
                                    t = t0 + i
                                    ins = e.activation(
                                        xnf32[:],
                                        xstage[:, gs + i*1024:gs + (i+1)*1024],
                                        AF.Square,
                                        accum_out=stats[:, 16+t:17+t])
                                    if i == 3:
                                        INC(ins, "sH")
                                e.wait_ge(sG, bG0 + 2 * g + 1)
                                INC(e.activation(stats[:, sc+8:sc+12],
                                                 stats[:, sc+4:sc+8],
                                                 AF.Sqrt,
                                                 bias=tiny[:, 15:16]), "sH")
                                e.wait_ge(sG, bG0 + 2 * g + 2)
                                for i in range(4):
                                    t = t0 + i
                                    INC(e.activation(
                                        hstage[:, gs + i*1024:gs + (i+1)*1024],
                                        xstage[:, gs + i*1024:gs + (i+1)*1024],
                                        AF.Identity,
                                        bias=stats[:, 48+t:49+t],
                                        scale=stats[:, 32+t:33+t]), "sB")
                                for i in range(4):
                                    t = t0 + i
                                    gi4 = g * 4 + i
                                    ps = pa if gi4 % 2 == 0 else pb
                                    e.wait_ge(sE, bE0 + gi4 + 1)
                                    INC(e.copy(xnfm.ap().rearrange(
                                        "p (c n) -> p c n",
                                        c=NC)[:, :, t*128:(t+1)*128],
                                        ps.ap()[:, 0:1024].rearrange(
                                            "p (c n) -> p c n", c=NC)), "sF")
                                    INC(e.copy(xnlo.ap().rearrange(
                                        "p (c n) -> p c n",
                                        c=NC)[:, :, t*128:(t+1)*128],
                                        ps.ap()[:, 1024:2048].rearrange(
                                            "p (c n) -> p c n", c=NC)), "sF")

                        @blk.vector
                        def _(e):
                            for g in range(4):
                                t0 = g * 4
                                gs = (g % 2) * 4096
                                sc = 64 + (g % 2) * 16
                                m4 = stats[:, sc:sc+4]
                                v4 = stats[:, sc+4:sc+8]
                                w4 = stats[:, sc+8:sc+12]
                                u4 = stats[:, sc+12:sc+16]
                                r4 = stats[:, 32+t0:36+t0]
                                e.wait_ge(s_x, bX0 + 64 * (g + 1))
                                for i in range(4):
                                    t = t0 + i
                                    e.reduce_sum(
                                        stats[:, t:t+1],
                                        xstage[:, gs + i*1024:gs + (i+1)*1024],
                                        axis=AX.X)
                                e.wait_ge(sH, bH0 + 2 * g + 1)
                                e.tensor_scalar(m4, stats[:, t0:t0+4],
                                                1.0 / D, None, ALU.mult)
                                e.tensor_scalar(v4, stats[:, 16+t0:20+t0],
                                                1.0 / D, None, ALU.mult)
                                e.drain()
                                e.tensor_mul(w4, m4, m4)
                                e.drain()
                                INC(e.tensor_sub(v4, v4, w4), "sG")
                                e.drain()
                                e.tensor_scalar(m4, m4, -1.0, None, ALU.mult)
                                e.wait_ge(sH, bH0 + 2 * g + 2)
                                e.reciprocal(r4, w4)
                                e.drain()
                                INC(e.tensor_mul(stats[:, 48+t0:52+t0],
                                                 m4, r4), "sG")
                                for i in range(4):
                                    t = t0 + i
                                    e.tensor_scalar(
                                        xnf32[:],
                                        xstage[:, gs + i*1024:gs + (i+1)*1024],
                                        stats[:, 32+t:33+t],
                                        stats[:, 48+t:49+t],
                                        ALU.mult, ALU.add)
                                    e.wait_ge(sB, bB0 + 4 * g + i + 1)
                                    INC(e.tensor_sub(
                                        lstage[:, gs + i*1024:gs + (i+1)*1024],
                                        xnf32[:],
                                        hstage[:, gs + i*1024:
                                               gs + (i+1)*1024]), "sC")

                        @blk.tensor
                        def _(e):
                            for g in range(4):
                                t0 = g * 4
                                gs = (g % 2) * 4096
                                for i in range(4):
                                    gi4 = g * 4 + i
                                    ps = pa if gi4 % 2 == 0 else pb
                                    e.wait_ge(sB, bB0 + 4 * g + i + 1)
                                    e.wait_ge(sC, bC0 + 4 * g + i + 1)
                                    if gi4 >= 2:
                                        e.wait_ge(sF, bF0 + 2 * (gi4 - 1))
                                    last = None
                                    for half, stg in ((0, hstage),
                                                      (1, lstage)):
                                        for dc in range(NC):
                                            last = e.matmul(
                                                ps[:, half*1024 + dc*128:
                                                   half*1024 + (dc+1)*128],
                                                stg[:, gs + i*1024 + dc*128:
                                                    gs + i*1024 + (dc+1)*128],
                                                ident16[:],
                                                start=True, stop=True)
                                    INC(last, "sE")

                    bE, bF = cnt["sE"], cnt["sF"]
                    with nc.Block(no_gpsimd_drain=True) as blk:
                        @blk.tensor
                        def _(e, bE=bE, bF=bF):
                            for f in range(3):
                                if f > 0:
                                    e.wait_ge(sF, bF + f)
                                last = None
                                for ps, (wt, xt) in enumerate(
                                        ((nodewh, xnfm), (nodewl, xnfm),
                                         (nodewh, xnlo))):
                                    for dc in range(NC):
                                        st = (ps == 0 and dc == 0)
                                        sp = (ps == 2 and dc == NC - 1)
                                        for gq in range(4):
                                            last = e.matmul(
                                                pa[0:NNODE, gq*512:(gq+1)*512],
                                                wt[:, dc*45 + f*15:
                                                   dc*45 + (f+1)*15],
                                                xt[:, dc*2048 + gq*512:
                                                   dc*2048 + (gq+1)*512],
                                                start=st, stop=sp)
                                INC(last, "sE")

                        @blk.vector
                        def _(e, bE=bE, bF=bF):
                            for f in range(3):
                                e.wait_ge(sE, bE + f + 1)
                                INC(e.tensor_scalar(
                                    rsb[:, f*S:(f+1)*S], pa[0:NNODE, :],
                                    nodeb[:, f:f+1], 0.0,
                                    ALU.add, ALU.is_ge), "sF")

                    with nc.Block(no_gpsimd_drain=True) as blk:
                        @blk.gpsimd
                        def _(e):
                            wload(e, rsb_hbm, rsb[:])
                            if debug:
                                ostore(e, d_xnfm, xnfm[:])
                                ostore(e, d_xnlo, xnlo[:])
                                ostore(e, d_rsb, rsb[:])
                            e.wait_ge(s_w, cnt["s_w"])

                with (
                    nc.sbuf_tensor("qsb", [128, NP * 2048], BF16,
                                   side="left") as qsb,
                    nc.sbuf_tensor("ksb", [128, NP * 2048], BF16,
                                   side="left") as ksb,
                    nc.sbuf_tensor("vsb", [128, NT * 1040], BF16,
                                   side="left") as vsb,
                ):
                    bA, bB, bC, bD = cnt["sA"], cnt["sB"], cnt["sC"], cnt["sD"]
                    sw_q = cnt["s_w"]
                    NU = 32
                    SLOT = (pa.ap()[:, 0:1024], pa.ap()[:, 1024:2048],
                            pb.ap()[:, 0:1024])
                    ROT = pb.ap()[:, 1024:2048]
                    with nc.Block(no_gpsimd_drain=True) as blk:
                        @blk.gpsimd
                        def _(e, bA=bA):
                            e.wait_ge(sA, bA + 16)
                            wload(e, wqh[:], hin["w_k"])

                        @blk.tensor
                        def _(e, bA=bA, bB=bB, bC=bC, sw_q=sw_q,
                              sw_k=cnt["s_w"]):
                            e.wait_ge(s_w, sw_q)

                            def emit_mm(u):
                                if u == 16:
                                    e.wait_ge(s_w, sw_k)
                                cc, h = u // 2, u % 2
                                pcc = cc % 8
                                sp = SLOT[u % 3]
                                if u >= 3:
                                    e.wait_ge(sB, bB + u - 2)
                                last = None
                                for dc in range(NC):
                                    for g2 in range(2):
                                        last = e.matmul(
                                            sp[:, g2*512:(g2+1)*512],
                                            wqh[:, dc*1024 + pcc*128:
                                                dc*1024 + (pcc+1)*128],
                                            xnfm[:, dc*2048 + h*1024
                                                 + g2*512:
                                                 dc*2048 + h*1024
                                                 + (g2+1)*512],
                                            start=(dc == 0), stop=(dc == NC-1))
                                INC(last, "sA")

                            def emit_rot(u):
                                cc, h = u // 2, u % 2
                                dstsb = qsb if cc < 8 else ksb
                                pcc = cc % 8
                                base = pcc*2048 + h*1024
                                e.wait_ge(sB, bB + u + 1)
                                if u >= 1:
                                    e.wait_ge(sC, bC + 2 * u)
                                last = None
                                for j in range(2):
                                    last = e.matmul(
                                        ROT[:, j*512:(j+1)*512], prope[:],
                                        dstsb[:, base + j*512:
                                              base + (j+1)*512],
                                        start=True, stop=True)
                                INC(last, "sD")

                            for u in range(NU + 1):
                                if u < NU:
                                    emit_mm(u)
                                if u >= 1:
                                    emit_rot(u - 1)

                        @blk.scalar
                        def _(e, bA=bA):
                            for u in range(NU):
                                cc, h = u // 2, u % 2
                                dstsb = qsb if cc < 8 else ksb
                                pcc = cc % 8
                                base = pcc*2048 + h*1024
                                e.wait_ge(sA, bA + u + 1)
                                INC(e.copy(dstsb[:, base:base + 1024],
                                           SLOT[u % 3]), "sB")

                        @blk.vector
                        def _(e, bD=bD):
                            for u in range(NU):
                                cc, h = u // 2, u % 2
                                dstsb = qsb if cc < 8 else ksb
                                pcc = cc % 8
                                sl = dstsb[:, pcc*2048 + h*1024:
                                           pcc*2048 + (h+1)*1024]
                                e.wait_ge(sD, bD + u + 1)
                                INC(e.tensor_mul(
                                    ROT[:, :], ROT[:, :],
                                    sinsb[:, h*1024:(h+1)*1024]), "sC")
                                e.tensor_mul(sl, sl,
                                             cossb[:, h*1024:(h+1)*1024])
                                INC(e.tensor_add(sl, sl, ROT[:, :]), "sC")

                    with nc.sbuf_tensor("sqsb", [128, 2 * 2048], BF16,
                                        side="right") as sqsb:
                        bA, bB, bC, bD, bE = (cnt["sA"], cnt["sB"], cnt["sC"],
                                              cnt["sD"], cnt["sE"])
                        with nc.Block(no_gpsimd_drain=True) as blk:
                            @blk.vector
                            def _(e, bB=bB):
                                for tt in range(NT):
                                    vv = vsb.ap()[:, tt*1040:(tt+1)*1040]
                                    vv3 = vv.rearrange("p (h z) -> p h z", h=16)
                                    e.memset(vv3[:, :, 64:65], 1.0)
                                for cc in range(16):
                                    srcsb = qsb if cc < 8 else ksb
                                    pcc = cc % 8
                                    sl = srcsb[:, pcc*2048:(pcc+1)*2048]
                                    if cc >= 2:
                                        e.wait_ge(sB, bB + cc - 1)
                                    INC(e.tensor_mul(
                                        sqsb[:, (cc % 2)*2048:
                                             (cc % 2 + 1)*2048], sl, sl), "sA")
                                    if cc == 9:
                                        e.wait_ge(sB, bB + 8)
                                        INC(e.reduce_max(tiny[0:16, 0:1],
                                                         pb[0:16, :],
                                                         axis=AX.X), "sC")
                                e.wait_ge(sB, bB + 16)
                                INC(e.reduce_max(tiny[0:16, 1:2], pb[0:16, :],
                                                 axis=AX.X), "sC")

                            @blk.tensor
                            def _(e, bA=bA, bC=bC, bE=bE):
                                for tt in range(NT):
                                    half = (tt % 2) * 1024
                                    if tt >= 2:
                                        e.wait_ge(sE, bE + tt - 1)
                                    last = None
                                    for dc in range(NC):
                                        for gq in range(2):
                                            last = e.matmul(
                                                pa[:, half + gq*512:
                                                   half + (gq+1)*512],
                                                xnfm[:, dc*2048 + tt*128:
                                                     dc*2048 + (tt+1)*128],
                                                wv[:, dc*1024 + gq*512:
                                                   dc*1024 + (gq+1)*512],
                                                start=(dc == 0),
                                                stop=(dc == NC-1))
                                    INC(last, "sF")
                                    cc = tt
                                    if cc == 8:
                                        e.wait_ge(sC, bC + 1)
                                    e.wait_ge(sA, bA + cc + 1)
                                    last = None
                                    for gq in range(4):
                                        last = e.matmul(
                                            pb[0:16, gq*512:(gq+1)*512],
                                            enorm[:, (cc % 8)*16:
                                                  (cc % 8 + 1)*16],
                                            sqsb[:, (cc % 2)*2048 + gq*512:
                                                 (cc % 2)*2048 + (gq+1)*512],
                                            start=(cc % 8 == 0),
                                            stop=(cc % 8 == 7))
                                    INC(last, "sB")

                            @blk.scalar
                            def _(e):
                                bF = cnt["sF"] - NT
                                for tt in range(NT):
                                    half = (tt % 2) * 1024
                                    e.wait_ge(sF, bF + tt + 1)
                                    src = pa.ap()[:, half:half+1024].rearrange(
                                        "p (h z) -> p h z", h=16)
                                    vv = vsb.ap()[:, tt*1040:(tt+1)*1040]
                                    dst = vv.rearrange(
                                        "p (h z) -> p h z", h=16)[:, :, 0:64]
                                    INC(e.copy(dst, src), "sE")

                    g1A, g1B, g1C = cnt["sA"], cnt["sB"], cnt["sC"]
                    with nc.Block(no_gpsimd_drain=True) as blk:
                        @blk.scalar
                        def _(e, g1A=g1A, g1B=g1B, g1C=g1C):
                            e.activation(tiny[0:16, 2:3], tiny[0:16, 0:1],
                                         AF.Sqrt)
                            INC(e.activation(tiny[0:16, 3:4], tiny[0:16, 1:2],
                                             AF.Sqrt), "sA")
                            e.wait_ge(sB, g1B + 1)
                            INC(e.copy(tinyr[0:16, 0:1], tiny[0:16, 4:5]),
                                "sA")
                            e.wait_ge(sB, g1B + 2)
                            INC(e.copy(tinyr[0:1, 1:2], tiny[0:1, 5:6]), "sA")
                            e.wait_ge(sC, g1C + 2)
                            e.activation(biasg[:], pa[:, 16:17], AF.Copy,
                                         scale=-1.0 / 8.0)

                        @blk.vector
                        def _(e, g1A=g1A, g1B=g1B, g1C=g1C):
                            e.wait_ge(sA, g1A + 1)
                            INC(e.tensor_mul(tiny[0:16, 4:5], tiny[0:16, 2:3],
                                             tiny[0:16, 3:4]), "sB")
                            e.wait_ge(sC, g1C + 1)
                            INC(e.reduce_max(tiny[0:1, 5:6], pa[0:1, 0:16],
                                             axis=AX.X), "sB")

                        @blk.tensor
                        def _(e, g1A=g1A, g1C=g1C):
                            e.wait_ge(sA, g1A + 2)
                            INC(e.matmul(pa[0:1, 0:16], tinyr[0:16, 0:1],
                                         ident16[0:16, 0:16], start=True,
                                         stop=True), "sC")
                            e.wait_ge(sA, g1A + 3)
                            INC(e.matmul(pa[:, 16:17], onesrow[:],
                                         tinyr[0:1, 1:2],
                                         start=True, stop=True), "sC")

                    if debug:
                        with nc.Block(no_gpsimd_drain=True) as blk:
                            @blk.gpsimd
                            def _(e):
                                ostore(e, d_q, qsb[:])
                                ostore(e, d_k, ksb[:])
                                ostore(e, d_v, vsb[:])
                                ostore(e, d_biasg, biasg[:])

                    stW.close()
                    with (
                        nc.sbuf_tensor("psb", [128, 8 * 1024], BF16,
                                       side="right") as psb,
                        nc.sbuf_tensor("mask01sb", [128, 4096], BF16,
                                       side="right") as mask01,
                        nc.sbuf_tensor("stgA", [64, 2 * S], BF16,
                                       side="right") as stgA,
                        nc.sbuf_tensor("stgB", [64, 2 * S], BF16,
                                       side="right") as stgB,
                        nc.sbuf_tensor("stgS", [128, 2048], F32,
                                       side="right") as stgS,
                    ):
                        with nc.Block(no_gpsimd_drain=True) as blk:
                            @blk.gpsimd
                            def _(e):
                                wload(e, mask01[:], hin["mask01"])
                                e.wait_ge(s_w, cnt["s_w"])

                        ndiag = []
                        nd = 0
                        for (qc, kt) in STRIPS:
                            ndiag.append(nd)
                            if kt >= 4 * qc:
                                nd += 1
                        nsp = len(STRIPS)

                        bA, bB, bC, bD, bE = (cnt["sA"], cnt["sB"], cnt["sC"],
                                              cnt["sD"], cnt["sE"])
                        with nc.Block(no_gpsimd_drain=True) as blk:
                            @blk.tensor
                            def _(e, bA=bA, bB=bB, bC=bC, bD=bD, bE=bE):
                                allst = [(p, qc, kt) for p in range(NP)
                                         for (qc, kt) in STRIPS]

                                def emit_st(gi):
                                    p, qc, kt = allst[gi]
                                    buf = gi % 2
                                    sp = pa[:, buf*1024:(buf+1)*1024]
                                    if gi >= 2:
                                        e.wait_ge(sB, bB + gi - 1)
                                    mA = e.matmul(
                                        sp[:, 0:512],
                                        ksb[0:64, p*2048 + kt*128:
                                            p*2048 + (kt+1)*128],
                                        qsb[0:64, p*2048 + qc*512:
                                            p*2048 + (qc+1)*512],
                                        start=True, stop=True)
                                    mB = e.matmul(
                                        sp[:, 512:1024],
                                        ksb[64:128, p*2048 + kt*128:
                                            p*2048 + (kt+1)*128],
                                        qsb[64:128, p*2048 + qc*512:
                                            p*2048 + (qc+1)*512],
                                        start=True, stop=True)
                                    INC(mB, "sA")

                                def emit_pv(gi):
                                    p, qc, kt = allst[gi]
                                    si = gi - p * nsp
                                    nkt = 4 * qc + 4
                                    buf = gi % 8
                                    m = p * 4 + qc
                                    acc = (m % 2) * 1024
                                    if kt == 0 and m >= 2:
                                        e.wait_ge(sD, bD + m - 1)
                                    diag = kt >= 4 * qc
                                    if diag:
                                        e.wait_ge(sC, bC + p * 16
                                                  + ndiag[si] + 1)
                                    else:
                                        e.wait_ge(sB, bB + gi + 1)
                                    pvA = e.matmul(
                                        pb[0:65, acc:acc + 512],
                                        vsb[:, kt*1040 + (2*p)*65:
                                            kt*1040 + (2*p)*65 + 65],
                                        psb[:, buf*1024:buf*1024 + 512],
                                        start=(kt == 0), stop=(kt == nkt - 1))
                                    pvB = e.matmul(
                                        pb[0:65, acc + 512:acc + 1024],
                                        vsb[:, kt*1040 + (2*p+1)*65:
                                            kt*1040 + (2*p+1)*65 + 65],
                                        psb[:, buf*1024 + 512:
                                            buf*1024 + 1024],
                                        start=(kt == 0), stop=(kt == nkt - 1))
                                    INC(pvB, "sE")

                                n = len(allst)
                                for gi in range(n + 2):
                                    if gi < n:
                                        emit_st(gi)
                                    if gi >= 2:
                                        emit_pv(gi - 2)

                            @blk.scalar
                            def _(e, bA=bA, bE=bE):
                                for p in range(NP):
                                    for si, (qc, kt) in enumerate(STRIPS):
                                        gi = p * nsp + si
                                        pbuf = gi % 2
                                        buf = gi % 8
                                        sp = pa[:, pbuf*1024:(pbuf+1)*1024]
                                        e.wait_ge(sA, bA + gi + 1)
                                        if gi >= 8:
                                            e.wait_ge(sE, bE + gi - 7)
                                        INC(e.activation(
                                            psb[:, buf*1024:(buf+1)*1024],
                                            sp,
                                            AF.Exp, bias=biasg[:, 0:1],
                                            scale=1.0 / 8.0), "sB")

                            @blk.vector
                            def _(e, bE=bE, bH=cnt["s_st"]):
                                bB = cnt["sB"] - NP * nsp
                                last_gi = {}
                                for si, (qc, kt) in enumerate(STRIPS):
                                    if kt == 4 * qc + 3:
                                        last_gi[qc] = si
                                for p in range(NP):
                                    if p >= 2:
                                        e.wait_ge(s_st, bH + 160 * (p - 1))
                                    for si, (qc, kt) in enumerate(STRIPS):
                                        gi = p * nsp + si
                                        buf = gi % 8
                                        if kt >= 4 * qc:
                                            r = kt - 4 * qc
                                            e.wait_ge(sB, bB + gi + 1)
                                            INC(e.tensor_mul(
                                                psb[:, buf*1024:(buf+1)*1024],
                                                psb[:, buf*1024:(buf+1)*1024],
                                                mask01[:, r*1024:(r+1)*1024]),
                                                "sC")
                                        if kt == 4 * qc + 3:
                                            m = p * 4 + qc
                                            acc = (m % 2) * 1024
                                            if m >= 2:
                                                p2, qc2 = ((m - 2) // 4,
                                                           (m - 2) % 4)
                                                e.wait_ge(s_st, bH + 160 * p2
                                                          + 32 * (qc2 + 1))
                                            e.wait_ge(sE, bE + p * nsp
                                                      + last_gi[qc] + 1)
                                            cA = (qc % 2) * 1024
                                            cB = cA + 512
                                            sa = (p % 2) * 2048
                                            e.tensor_copy(
                                                stgA[:, sa + qc*512:
                                                     sa + (qc+1)*512],
                                                pb[0:64, acc:acc+512])
                                            e.tensor_copy(
                                                stgS[64:65, cA:cA+512],
                                                pb[64:65, acc:acc+512])
                                            e.tensor_copy(
                                                stgB[:, sa + qc*512:
                                                     sa + (qc+1)*512],
                                                pb[0:64, acc+512:acc+1024])
                                            INC(e.tensor_copy(
                                                stgS[64:65, cB:cB+512],
                                                pb[64:65, acc+512:acc+1024]),
                                                "sD")

                            @blk.sync
                            def _(e):
                                bD = cnt["sD"] - NP * 4
                                for p in range(NP):
                                    for qc in range(4):
                                        e.wait_ge(sD, bD + p * 4 + qc + 1)
                                        cA = (qc % 2) * 1024
                                        cB = cA + 512
                                        INC(e.dma_start(
                                            out=sums_hbm[0, p,
                                                         qc*512:(qc+1)*512],
                                            in_=stgS[64:65, cA:cA+512]),
                                            "s_st", 16)
                                        INC(e.dma_start(
                                            out=sums_hbm[1, p,
                                                         qc*512:(qc+1)*512],
                                            in_=stgS[64:65, cB:cB+512]),
                                            "s_st", 16)
                                    sa = (p % 2) * 2048
                                    INC(e.dma_start(
                                        out=arawA_hbm[:, p*S:(p+1)*S],
                                        in_=stgA[:, sa:sa+2048]), "s_st", 16)
                                    INC(e.dma_start(
                                        out=arawB_hbm[:, p*S:(p+1)*S],
                                        in_=stgB[:, sa:sa+2048]), "s_st", 16)
                                e.wait_ge(s_st, cnt["s_st"])

            q2sb = _top.enter_context(nc.sbuf_tensor(
                "q2sb", [128, NC * 2048], BF16, side="right"))
            with (
                nc.sbuf_tensor("w1a", [128, 8192], BF16, side="right") as w1a,
                nc.sbuf_tensor("w2a", [128, 8192], BF16, side="right") as w2a,
                nc.sbuf_tensor("b2r", [16, 2 * 1024], F32R, side="right") as b2r,
                nc.sbuf_tensor("b1sb", [128, 24], F32, side="right") as b1sb,
                nc.sbuf_tensor("rsb2", [NNODE, 3 * S], BF16,
                               side="right") as rsb2,
            ):
                sw_after_rsb = [0]
                with nc.Block(no_gpsimd_drain=True) as blk:
                    @blk.gpsimd
                    def _(e):
                        wload(e, w1a[:], hin["w1_q"])
                        wload(e, w2a[:], hin["w2_q"])
                        wload(e, b2r[:, 0:1024], hin["b2_q"])
                        wload(e, b2r[:, 1024:2048], hin["b2_k"])
                        for i, f in enumerate("qkv"):
                            wload(e, b1sb[:, i*8:(i+1)*8], hin[f"b1_{f}"])
                        wload(e, rsb2[:], rsb_hbm)
                        sw_after_rsb[0] = cnt["s_w"]

                with (
                    nc.sbuf_tensor("asb", [128, NP * 2048], BF16,
                                   side="right") as asb,
                    nc.sbuf_tensor("wo", [128, NC * 1024], BF16,
                                   side="right") as wo,
                    nc.sbuf_tensor("rsrc", [16, S], F32, side="right") as rsrc,
                    nc.sbuf_tensor("rdst", [16, S], F32R, side="right") as rdst,
                    nc.sbuf_tensor("ostage", [128, 2 * 1024], F32,
                                   side="right") as ostage,
                    nc.sbuf_tensor("ebcsb", [16, 8 * 128], F32R,
                                   side="right") as ebc,
                ):
                    bXs = cnt["s_x"]
                    swB = [0]
                    swE = [0]
                    with nc.Block(no_gpsimd_drain=True) as blk:
                        @blk.gpsimd
                        def _(e, bSt=cnt["s_st"]):
                            wload(e, wo[:], hin["w_o"])
                            wload(e, ebc[:], hin["ebc"])
                            swE[0] = cnt["s_w"]
                            e.wait_ge(s_st, bSt)
                            INC(e.dma_start(
                                out=asb.ap()[64:128, :],
                                in_=arawB_hbm), "s_w", 16)
                            swB[0] = cnt["s_w"]

                        @blk.sync
                        def _(e):
                            INC(e.dma_start(
                                out=rsrc[:],
                                in_=sums_hbm.rearrange("h p c -> (h p) c")),
                                "s_x", 16)
                            INC(e.dma_start(
                                out=asb.ap()[0:64, :],
                                in_=arawA_hbm), "s_x", 16)

                        @blk.vector
                        def _(e, bXs=bXs):
                            e.wait_ge(s_x, bXs + 16)
                            with nc.allow_low_precision(
                                    reason="float32r out is 4-byte fp32"):
                                e.reciprocal(rdst[:], rsrc[:])

                    bB, bC, bD = cnt["sB"], cnt["sC"], cnt["sD"]
                    with nc.Block(no_gpsimd_drain=True) as blk:
                        @blk.tensor
                        def _(e, bD=bD, swE=swE[0]):
                            for p in range(NP):
                                ps = pa if p % 2 == 0 else pb
                                if p == 0:
                                    e.wait_ge(s_w, swE)
                                if p >= 2:
                                    e.wait_ge(sD, bD + p - 1)
                                last = None
                                for gq in range(4):
                                    last = e.matmul(
                                        ps[:, gq*512:(gq+1)*512],
                                        ebc[:, p*128:(p+1)*128],
                                        rdst[0:16, gq*512:(gq+1)*512],
                                        start=True, stop=True)
                                INC(last, "sC")

                        @blk.vector
                        def _(e, bC=bC, bXs=bXs, swB=swB[0]):
                            for p in range(NP):
                                ps = pa if p % 2 == 0 else pb
                                if p == 0:
                                    e.wait_ge(s_x, bXs + 32)
                                    e.wait_ge(s_w, swB)
                                e.wait_ge(sC, bC + p + 1)
                                INC(e.tensor_mul(
                                    asb[:, p*2048:(p+1)*2048],
                                    asb[:, p*2048:(p+1)*2048], ps[:, :]), "sD")

                    if debug:
                        with nc.Block(no_gpsimd_drain=True) as blk:
                            @blk.gpsimd
                            def _(e):
                                ostore(e, d_anorm, asb[:])

                    bE, bF, bG = cnt["sE"], cnt["sF"], cnt["sG"]
                    sw_wo = cnt["s_w"]
                    with nc.Block(no_gpsimd_drain=True) as blk:
                        @blk.tensor
                        def _(e, bE=bE, bG=bG, sw_wo=sw_wo):
                            e.wait_ge(s_w, sw_wo)
                            for qt in range(NT):
                                half = (qt % 2) * 1024
                                if qt >= 2:
                                    e.wait_ge(sE, bE + qt - 1)
                                last = None
                                for p in range(NP):
                                    for gq in range(2):
                                        last = e.matmul(
                                            pb[:, half + gq*512:
                                               half + (gq+1)*512],
                                            asb[:, p*2048 + qt*128:
                                                p*2048 + (qt+1)*128],
                                            wo[:, p*1024 + gq*512:
                                               p*1024 + (gq+1)*512],
                                            start=(p == 0), stop=(p == NP-1))
                                INC(last, "sF")

                        @blk.scalar
                        def _(e, bF=bF, bG=cnt["s_st"]):
                            for qt in range(NT):
                                half = (qt % 2) * 1024
                                e.wait_ge(sF, bF + qt + 1)
                                if qt >= 2:
                                    e.wait_ge(s_st, bG + 16 * (qt - 1))
                                INC(e.copy(
                                    ostage[:, half:half + 1024],
                                    pb[:, half:half + 1024]), "sE")

                        @blk.sync
                        def _(e, bG=cnt["s_st"]):
                            bE2 = cnt["sE"] - NT
                            for qt in range(NT):
                                half = (qt % 2) * 1024
                                e.wait_ge(sE, bE2 + qt + 1)
                                if qt > 0:
                                    e.wait_ge(s_st, bG + 16 * qt)
                                INC(e.dma_start(
                                    out=attn_hbm[qt*128:(qt+1)*128, :],
                                    in_=ostage[:, half:half + 1024]), "s_st",
                                    16)

                with (
                    nc.sbuf_tensor("w1b", [128, 8192], BF16,
                                   side="right") as w1b,
                    nc.sbuf_tensor("w2b", [128, 8192], BF16,
                                   side="right") as w2b,
                    nc.sbuf_tensor("hid", [128, NP * 2048], BF16,
                                   side="right") as hid,
                    nc.sbuf_tensor("self32r", [16, S], F32R,
                                   side="right") as self32r,
                    nc.sbuf_tensor("maskm1", [16, S], BF16,
                                   side="right") as maskm1,
                    nc.sbuf_tensor("fstage", [128, 2 * 2048], BF16,
                                   side="right") as fstage,
                ):
                    with nc.Block(no_gpsimd_drain=True) as blk:
                        @blk.gpsimd
                        def _(e):
                            wload(e, w1b[:], hin["w1_k"])
                            wload(e, w2b[:], hin["w2_k"])

                    for fi, f in enumerate("qkv"):
                        w1x = (w1a, w1b, w1a)[fi]
                        w2x = (w2a, w2b, w2a)[fi]
                        dsth = (q2_hbm, k2_hbm, v2_hbm)[fi]

                        bA, bB, bC, bD, bE, bG = (
                            cnt["sA"], cnt["sB"], cnt["sC"], cnt["sD"],
                            cnt["sE"], cnt["s_st"])
                        sw_need = sw_after_rsb[0] if fi == 0 else cnt["s_w"]
                        with nc.Block(no_gpsimd_drain=True) as blk:
                            @blk.tensor
                            def _(e, fi=fi, w1x=w1x, w2x=w2x, bA=bA, bB=bB,
                                  bC=bC, bD=bD, bE=bE, bG=bG,
                                  sw_need=sw_need):
                                e.wait_ge(s_w, sw_need)
                                last = None
                                for gq in range(4):
                                    last = e.matmul(
                                        pb[0:16, gq*512:(gq+1)*512], aroute[:],
                                        rsb2[:, fi*S + gq*512:
                                             fi*S + (gq+1)*512],
                                        start=True, stop=True)
                                INC(last, "sA")
                                for p in range(NP):
                                    ps = pa if p % 2 == 0 else pb
                                    if p == 1:
                                        e.wait_ge(sB, bB + 1)
                                    if p >= 2:
                                        e.wait_ge(sC, bC + p - 1)
                                    for dc in range(NC):
                                        for gq in range(4):
                                            e.matmul(
                                                ps[:, gq*512:(gq+1)*512],
                                                w1x[:, dc*1024 + p*128:
                                                    dc*1024 + (p+1)*128],
                                                xnfm[:, dc*2048 + gq*512:
                                                     dc*2048 + (gq+1)*512],
                                                start=(dc == 0), stop=False)
                                    e.wait_ge(sB, bB + 1)
                                    last = None
                                    for gq in range(4):
                                        last = e.matmul(
                                            ps[:, gq*512:(gq+1)*512],
                                            bige[:, p*128:(p+1)*128],
                                            maskm1[0:16, gq*512:(gq+1)*512],
                                            start=False, stop=True)
                                    INC(last, "sD")
                                if fi < 2:
                                    for oc in range(NC):
                                        ps = pa if oc % 2 == 0 else pb
                                        e.wait_ge(sC, bC + NP)
                                        if oc >= 2:
                                            e.wait_ge(sE, bE + oc - 1)
                                        for p in range(NP):
                                            for gq in range(4):
                                                e.matmul(
                                                    ps[:, gq*512:(gq+1)*512],
                                                    w2x[:, p*1024 + oc*128:
                                                        p*1024 + (oc+1)*128],
                                                    hid[:, p*2048 + gq*512:
                                                        p*2048 + (gq+1)*512],
                                                    start=(p == 0), stop=False)
                                        last = None
                                        for gq in range(4):
                                            last = e.matmul(
                                                ps[:, gq*512:(gq+1)*512],
                                                b2r[0:16,
                                                    (fi % 2)*1024 + oc*128:
                                                    (fi % 2)*1024
                                                    + (oc+1)*128],
                                                self32r[0:16,
                                                        gq*512:(gq+1)*512],
                                                start=False, stop=True)
                                        INC(last, "sF")
                                else:
                                    for tt in range(NT):
                                        ps = pa if tt % 2 == 0 else pb
                                        half = 0
                                        e.wait_ge(sC, bC + NP)
                                        if tt >= 2:
                                            e.wait_ge(sE, bE + tt - 1)
                                        for p in range(NP):
                                            for gq in range(2):
                                                e.matmul(
                                                    ps[:, gq*512:(gq+1)*512],
                                                    hid[:, p*2048 + tt*128:
                                                        p*2048 + (tt+1)*128],
                                                    w2x[:, p*1024 + gq*512:
                                                        p*1024 + (gq+1)*512],
                                                    start=(p == 0), stop=False)
                                        last = None
                                        for gq in range(2):
                                            last = e.matmul(
                                                ps[:, gq*512:(gq+1)*512],
                                                self32r[0:16,
                                                        tt*128:(tt+1)*128],
                                                b2r[0:16,
                                                    (fi % 2)*1024 + gq*512:
                                                    (fi % 2)*1024
                                                    + (gq+1)*512],
                                                start=False, stop=True)
                                        INC(last, "sF")

                            @blk.vector
                            def _(e, fi=fi, bA=bA):
                                e.wait_ge(sA, bA + 1)
                                e.tensor_scalar(
                                    self32r[0:16, :], pb[0:16, :],
                                    thr[:, 0:1], None, ALU.is_ge)
                                INC(e.tensor_scalar(
                                    maskm1[0:16, :], pb[0:16, :],
                                    thr[:, 0:1], -1.0, ALU.is_ge, ALU.add),
                                    "sB")

                            @blk.scalar
                            def _(e, fi=fi, bD=bD):
                                for p in range(NP):
                                    ps = pa if p % 2 == 0 else pb
                                    e.wait_ge(sD, bD + p + 1)
                                    INC(e.activation(
                                        hid[:, p*2048:(p+1)*2048], ps[:, :],
                                        AF.Relu,
                                        bias=b1sb[:, fi*8+p:fi*8+p+1]), "sC")
                                nsub = NC if fi < 2 else NT
                                w = 2048 if fi < 2 else 1024
                                bF = cnt["sF"] - nsub
                                for u in range(nsub):
                                    ps = pa if u % 2 == 0 else pb
                                    e.wait_ge(sF, bF + u + 1)
                                    if fi == 0:
                                        INC(e.copy(
                                            q2sb[:, u*2048:(u+1)*2048],
                                            ps[:, 0:w]), "sE")
                                        continue
                                    if u >= 2:
                                        e.wait_ge(s_st, bG + 16 * (u - 1))
                                    INC(e.copy(
                                        fstage[:, (u % 2)*2048:
                                               (u % 2)*2048 + w],
                                        ps[:, 0:w]), "sE")

                            @blk.sync
                            def _(e, fi=fi, dsth=dsth, bG=bG):
                                if fi == 0:
                                    return
                                nsub = NC if fi < 2 else NT
                                w = 2048 if fi < 2 else 1024
                                bE2 = cnt["sE"] - nsub
                                for u in range(nsub):
                                    e.wait_ge(sE, bE2 + u + 1)
                                    if u > 0:
                                        e.wait_ge(s_st, bG + 16 * u)
                                    INC(e.dma_start(
                                        out=dsth[:, u*w:(u+1)*w],
                                        in_=fstage[:, (u % 2)*2048:
                                                   (u % 2)*2048 + w]),
                                        "s_st", 16)

                            if fi == 1:
                                @blk.gpsimd
                                def _(e):
                                    wload(e, w1a[:], hin["w1_v"])
                                    wload(e, w2a[:], hin["w2_v"])
                                    wload(e, b2r[:, 0:1024], hin["b2_v"])

                        if debug and f == "q":
                            with nc.Block(no_gpsimd_drain=True) as blk:
                                @blk.gpsimd
                                def _(e, fi=fi):
                                    ostore(e, d_hid, hid[:])
                                    ostore(e, d_sel.rearrange(
                                        "p (f s) -> p f s", f=3)[:, fi, :],
                                        self32r[0:16, :])
                                    ostore(e, q2_hbm, q2sb[:])

        with (
            nc.sbuf_tensor("k2sb", [128, NC * 2048], BF16, side="right") as k2sb,
            nc.sbuf_tensor("v2sb", [128, NT * 1024], BF16, side="right") as v2sb,
        ):
            b_ld = {}
            with nc.Block(no_gpsimd_drain=True) as blk:
                @blk.scalar
                def _(e):
                    e.wait_ge(s_st, cnt["s_st"])
                    INC(e.dma_start(out=k2sb[:, 0:8192],
                                    in_=k2_hbm[:, 0:8192]), "s_tr", 16)
                    b_ld["k2"] = cnt["s_tr"]

                @blk.sync
                def _(e):
                    e.wait_ge(s_st, cnt["s_st"])
                    INC(e.dma_start(out=k2sb[:, 8192:16384],
                                    in_=k2_hbm[:, 8192:16384]), "s_x", 16)
                    b_ld["k2x"] = cnt["s_x"]

                @blk.gpsimd
                def _(e):
                    e.wait_ge(s_st, cnt["s_st"])
                    INC(e.dma_start(out=v2sb[:], in_=v2_hbm), "s_w", 16)
                    b_ld["v2"] = cnt["s_w"]

            with nc.sbuf_tensor("sq2", [128, 2 * 2048], BF16,
                                side="right") as sq2:
                bA, bB = cnt["sA"], cnt["sB"]
                with nc.Block(no_gpsimd_drain=True) as blk:
                    @blk.vector
                    def _(e, bA=bA, bB=bB):
                        for i, src in enumerate((q2sb, k2sb)):
                            for dc in range(NC):
                                j = i * NC + dc
                                if i == 1 and dc == 0:
                                    e.wait_ge(s_tr, b_ld["k2"])
                                if i == 1 and dc == 4:
                                    e.wait_ge(s_x, b_ld["k2x"])
                                if j >= 2:
                                    e.wait_ge(sB, bB + j - 1)
                                INC(e.tensor_mul(
                                    sq2[:, (j % 2)*2048:(j % 2 + 1)*2048],
                                    src[:, dc*2048:(dc+1)*2048],
                                    src[:, dc*2048:(dc+1)*2048]), "sA")

                    @blk.tensor
                    def _(e, bA=bA):
                        for i in range(2):
                            ps = pa if i == 0 else pb
                            for dc in range(NC):
                                j = i * NC + dc
                                e.wait_ge(sA, bA + j + 1)
                                last = None
                                for gq in range(4):
                                    last = e.matmul(
                                        ps[0:1, gq*512:(gq+1)*512], onescol[:],
                                        sq2[:, (j % 2)*2048 + gq*512:
                                            (j % 2)*2048 + (gq+1)*512],
                                        start=(dc == 0), stop=(dc == NC - 1))
                                INC(last, "sB")

                b2A, b2B, b2C = cnt["sA"], cnt["sB"], cnt["sC"]
                with nc.Block(no_gpsimd_drain=True) as blk:
                    @blk.vector
                    def _(e, b2B=b2B):
                        e.reduce_max(tiny[0:1, 6:7], pa[0:1, :], axis=AX.X)
                        INC(e.reduce_max(tiny[0:1, 7:8], pb[0:1, :],
                                         axis=AX.X), "sA")
                        e.wait_ge(sB, b2B + 1)
                        INC(e.tensor_mul(tiny[0:1, 10:11], tiny[0:1, 8:9],
                                         tiny[0:1, 9:10]), "sB")

                    @blk.scalar
                    def _(e, b2A=b2A, b2B=b2B, b2C=b2C):
                        e.wait_ge(sA, b2A + 1)
                        e.activation(tiny[0:1, 8:9], tiny[0:1, 6:7], AF.Sqrt)
                        INC(e.activation(tiny[0:1, 9:10], tiny[0:1, 7:8],
                                         AF.Sqrt), "sB")
                        e.wait_ge(sB, b2B + 2)
                        INC(e.copy(tinyr[0:1, 2:3], tiny[0:1, 10:11]), "sB")
                        e.wait_ge(sC, b2C + 1)
                        e.activation(bias2[:], pa[:, 16:17], AF.Copy,
                                     scale=-1.0 / 32.0)

                    @blk.tensor
                    def _(e, b2B=b2B):
                        e.wait_ge(sB, b2B + 3)
                        INC(e.matmul(pa[:, 16:17], onesrow[:],
                                     tinyr[0:1, 2:3],
                                     start=True, stop=True), "sC")
                if debug:
                    with nc.Block(no_gpsimd_drain=True) as blk:
                        @blk.gpsimd
                        def _(e):
                            ostore(e, d_bias2, bias2[:])

            with (
                nc.sbuf_tensor("p2sb", [128, 2 * 16 * 512], BF16,
                               side="right") as p2sb,
                nc.sbuf_tensor("o2sb", [128, 2 * NC * 512], F32,
                               side="right") as o2sb,
                nc.sbuf_tensor("r2row", [1, 512], F32, side="right") as r2row,
                nc.sbuf_tensor("r2col", [128, 2], F32, side="right") as r2col,
                nc.sbuf_tensor("xstg", [128, 2 * 1024], F32,
                               side="right") as xstg,
                nc.sbuf_tensor("astg", [128, 2 * 1024], F32,
                               side="right") as astg,
                nc.sbuf_tensor("ysb", [128, 2 * 1024], F32,
                               side="right") as ysb,
            ):
                bA, bB, bC, bD, bE, bF, bG, bH = (
                    cnt["sA"], cnt["sB"], cnt["sC"], cnt["sD"], cnt["sE"],
                    cnt["sF"], cnt["sG"], cnt["sH"])
                bX, bO, bP = cnt["s_x"], cnt["s_out"], cnt.get("s_p", 0)
                with nc.Block(no_gpsimd_drain=True) as blk:
                    @blk.tensor
                    def _(e):
                        for qc in range(4):
                            pslot = (qc % 2) * 16 * 512
                            oslot = (qc % 2) * NC * 512

                            def emit_sc(kt, qc=qc):
                                buf = kt % 2
                                if qc * 16 + kt >= 2:
                                    e.wait_ge(sB, bB + qc * 16 + kt - 1)
                                last = None
                                for dc in range(NC):
                                    last = e.matmul(
                                        pa[:, buf*512:(buf+1)*512],
                                        k2sb[:, dc*2048 + kt*128:
                                             dc*2048 + (kt+1)*128],
                                        q2sb[:, dc*2048 + qc*512:
                                             dc*2048 + (qc+1)*512],
                                        start=(dc == 0), stop=(dc == NC - 1))
                                INC(last, "sA")

                            def emit_sum(kt, qc=qc, pslot=pslot):
                                e.wait_ge(sB, bB + qc * 16 + kt + 1)
                                if kt == 0 and qc > 0:
                                    e.wait_ge(sG, bG + qc)
                                last = e.matmul(
                                    pa[0:1, 1024:1536], onescol[:],
                                    p2sb[:, pslot + kt*512:
                                         pslot + (kt+1)*512],
                                    start=(kt == 0), stop=(kt == NT - 1))
                                if kt == NT - 1:
                                    INC(last, "sC")

                            for kt in range(NT + 1):
                                if kt < NT:
                                    emit_sc(kt)
                                if kt >= 1:
                                    emit_sum(kt - 1)

                            for dc in range(NC):
                                ps2 = (pa if dc % 2 == 0 else pb)
                                if qc == 0 and dc == 0:
                                    e.wait_ge(s_w, b_ld["v2"])
                                if dc >= 2:
                                    e.wait_ge(sD, bD + qc * 8 + dc - 1)
                                last = None
                                for kt in range(NT):
                                    last = e.matmul(
                                        ps2[:, 1536:2048],
                                        v2sb[:, kt*1024 + dc*128:
                                             kt*1024 + (dc+1)*128],
                                        p2sb[:, pslot + kt*512:
                                             pslot + (kt+1)*512],
                                        start=(kt == 0), stop=(kt == NT - 1))
                                INC(last, "sE")

                            for qs in range(4):
                                if qc * 4 + qs >= 1:
                                    e.wait_ge(s_p, bP + qc * 4 + qs)
                                e.wait_ge(sD, bD + qc * 8 + 8)
                                for dc in range(NC):
                                    e.transpose(
                                        pb[:, dc*128:(dc+1)*128],
                                        o2sb[:, oslot + dc*512 + qs*128:
                                             oslot + dc*512 + (qs+1)*128],
                                        ident32[:])
                                e.wait_ge(sG, bG + qc + 1)
                                last = e.matmul(
                                    pb[:, 1024:1025],
                                    r2row[0:1, qs*128:(qs+1)*128],
                                    one11[:], start=True, stop=True)
                                INC(last, "sF")

                    @blk.scalar
                    def _(e):
                        for qc in range(4):
                            pslot = (qc % 2) * 16 * 512
                            oslot = (qc % 2) * NC * 512
                            for kt in range(NT):
                                buf = kt % 2
                                e.wait_ge(sA, bA + qc * 16 + kt + 1)
                                if qc >= 2:
                                    e.wait_ge(sE, bE + 8 * (qc - 1))
                                INC(e.activation(
                                    p2sb[:, pslot + kt*512:
                                         pslot + (kt+1)*512],
                                    pa[:, buf*512:(buf+1)*512],
                                    AF.Exp, bias=bias2[:, 0:1],
                                    scale=1.0 / 32.0), "sB")
                            for dc in range(NC):
                                ps2 = (pa if dc % 2 == 0 else pb)
                                e.wait_ge(sE, bE + qc * 8 + dc + 1)
                                INC(e.copy(
                                    o2sb[:, oslot + dc*512:
                                         oslot + (dc+1)*512],
                                    ps2[:, 1536:2048]), "sD")
                            for qs in range(4):
                                if qc * 4 + qs >= 2:
                                    e.wait_ge(s_p, bP + qc * 4 + qs - 1)
                                e.wait_ge(sF, bF + qc * 4 + qs + 1)
                                INC(e.copy(r2col[:, qs % 2:qs % 2 + 1],
                                           pb[:, 1024:1025]), "sC")

                    @blk.vector
                    def _(e):
                        for qc in range(4):
                            e.wait_ge(sC, bC + qc * 5 + 1)
                            INC(e.reciprocal(r2row[:], pa[0:1, 1024:1536]),
                                "sG")
                            for qs in range(4):
                                slot = (qs % 2) * 1024
                                if qc * 4 + qs >= 2:
                                    e.wait_ge(s_out,
                                              bO + 16 * (qc * 4 + qs - 1))
                                e.wait_ge(sC, bC + qc * 5 + qs + 2)
                                e.wait_ge(s_x, bX + 32 * (qc * 4 + qs + 1))
                                INC(e.scalar_tensor_tensor(
                                    ysb[:, slot:slot+1024], pb[:, 0:1024],
                                    r2col[:, qs % 2:qs % 2 + 1],
                                    astg[:, slot:slot+1024],
                                    ALU.mult, ALU.add), "s_p")
                                INC(e.tensor_add(
                                    ysb[:, slot:slot+1024],
                                    ysb[:, slot:slot+1024],
                                    xstg[:, slot:slot+1024]), "sH")

                    @blk.sync
                    def _(e):
                        for qc in range(4):
                            for qs in range(4):
                                qg = qc * 4 + qs
                                slot = (qs % 2) * 1024
                                if qg >= 2:
                                    e.wait_ge(sH, bH + qg - 1)
                                if qg > 0:
                                    e.wait_ge(s_x, bX + 32 * qg)
                                INC(e.dma_start(
                                    out=xstg[:, slot:slot+1024],
                                    in_=x_h[qg*128:(qg+1)*128, :]), "s_x", 16)
                                INC(e.dma_start(
                                    out=astg[:, slot:slot+1024],
                                    in_=attn_hbm[qg*128:(qg+1)*128, :]),
                                    "s_x", 16)

                    @blk.gpsimd
                    def _(e):
                        for qc in range(4):
                            for qs in range(4):
                                qg = qc * 4 + qs
                                slot = (qs % 2) * 1024
                                e.wait_ge(sH, bH + qg + 1)
                                if qg > 0:
                                    e.wait_ge(s_out, bO + 16 * qg)
                                INC(e.dma_start(
                                    out=y_h[qg*128:(qg+1)*128, :],
                                    in_=ysb[:, slot:slot+1024]), "s_out", 16)

        with nc.Block(no_gpsimd_drain=True) as blk:
            @blk.gpsimd
            def _(e):
                e.wait_ge(s_out, cnt["s_out"])

    return nc


def kernel(**inputs):
    hp = host_prep(inputs)
    x = np.ascontiguousarray(np.asarray(inputs["L0_attn_hidden_states"],
                                        dtype=np.float32))
    nc = build_program(debug=False)
    in_maps = []
    for c in range(B):
        m = {"x": np.ascontiguousarray(x[c])}
        m.update(hp)
        in_maps.append(m)
    res = run_bass_kernel_spmd(nc, in_maps, core_ids=list(range(8)))
    out = np.stack([np.asarray(res.results[c]["y"]) for c in range(B)], axis=0)
    return out.astype(np.float32)


# version 27
# speedup vs baseline: 1.0262x; 1.0073x over previous
import contextlib
import numpy as np
import ml_dtypes
import concourse.bass as bass
import concourse.mybir as mybir
from concourse.bass_utils import run_bass_kernel_spmd

F32 = mybir.dt.float32
F32R = mybir.dt.float32r
BF16 = mybir.dt.bfloat16
AF = mybir.ActivationFunctionType
ALU = mybir.AluOpType
AX = mybir.AxisListType

B, S, D = 8, 2048, 1024
NH, HD, RD = 16, 64, 32
NT = S // 128
NC = D // 128
NP = 8
NLEAF, NNODE = 16, 15
BIG = 1.0e9
EPS = 1e-5


def _chunkmajor(w):
    n = w.shape[1]
    return np.ascontiguousarray(
        w.reshape(NC, 128, n).transpose(1, 0, 2).reshape(128, NC * n))


def host_prep(inp):
    h = {}
    w_qkvT = np.ascontiguousarray(np.asarray(inp["L0_attn_w_qkv"]).T)
    qcols = np.concatenate([w_qkvT[:, mp*768:mp*768+256] for mp in range(4)], 1)
    vcols = np.concatenate([w_qkvT[:, mp*768+256:mp*768+512] for mp in range(4)], 1)
    kcols = np.concatenate([w_qkvT[:, mp*768+512:mp*768+768] for mp in range(4)], 1)
    h["w_q"] = _chunkmajor(qcols)
    h["w_k"] = _chunkmajor(kcols)
    h["w_v"] = _chunkmajor(vcols)
    h["w_o"] = _chunkmajor(
        np.ascontiguousarray(np.asarray(inp["L0_attn_w_out"]).T))

    pos = np.arange(S, dtype=np.float64)
    invf = 1.0 / 10000.0 ** (np.arange(0, RD, 2, dtype=np.float64) / RD)
    ang = np.outer(invf, pos)
    cos = np.ones((64, S), np.float64)
    sin = np.zeros((64, S), np.float64)
    for d in range(RD):
        cos[d] = np.cos(ang[d // 2])
        sin[d] = np.sin(ang[d // 2])
    h["cos"] = np.tile(cos, (2, 1)).astype(np.float32)
    h["sin"] = np.tile(sin, (2, 1)).astype(np.float32)
    P = np.zeros((128, 128), np.float32)
    for base in (0, 64):
        for j in range(RD // 2):
            P[base + 2*j + 1, base + 2*j] = -1.0
            P[base + 2*j, base + 2*j + 1] = 1.0
    h["prope"] = P
    h["ident"] = np.eye(128, dtype=np.float32)

    m = np.zeros((128, 4, 2, 512), np.float32)
    for r in range(4):
        for ck in range(128):
            m[ck, r, :, :] = (np.arange(512) >= 128*r + ck)[None, :]
    h["mask01"] = m.reshape(128, 4096)

    e = np.zeros((128, NP, 16), np.float32)
    for p in range(NP):
        e[0:64, p, 2*p] = 1.0
        e[64:128, p, 2*p+1] = 1.0
    h["enorm"] = e.reshape(128, NP * 16)

    A = np.zeros((NNODE, NLEAF), np.float32)
    thr = np.zeros((NLEAF, 1), np.float32)
    for leaf in range(NLEAF):
        bits = [(leaf >> (3 - t)) & 1 for t in range(4)]
        cur, nleft = 0, 0
        for t in range(4):
            A[cur, leaf] = 1.0 if bits[t] else -1.0
            nleft += 0 if bits[t] else 1
            cur = 2*cur + 1 + bits[t]
        thr[leaf, 0] = 3.5 - nleft
    h["aroute"] = A
    h["thr"] = thr

    node_wT = np.concatenate(
        [np.asarray(inp[f"L0_ffn_{f}_node_w"]).T for f in "qkv"], axis=1)
    h["node_w"] = _chunkmajor(node_wT)
    h["node_b"] = np.ascontiguousarray(np.stack(
        [np.asarray(inp[f"L0_ffn_{f}_node_b"]).astype(np.float32)
         for f in "qkv"], axis=1))

    bige = np.zeros((16, NP, 128), np.float32)
    for p in range(NP):
        bige[2*p, p, 0:64] = BIG
        bige[2*p+1, p, 64:128] = BIG
    h["bige"] = bige.reshape(16, NP * 128)
    half01 = np.zeros((2, 128), np.float32)
    half01[0, 0:64] = 1.0
    half01[1, 64:128] = 1.0
    h["half01"] = half01
    ebc = np.zeros((16, NP * 128), np.float32)
    for p in range(NP):
        ebc[p, p*128:p*128 + 64] = 1.0
        ebc[8 + p, p*128 + 64:p*128 + 128] = 1.0
    h["ebc"] = ebc
    h["onescol"] = np.ones((128, 1), np.float32)
    h["onesrow"] = np.ones((1, 128), np.float32)
    h["one11"] = np.ones((1, 1), np.float32)

    for f in "qkv":
        w1 = np.asarray(inp[f"L0_ffn_{f}_w1"])
        w1T = np.ascontiguousarray(w1.transpose(1, 0, 2).reshape(D, 1024))
        h[f"w1_{f}"] = _chunkmajor(w1T)
        w2 = np.asarray(inp[f"L0_ffn_{f}_w2"])
        h[f"w2_{f}"] = np.ascontiguousarray(
            w2.reshape(NP, 128, 1024).transpose(1, 0, 2).reshape(128, NP * 1024))
        b1 = np.asarray(inp[f"L0_ffn_{f}_b1"])
        h[f"b1_{f}"] = np.ascontiguousarray(
            b1.reshape(NP, 128).T).astype(np.float32)
        h[f"b2_{f}"] = np.asarray(inp[f"L0_ffn_{f}_b2"]).astype(np.float32)
    return h


HOSTSHAPES = {
    "w_q": [128, NC * 1024], "w_k": [128, NC * 1024],
    "w_v": [128, NC * 1024], "w_o": [128, NC * 1024],
    "cos": [128, S], "sin": [128, S], "prope": [128, 128], "ident": [128, 128],
    "mask01": [128, 4096], "enorm": [128, NP * 16],
    "aroute": [NNODE, NLEAF], "thr": [NLEAF, 1],
    "node_w": [128, NC * 45], "node_b": [NNODE, 3], "bige": [16, NP * 128],
    "half01": [2, 128], "onescol": [128, 1], "onesrow": [1, 128], "one11": [1, 1],
    "ebc": [16, 8 * 128],
}
for _f in "qkv":
    HOSTSHAPES[f"w1_{_f}"] = [128, 8192]
    HOSTSHAPES[f"w2_{_f}"] = [128, 8192]
    HOSTSHAPES[f"b1_{_f}"] = [128, 8]
    HOSTSHAPES[f"b2_{_f}"] = [16, 1024]

STRIPS = [(qc, kt) for qc in range(4) for kt in range(4 * qc + 4)]


def build_program(debug=False):
    nc = bass.Bass("TRN2", target_bir_lowering=False,
                   detect_race_conditions=False)

    def din(name, shape, dt=F32):
        return nc.dram_tensor(name, shape, dt, kind="ExternalInput").ap()

    x_h = din("x", [S, D])
    hin = {k: din(k, v) for k, v in HOSTSHAPES.items()}
    y_h = nc.dram_tensor("y", [S, D], F32, kind="ExternalOutput").ap()

    sk = "ExternalOutput" if debug else "Internal"
    attn_hbm = nc.dram_tensor("attn_sc", [S, D], F32, kind=sk).ap()
    q2_hbm = nc.dram_tensor("q2_sc", [128, NC * 2048], BF16, kind=sk).ap()
    k2_hbm = nc.dram_tensor("k2_sc", [128, NC * 2048], BF16, kind=sk).ap()
    v2_hbm = nc.dram_tensor("v2_sc", [128, NT * 1024], BF16, kind=sk).ap()
    arawA_hbm = nc.dram_tensor("arawA_sc", [64, NP * S], BF16, kind=sk).ap()
    arawB_hbm = nc.dram_tensor("arawB_sc", [64, NP * S], BF16, kind=sk).ap()
    sums_hbm = nc.dram_tensor("sums_sc", [2, NP, S], F32, kind=sk).ap()
    rsb_hbm = nc.dram_tensor("rsb_sc", [NNODE, 3 * S], BF16, kind="Internal").ap()

    def dbg(name, shape, dt=F32):
        if not debug:
            return None
        return nc.dram_tensor("dbg_" + name, shape, dt,
                              kind="ExternalOutput").ap()

    d_xnfm = dbg("xnfm", [128, NC * 2048], BF16)
    d_xnlo = dbg("xnlo", [128, NC * 2048], BF16)
    d_rsb = dbg("rsb", [NNODE, 3 * S], BF16)
    d_q = dbg("q", [128, NP * 2048], BF16)
    d_k = dbg("k", [128, NP * 2048], BF16)
    d_v = dbg("v", [128, NT * 1040], BF16)
    d_biasg = dbg("biasg", [128, 1])
    d_anorm = dbg("anorm", [128, NP * 2048], BF16)
    d_sel = dbg("sel", [16, 3 * S])
    d_hid = dbg("hid", [128, NP * 2048], BF16)
    d_bias2 = dbg("bias2", [128, 1])
    d_stats = dbg("stats", [128, 80])
    d_hstage = dbg("hstage", [128, 4096], BF16)
    d_xstage = dbg("xstage", [128, 4096])

    cnt = {}

    with contextlib.ExitStack() as _top:
        def _sb(name, shape, dt, side="left"):
            return _top.enter_context(nc.sbuf_tensor("sb_" + name, shape, dt, side=side))
        ident32 = _sb("ident32", [128, 128], F32)
        ident16 = _sb("ident16", [128, 128], BF16)
        prope = _sb("prope", [128, 128], BF16)
        enorm = _sb("enorm", [128, NP * 16], BF16)
        aroute = _sb("aroute", [NNODE, NLEAF], BF16)
        thr = _sb("thr", [NLEAF, 1], F32)
        nodeb = _sb("nodeb", [NNODE, 3], F32)
        bige = _sb("bige", [16, NP * 128], BF16)
        half01 = _sb("half01", [2, 128], F32R)
        onescol = _sb("onescol", [128, 1], BF16)
        onesrow = _sb("onesrow", [1, 128], BF16)
        one11 = _sb("one11", [1, 1], F32)
        nodewh = _sb("nodewh", [128, NC * 45], BF16)
        nodewl = _sb("nodewl", [128, NC * 45], BF16)
        biasg = _sb("biasg", [128, 1], F32)
        bias2 = _sb("bias2", [128, 1], F32)
        tiny = _sb("tiny", [128, 16], F32)
        tinyr = _sb("tinyr", [128, 4], BF16)
        pa = _top.enter_context(nc.psum_tensor("pa", [128, 2048], F32))
        pb = _top.enter_context(nc.psum_tensor("pb", [128, 2048], F32))
        sA = _top.enter_context(nc.semaphore("sA"))
        sB = _top.enter_context(nc.semaphore("sB"))
        sC = _top.enter_context(nc.semaphore("sC"))
        sD = _top.enter_context(nc.semaphore("sD"))
        sE = _top.enter_context(nc.semaphore("sE"))
        sF = _top.enter_context(nc.semaphore("sF"))
        sG = _top.enter_context(nc.semaphore("sG"))
        sH = _top.enter_context(nc.semaphore("sH"))
        s_w = _top.enter_context(nc.semaphore("s_w"))
        s_x = _top.enter_context(nc.semaphore("s_x"))
        s_tr = _top.enter_context(nc.semaphore("s_tr"))
        s_st = _top.enter_context(nc.semaphore("s_st"))
        s_out = _top.enter_context(nc.semaphore("s_out"))
        s_p = _top.enter_context(nc.semaphore("s_p"))
        SEM = {"sA": sA, "sB": sB, "sC": sC, "sD": sD, "sE": sE, "sF": sF,
               "sG": sG, "sH": sH, "s_w": s_w, "s_x": s_x, "s_tr": s_tr,
               "s_st": s_st, "s_out": s_out, "s_p": s_p}
        for k in SEM:
            cnt[k] = 0

        def INC(instr, name, k=1):
            instr.then_inc(SEM[name], k)
            cnt[name] += k
            return instr

        def wload(e, dst, src):
            INC(e.dma_start(out=dst, in_=src), "s_w", 16)

        def ostore(e, dst, src):
            INC(e.dma_start(out=dst, in_=src), "s_out", 16)

        with nc.sbuf_tensor("nodew32", [128, NC * 45], F32,
                            side="right") as nodew32:
            with nc.Block(no_gpsimd_drain=True) as blk:
                @blk.gpsimd
                def _(e):
                    for dst, key in (
                        (prope, "prope"), (enorm, "enorm"), (aroute, "aroute"),
                        (bige, "bige"), (onescol, "onescol"),
                        (ident32, "ident"), (ident16, "ident"), (thr, "thr"),
                        (nodeb, "node_b"), (half01, "half01"),
                        (onesrow, "onesrow"), (one11, "one11"),
                        (nodew32, "node_w"),
                    ):
                        wload(e, dst[:], hin[key])
                    e.wait_ge(s_w, 16 * (cnt["s_w"] // 16))

            with nc.Block(no_gpsimd_drain=True) as blk:
                @blk.scalar
                def _(e):
                    e.copy(nodewh[:], nodew32[:])
            with nc.Block(no_gpsimd_drain=True) as blk:
                @blk.vector
                def _(e):
                    e.tensor_sub(nodewl[:], nodew32[:], nodewh[:])
                    e.memset(tiny[:, 15:16], EPS)

        with nc.sbuf_tensor("xnfm", [128, NC * 2048], BF16, side="left") as xnfm:
            with contextlib.ExitStack() as stW:
                wqh = stW.enter_context(nc.sbuf_tensor(
                    "wqh", [128, NC * 1024], BF16, side="right"))
                wv = stW.enter_context(nc.sbuf_tensor(
                    "wv", [128, NC * 1024], BF16, side="right"))
                cossb = stW.enter_context(nc.sbuf_tensor(
                    "cossb", [128, S], BF16, side="right"))
                sinsb = stW.enter_context(nc.sbuf_tensor(
                    "sinsb", [128, S], BF16, side="right"))
                with (
                    nc.sbuf_tensor("rsb", [NNODE, 3 * S], BF16,
                                   side="right") as rsb,
                    nc.sbuf_tensor("xnlo", [128, NC * 2048], BF16,
                                   side="right") as xnlo,
                    nc.sbuf_tensor("xstage", [128, 8 * 1024], F32,
                                   side="right") as xstage,
                    nc.sbuf_tensor("xnf32", [128, 1024], F32,
                                   side="right") as xnf32,
                    nc.sbuf_tensor("hstage", [128, 8 * 1024], BF16,
                                   side="right") as hstage,
                    nc.sbuf_tensor("lstage", [128, 8 * 1024], BF16,
                                   side="right") as lstage,
                    nc.sbuf_tensor("stats", [128, 96], F32,
                                   side="right") as stats,
                ):
                    with nc.Block(no_gpsimd_drain=True) as blk:
                        @blk.gpsimd
                        def _(e):
                            wload(e, wqh[:], hin["w_q"])
                            wload(e, wv[:], hin["w_v"])
                            wload(e, cossb[:], hin["cos"])
                            wload(e, sinsb[:], hin["sin"])

                    bB0, bC0, bE0, bF0, bG0, bH0 = (
                        cnt["sB"], cnt["sC"], cnt["sE"], cnt["sF"],
                        cnt["sG"], cnt["sH"])
                    bX0 = cnt["s_x"]
                    with nc.Block(no_gpsimd_drain=True) as blk:
                        @blk.sync
                        def _(e):
                            for g in range(4):
                                for i in range(4):
                                    t = g * 4 + i
                                    gs = (g % 2) * 4096
                                    if g >= 2 and i == 0:
                                        e.wait_ge(sB, bB0 + 4 * (g - 1))
                                        e.wait_ge(sC, bC0 + 4 * (g - 1))
                                    INC(e.dma_start(
                                        out=xstage[:, gs + i*1024:
                                                   gs + (i+1)*1024],
                                        in_=x_h[t*128:(t+1)*128, :]),
                                        "s_x", 16)

                        @blk.scalar
                        def _(e):
                            for g in range(4):
                                t0 = g * 4
                                gs = (g % 2) * 4096
                                sc = 64 + (g % 2) * 16
                                e.wait_ge(s_x, bX0 + 64 * (g + 1))
                                for i in range(4):
                                    t = t0 + i
                                    ins = e.activation(
                                        xnf32[:],
                                        xstage[:, gs + i*1024:gs + (i+1)*1024],
                                        AF.Square,
                                        accum_out=stats[:, 16+t:17+t])
                                    if i == 3:
                                        INC(ins, "sH")
                                e.wait_ge(sG, bG0 + 2 * g + 1)
                                INC(e.activation(stats[:, sc+8:sc+12],
                                                 stats[:, sc+4:sc+8],
                                                 AF.Sqrt,
                                                 bias=tiny[:, 15:16]), "sH")
                                e.wait_ge(sG, bG0 + 2 * g + 2)
                                for i in range(4):
                                    t = t0 + i
                                    INC(e.activation(
                                        hstage[:, gs + i*1024:gs + (i+1)*1024],
                                        xstage[:, gs + i*1024:gs + (i+1)*1024],
                                        AF.Identity,
                                        bias=stats[:, 48+t:49+t],
                                        scale=stats[:, 32+t:33+t]), "sB")
                                for i in range(4):
                                    t = t0 + i
                                    gi4 = g * 4 + i
                                    ps = pa if gi4 % 2 == 0 else pb
                                    e.wait_ge(sE, bE0 + gi4 + 1)
                                    INC(e.copy(xnfm.ap().rearrange(
                                        "p (c n) -> p c n",
                                        c=NC)[:, :, t*128:(t+1)*128],
                                        ps.ap()[:, 0:1024].rearrange(
                                            "p (c n) -> p c n", c=NC)), "sF")
                                    INC(e.copy(xnlo.ap().rearrange(
                                        "p (c n) -> p c n",
                                        c=NC)[:, :, t*128:(t+1)*128],
                                        ps.ap()[:, 1024:2048].rearrange(
                                            "p (c n) -> p c n", c=NC)), "sF")

                        @blk.vector
                        def _(e):
                            for g in range(4):
                                t0 = g * 4
                                gs = (g % 2) * 4096
                                sc = 64 + (g % 2) * 16
                                m4 = stats[:, sc:sc+4]
                                v4 = stats[:, sc+4:sc+8]
                                w4 = stats[:, sc+8:sc+12]
                                u4 = stats[:, sc+12:sc+16]
                                r4 = stats[:, 32+t0:36+t0]
                                e.wait_ge(s_x, bX0 + 64 * (g + 1))
                                for i in range(4):
                                    t = t0 + i
                                    e.reduce_sum(
                                        stats[:, t:t+1],
                                        xstage[:, gs + i*1024:gs + (i+1)*1024],
                                        axis=AX.X)
                                e.wait_ge(sH, bH0 + 2 * g + 1)
                                e.tensor_scalar(m4, stats[:, t0:t0+4],
                                                1.0 / D, None, ALU.mult)
                                e.tensor_scalar(v4, stats[:, 16+t0:20+t0],
                                                1.0 / D, None, ALU.mult)
                                e.drain()
                                e.tensor_mul(w4, m4, m4)
                                e.drain()
                                INC(e.tensor_sub(v4, v4, w4), "sG")
                                e.drain()
                                e.tensor_scalar(m4, m4, -1.0, None, ALU.mult)
                                e.wait_ge(sH, bH0 + 2 * g + 2)
                                e.reciprocal(r4, w4)
                                e.drain()
                                INC(e.tensor_mul(stats[:, 48+t0:52+t0],
                                                 m4, r4), "sG")
                                for i in range(4):
                                    t = t0 + i
                                    e.tensor_scalar(
                                        xnf32[:],
                                        xstage[:, gs + i*1024:gs + (i+1)*1024],
                                        stats[:, 32+t:33+t],
                                        stats[:, 48+t:49+t],
                                        ALU.mult, ALU.add)
                                    e.wait_ge(sB, bB0 + 4 * g + i + 1)
                                    INC(e.tensor_sub(
                                        lstage[:, gs + i*1024:gs + (i+1)*1024],
                                        xnf32[:],
                                        hstage[:, gs + i*1024:
                                               gs + (i+1)*1024]), "sC")

                        @blk.tensor
                        def _(e):
                            for g in range(4):
                                t0 = g * 4
                                gs = (g % 2) * 4096
                                for i in range(4):
                                    gi4 = g * 4 + i
                                    ps = pa if gi4 % 2 == 0 else pb
                                    e.wait_ge(sB, bB0 + 4 * g + i + 1)
                                    e.wait_ge(sC, bC0 + 4 * g + i + 1)
                                    if gi4 >= 2:
                                        e.wait_ge(sF, bF0 + 2 * (gi4 - 1))
                                    last = None
                                    for half, stg in ((0, hstage),
                                                      (1, lstage)):
                                        for dc in range(NC):
                                            last = e.matmul(
                                                ps[:, half*1024 + dc*128:
                                                   half*1024 + (dc+1)*128],
                                                stg[:, gs + i*1024 + dc*128:
                                                    gs + i*1024 + (dc+1)*128],
                                                ident16[:],
                                                start=True, stop=True)
                                    INC(last, "sE")

                    bE, bF = cnt["sE"], cnt["sF"]
                    with nc.Block(no_gpsimd_drain=True) as blk:
                        @blk.tensor
                        def _(e, bE=bE, bF=bF):
                            for f in range(3):
                                if f > 0:
                                    e.wait_ge(sF, bF + f)
                                last = None
                                for ps, (wt, xt) in enumerate(
                                        ((nodewh, xnfm), (nodewl, xnfm),
                                         (nodewh, xnlo))):
                                    for dc in range(NC):
                                        st = (ps == 0 and dc == 0)
                                        sp = (ps == 2 and dc == NC - 1)
                                        for gq in range(4):
                                            last = e.matmul(
                                                pa[0:NNODE, gq*512:(gq+1)*512],
                                                wt[:, dc*45 + f*15:
                                                   dc*45 + (f+1)*15],
                                                xt[:, dc*2048 + gq*512:
                                                   dc*2048 + (gq+1)*512],
                                                start=st, stop=sp)
                                INC(last, "sE")

                        @blk.vector
                        def _(e, bE=bE, bF=bF):
                            for f in range(3):
                                e.wait_ge(sE, bE + f + 1)
                                INC(e.tensor_scalar(
                                    rsb[:, f*S:(f+1)*S], pa[0:NNODE, :],
                                    nodeb[:, f:f+1], 0.0,
                                    ALU.add, ALU.is_ge), "sF")

                    with nc.Block(no_gpsimd_drain=True) as blk:
                        @blk.gpsimd
                        def _(e):
                            wload(e, rsb_hbm, rsb[:])
                            if debug:
                                ostore(e, d_xnfm, xnfm[:])
                                ostore(e, d_xnlo, xnlo[:])
                                ostore(e, d_rsb, rsb[:])
                            e.wait_ge(s_w, cnt["s_w"])

                with (
                    nc.sbuf_tensor("qsb", [128, NP * 2048], BF16,
                                   side="left") as qsb,
                    nc.sbuf_tensor("ksb", [128, NP * 2048], BF16,
                                   side="left") as ksb,
                    nc.sbuf_tensor("vsb", [128, NT * 1040], BF16,
                                   side="left") as vsb,
                ):
                    bA, bB, bC, bD = cnt["sA"], cnt["sB"], cnt["sC"], cnt["sD"]
                    sw_q = cnt["s_w"]
                    NU = 32
                    SLOT = (pa.ap()[:, 0:1024], pa.ap()[:, 1024:2048],
                            pb.ap()[:, 0:1024])
                    ROT = pb.ap()[:, 1024:2048]
                    swk = [0] * NP
                    with nc.Block(no_gpsimd_drain=True) as blk:
                        @blk.gpsimd
                        def _(e, bA=bA, sw_q=sw_q):
                            wk3 = hin["w_k"].rearrange(
                                "p (d n) -> p d n", d=NC)
                            wq3 = wqh.ap().rearrange(
                                "p (d n) -> p d n", d=NC)
                            for pcc in range(NP):
                                if pcc >= 1:
                                    e.wait_ge(s_w, sw_q + 16 * pcc)
                                e.wait_ge(sA, bA + 2 * pcc + 2)
                                INC(e.dma_start(
                                    out=wq3[:, :, pcc*128:(pcc+1)*128],
                                    in_=wk3[:, :, pcc*128:(pcc+1)*128]),
                                    "s_w", 16)
                                swk[pcc] = cnt["s_w"]

                        @blk.tensor
                        def _(e, bA=bA, bB=bB, bC=bC, sw_q=sw_q):
                            e.wait_ge(s_w, sw_q)

                            def emit_mm(u):
                                if u >= 16 and u % 2 == 0:
                                    e.wait_ge(s_w, swk[(u - 16) // 2])
                                cc, h = u // 2, u % 2
                                pcc = cc % 8
                                sp = SLOT[u % 3]
                                if u >= 3:
                                    e.wait_ge(sB, bB + u - 2)
                                last = None
                                for dc in range(NC):
                                    for g2 in range(2):
                                        last = e.matmul(
                                            sp[:, g2*512:(g2+1)*512],
                                            wqh[:, dc*1024 + pcc*128:
                                                dc*1024 + (pcc+1)*128],
                                            xnfm[:, dc*2048 + h*1024
                                                 + g2*512:
                                                 dc*2048 + h*1024
                                                 + (g2+1)*512],
                                            start=(dc == 0), stop=(dc == NC-1))
                                INC(last, "sA")

                            def emit_rot(u):
                                cc, h = u // 2, u % 2
                                dstsb = qsb if cc < 8 else ksb
                                pcc = cc % 8
                                base = pcc*2048 + h*1024
                                e.wait_ge(sB, bB + u + 1)
                                if u >= 1:
                                    e.wait_ge(sC, bC + 2 * u)
                                last = None
                                for j in range(2):
                                    last = e.matmul(
                                        ROT[:, j*512:(j+1)*512], prope[:],
                                        dstsb[:, base + j*512:
                                              base + (j+1)*512],
                                        start=True, stop=True)
                                INC(last, "sD")

                            for u in range(NU + 1):
                                if u < NU:
                                    emit_mm(u)
                                if u >= 1:
                                    emit_rot(u - 1)

                        @blk.scalar
                        def _(e, bA=bA):
                            for u in range(NU):
                                cc, h = u // 2, u % 2
                                dstsb = qsb if cc < 8 else ksb
                                pcc = cc % 8
                                base = pcc*2048 + h*1024
                                e.wait_ge(sA, bA + u + 1)
                                INC(e.copy(dstsb[:, base:base + 1024],
                                           SLOT[u % 3]), "sB")

                        @blk.vector
                        def _(e, bD=bD):
                            for u in range(NU):
                                cc, h = u // 2, u % 2
                                dstsb = qsb if cc < 8 else ksb
                                pcc = cc % 8
                                sl = dstsb[:, pcc*2048 + h*1024:
                                           pcc*2048 + (h+1)*1024]
                                e.wait_ge(sD, bD + u + 1)
                                INC(e.tensor_mul(
                                    ROT[:, :], ROT[:, :],
                                    sinsb[:, h*1024:(h+1)*1024]), "sC")
                                e.tensor_mul(sl, sl,
                                             cossb[:, h*1024:(h+1)*1024])
                                INC(e.tensor_add(sl, sl, ROT[:, :]), "sC")

                    with nc.sbuf_tensor("sqsb", [128, 2 * 2048], BF16,
                                        side="right") as sqsb:
                        bA, bB, bC, bD, bE = (cnt["sA"], cnt["sB"], cnt["sC"],
                                              cnt["sD"], cnt["sE"])
                        with nc.Block(no_gpsimd_drain=True) as blk:
                            @blk.vector
                            def _(e, bB=bB):
                                for tt in range(NT):
                                    vv = vsb.ap()[:, tt*1040:(tt+1)*1040]
                                    vv3 = vv.rearrange("p (h z) -> p h z", h=16)
                                    e.memset(vv3[:, :, 64:65], 1.0)
                                for cc in range(16):
                                    srcsb = qsb if cc < 8 else ksb
                                    pcc = cc % 8
                                    sl = srcsb[:, pcc*2048:(pcc+1)*2048]
                                    if cc >= 2:
                                        e.wait_ge(sB, bB + cc - 1)
                                    INC(e.tensor_mul(
                                        sqsb[:, (cc % 2)*2048:
                                             (cc % 2 + 1)*2048], sl, sl), "sA")
                                    if cc == 9:
                                        e.wait_ge(sB, bB + 8)
                                        INC(e.reduce_max(tiny[0:16, 0:1],
                                                         pb[0:16, :],
                                                         axis=AX.X), "sC")
                                e.wait_ge(sB, bB + 16)
                                INC(e.reduce_max(tiny[0:16, 1:2], pb[0:16, :],
                                                 axis=AX.X), "sC")

                            @blk.tensor
                            def _(e, bA=bA, bC=bC, bE=bE):
                                for tt in range(NT):
                                    half = (tt % 2) * 1024
                                    if tt >= 2:
                                        e.wait_ge(sE, bE + tt - 1)
                                    last = None
                                    for dc in range(NC):
                                        for gq in range(2):
                                            last = e.matmul(
                                                pa[:, half + gq*512:
                                                   half + (gq+1)*512],
                                                xnfm[:, dc*2048 + tt*128:
                                                     dc*2048 + (tt+1)*128],
                                                wv[:, dc*1024 + gq*512:
                                                   dc*1024 + (gq+1)*512],
                                                start=(dc == 0),
                                                stop=(dc == NC-1))
                                    INC(last, "sF")
                                    cc = tt
                                    if cc == 8:
                                        e.wait_ge(sC, bC + 1)
                                    e.wait_ge(sA, bA + cc + 1)
                                    last = None
                                    for gq in range(4):
                                        last = e.matmul(
                                            pb[0:16, gq*512:(gq+1)*512],
                                            enorm[:, (cc % 8)*16:
                                                  (cc % 8 + 1)*16],
                                            sqsb[:, (cc % 2)*2048 + gq*512:
                                                 (cc % 2)*2048 + (gq+1)*512],
                                            start=(cc % 8 == 0),
                                            stop=(cc % 8 == 7))
                                    INC(last, "sB")

                            @blk.scalar
                            def _(e):
                                bF = cnt["sF"] - NT
                                for tt in range(NT):
                                    half = (tt % 2) * 1024
                                    e.wait_ge(sF, bF + tt + 1)
                                    src = pa.ap()[:, half:half+1024].rearrange(
                                        "p (h z) -> p h z", h=16)
                                    vv = vsb.ap()[:, tt*1040:(tt+1)*1040]
                                    dst = vv.rearrange(
                                        "p (h z) -> p h z", h=16)[:, :, 0:64]
                                    INC(e.copy(dst, src), "sE")

                    g1A, g1B, g1C = cnt["sA"], cnt["sB"], cnt["sC"]
                    with nc.Block(no_gpsimd_drain=True) as blk:
                        @blk.scalar
                        def _(e, g1A=g1A, g1B=g1B, g1C=g1C):
                            e.activation(tiny[0:16, 2:3], tiny[0:16, 0:1],
                                         AF.Sqrt)
                            INC(e.activation(tiny[0:16, 3:4], tiny[0:16, 1:2],
                                             AF.Sqrt), "sA")
                            e.wait_ge(sB, g1B + 1)
                            INC(e.copy(tinyr[0:16, 0:1], tiny[0:16, 4:5]),
                                "sA")
                            e.wait_ge(sB, g1B + 2)
                            INC(e.copy(tinyr[0:1, 1:2], tiny[0:1, 5:6]), "sA")
                            e.wait_ge(sC, g1C + 2)
                            e.activation(biasg[:], pa[:, 16:17], AF.Copy,
                                         scale=-1.0 / 8.0)

                        @blk.vector
                        def _(e, g1A=g1A, g1B=g1B, g1C=g1C):
                            e.wait_ge(sA, g1A + 1)
                            INC(e.tensor_mul(tiny[0:16, 4:5], tiny[0:16, 2:3],
                                             tiny[0:16, 3:4]), "sB")
                            e.wait_ge(sC, g1C + 1)
                            INC(e.reduce_max(tiny[0:1, 5:6], pa[0:1, 0:16],
                                             axis=AX.X), "sB")

                        @blk.tensor
                        def _(e, g1A=g1A, g1C=g1C):
                            e.wait_ge(sA, g1A + 2)
                            INC(e.matmul(pa[0:1, 0:16], tinyr[0:16, 0:1],
                                         ident16[0:16, 0:16], start=True,
                                         stop=True), "sC")
                            e.wait_ge(sA, g1A + 3)
                            INC(e.matmul(pa[:, 16:17], onesrow[:],
                                         tinyr[0:1, 1:2],
                                         start=True, stop=True), "sC")

                    if debug:
                        with nc.Block(no_gpsimd_drain=True) as blk:
                            @blk.gpsimd
                            def _(e):
                                ostore(e, d_q, qsb[:])
                                ostore(e, d_k, ksb[:])
                                ostore(e, d_v, vsb[:])
                                ostore(e, d_biasg, biasg[:])

                    stW.close()
                    with (
                        nc.sbuf_tensor("psb", [128, 8 * 1024], BF16,
                                       side="right") as psb,
                        nc.sbuf_tensor("mask01sb", [128, 4096], BF16,
                                       side="right") as mask01,
                        nc.sbuf_tensor("stgA", [64, 2 * S], BF16,
                                       side="right") as stgA,
                        nc.sbuf_tensor("stgB", [64, 2 * S], BF16,
                                       side="right") as stgB,
                        nc.sbuf_tensor("stgS", [128, 2048], F32,
                                       side="right") as stgS,
                    ):
                        with nc.Block(no_gpsimd_drain=True) as blk:
                            @blk.gpsimd
                            def _(e):
                                wload(e, mask01[:], hin["mask01"])
                                e.wait_ge(s_w, cnt["s_w"])

                        ndiag = []
                        nd = 0
                        for (qc, kt) in STRIPS:
                            ndiag.append(nd)
                            if kt >= 4 * qc:
                                nd += 1
                        nsp = len(STRIPS)

                        bA, bB, bC, bD, bE = (cnt["sA"], cnt["sB"], cnt["sC"],
                                              cnt["sD"], cnt["sE"])
                        with nc.Block(no_gpsimd_drain=True) as blk:
                            @blk.tensor
                            def _(e, bA=bA, bB=bB, bC=bC, bD=bD, bE=bE):
                                allst = [(p, qc, kt) for p in range(NP)
                                         for (qc, kt) in STRIPS]

                                def emit_st(gi):
                                    p, qc, kt = allst[gi]
                                    buf = gi % 2
                                    sp = pa[:, buf*1024:(buf+1)*1024]
                                    if gi >= 2:
                                        e.wait_ge(sB, bB + gi - 1)
                                    mA = e.matmul(
                                        sp[:, 0:512],
                                        ksb[0:64, p*2048 + kt*128:
                                            p*2048 + (kt+1)*128],
                                        qsb[0:64, p*2048 + qc*512:
                                            p*2048 + (qc+1)*512],
                                        start=True, stop=True)
                                    mB = e.matmul(
                                        sp[:, 512:1024],
                                        ksb[64:128, p*2048 + kt*128:
                                            p*2048 + (kt+1)*128],
                                        qsb[64:128, p*2048 + qc*512:
                                            p*2048 + (qc+1)*512],
                                        start=True, stop=True)
                                    INC(mB, "sA")

                                def emit_pv(gi):
                                    p, qc, kt = allst[gi]
                                    si = gi - p * nsp
                                    nkt = 4 * qc + 4
                                    buf = gi % 8
                                    m = p * 4 + qc
                                    acc = (m % 2) * 1024
                                    if kt == 0 and m >= 2:
                                        e.wait_ge(sD, bD + m - 1)
                                    diag = kt >= 4 * qc
                                    if diag:
                                        e.wait_ge(sC, bC + p * 16
                                                  + ndiag[si] + 1)
                                    else:
                                        e.wait_ge(sB, bB + gi + 1)
                                    pvA = e.matmul(
                                        pb[0:65, acc:acc + 512],
                                        vsb[:, kt*1040 + (2*p)*65:
                                            kt*1040 + (2*p)*65 + 65],
                                        psb[:, buf*1024:buf*1024 + 512],
                                        start=(kt == 0), stop=(kt == nkt - 1))
                                    pvB = e.matmul(
                                        pb[0:65, acc + 512:acc + 1024],
                                        vsb[:, kt*1040 + (2*p+1)*65:
                                            kt*1040 + (2*p+1)*65 + 65],
                                        psb[:, buf*1024 + 512:
                                            buf*1024 + 1024],
                                        start=(kt == 0), stop=(kt == nkt - 1))
                                    INC(pvB, "sE")

                                n = len(allst)
                                for gi in range(n + 2):
                                    if gi < n:
                                        emit_st(gi)
                                    if gi >= 2:
                                        emit_pv(gi - 2)

                            @blk.scalar
                            def _(e, bA=bA, bE=bE):
                                for p in range(NP):
                                    for si, (qc, kt) in enumerate(STRIPS):
                                        gi = p * nsp + si
                                        pbuf = gi % 2
                                        buf = gi % 8
                                        sp = pa[:, pbuf*1024:(pbuf+1)*1024]
                                        e.wait_ge(sA, bA + gi + 1)
                                        if gi >= 8:
                                            e.wait_ge(sE, bE + gi - 7)
                                        INC(e.activation(
                                            psb[:, buf*1024:(buf+1)*1024],
                                            sp,
                                            AF.Exp, bias=biasg[:, 0:1],
                                            scale=1.0 / 8.0), "sB")

                            @blk.vector
                            def _(e, bE=bE, bH=cnt["s_st"]):
                                bB = cnt["sB"] - NP * nsp
                                last_gi = {}
                                for si, (qc, kt) in enumerate(STRIPS):
                                    if kt == 4 * qc + 3:
                                        last_gi[qc] = si
                                for p in range(NP):
                                    if p >= 2:
                                        e.wait_ge(s_st, bH + 160 * (p - 1))
                                    for si, (qc, kt) in enumerate(STRIPS):
                                        gi = p * nsp + si
                                        buf = gi % 8
                                        if kt >= 4 * qc:
                                            r = kt - 4 * qc
                                            e.wait_ge(sB, bB + gi + 1)
                                            INC(e.tensor_mul(
                                                psb[:, buf*1024:(buf+1)*1024],
                                                psb[:, buf*1024:(buf+1)*1024],
                                                mask01[:, r*1024:(r+1)*1024]),
                                                "sC")
                                        if kt == 4 * qc + 3:
                                            m = p * 4 + qc
                                            acc = (m % 2) * 1024
                                            if m >= 2:
                                                p2, qc2 = ((m - 2) // 4,
                                                           (m - 2) % 4)
                                                e.wait_ge(s_st, bH + 160 * p2
                                                          + 32 * (qc2 + 1))
                                            e.wait_ge(sE, bE + p * nsp
                                                      + last_gi[qc] + 1)
                                            cA = (qc % 2) * 1024
                                            cB = cA + 512
                                            sa = (p % 2) * 2048
                                            e.tensor_copy(
                                                stgA[:, sa + qc*512:
                                                     sa + (qc+1)*512],
                                                pb[0:64, acc:acc+512])
                                            e.tensor_copy(
                                                stgS[64:65, cA:cA+512],
                                                pb[64:65, acc:acc+512])
                                            e.tensor_copy(
                                                stgB[:, sa + qc*512:
                                                     sa + (qc+1)*512],
                                                pb[0:64, acc+512:acc+1024])
                                            INC(e.tensor_copy(
                                                stgS[64:65, cB:cB+512],
                                                pb[64:65, acc+512:acc+1024]),
                                                "sD")

                            @blk.sync
                            def _(e):
                                bD = cnt["sD"] - NP * 4
                                for p in range(NP):
                                    for qc in range(4):
                                        e.wait_ge(sD, bD + p * 4 + qc + 1)
                                        cA = (qc % 2) * 1024
                                        cB = cA + 512
                                        INC(e.dma_start(
                                            out=sums_hbm[0, p,
                                                         qc*512:(qc+1)*512],
                                            in_=stgS[64:65, cA:cA+512]),
                                            "s_st", 16)
                                        INC(e.dma_start(
                                            out=sums_hbm[1, p,
                                                         qc*512:(qc+1)*512],
                                            in_=stgS[64:65, cB:cB+512]),
                                            "s_st", 16)
                                    sa = (p % 2) * 2048
                                    INC(e.dma_start(
                                        out=arawA_hbm[:, p*S:(p+1)*S],
                                        in_=stgA[:, sa:sa+2048]), "s_st", 16)
                                    INC(e.dma_start(
                                        out=arawB_hbm[:, p*S:(p+1)*S],
                                        in_=stgB[:, sa:sa+2048]), "s_st", 16)
                                e.wait_ge(s_st, cnt["s_st"])

            q2sb = _top.enter_context(nc.sbuf_tensor(
                "q2sb", [128, NC * 2048], BF16, side="right"))
            with (
                nc.sbuf_tensor("w1a", [128, 8192], BF16, side="right") as w1a,
                nc.sbuf_tensor("w2a", [128, 8192], BF16, side="right") as w2a,
                nc.sbuf_tensor("b2r", [16, 2 * 1024], F32R, side="right") as b2r,
                nc.sbuf_tensor("b1sb", [128, 24], F32, side="right") as b1sb,
                nc.sbuf_tensor("rsb2", [NNODE, 3 * S], BF16,
                               side="right") as rsb2,
            ):
                sw_after_rsb = [0]
                with nc.Block(no_gpsimd_drain=True) as blk:
                    @blk.gpsimd
                    def _(e):
                        wload(e, w1a[:], hin["w1_q"])
                        wload(e, w2a[:], hin["w2_q"])
                        wload(e, b2r[:, 0:1024], hin["b2_q"])
                        wload(e, b2r[:, 1024:2048], hin["b2_k"])
                        for i, f in enumerate("qkv"):
                            wload(e, b1sb[:, i*8:(i+1)*8], hin[f"b1_{f}"])
                        wload(e, rsb2[:], rsb_hbm)
                        sw_after_rsb[0] = cnt["s_w"]

                with (
                    nc.sbuf_tensor("asb", [128, NP * 2048], BF16,
                                   side="right") as asb,
                    nc.sbuf_tensor("wo", [128, NC * 1024], BF16,
                                   side="right") as wo,
                    nc.sbuf_tensor("rsrc", [16, S], F32, side="right") as rsrc,
                    nc.sbuf_tensor("rdst", [16, S], F32R, side="right") as rdst,
                    nc.sbuf_tensor("ostage", [128, 2 * 1024], F32,
                                   side="right") as ostage,
                    nc.sbuf_tensor("ebcsb", [16, 8 * 128], F32R,
                                   side="right") as ebc,
                ):
                    bXs = cnt["s_x"]
                    swB = [0]
                    swE = [0]
                    strA = [0]
                    with nc.Block(no_gpsimd_drain=True) as blk:
                        @blk.gpsimd
                        def _(e, bSt=cnt["s_st"]):
                            wload(e, wo[:], hin["w_o"])
                            wload(e, ebc[:], hin["ebc"])
                            swE[0] = cnt["s_w"]
                            e.wait_ge(s_st, bSt)
                            INC(e.dma_start(
                                out=asb.ap()[64:128, :],
                                in_=arawB_hbm), "s_w", 16)
                            swB[0] = cnt["s_w"]

                        @blk.sync
                        def _(e):
                            INC(e.dma_start(
                                out=rsrc[:],
                                in_=sums_hbm.rearrange("h p c -> (h p) c")),
                                "s_x", 16)
                            INC(e.dma_start(
                                out=asb.ap()[0:32, :],
                                in_=arawA_hbm[0:32, :]), "s_x", 16)

                        @blk.scalar
                        def _(e, bSt=cnt["s_st"]):
                            e.wait_ge(s_st, bSt)
                            INC(e.dma_start(
                                out=asb.ap()[32:64, :],
                                in_=arawA_hbm[32:64, :]), "s_tr", 16)
                            strA[0] = cnt["s_tr"]

                        @blk.vector
                        def _(e, bXs=bXs):
                            e.wait_ge(s_x, bXs + 16)
                            with nc.allow_low_precision(
                                    reason="float32r out is 4-byte fp32"):
                                e.reciprocal(rdst[:], rsrc[:])

                    bB, bC, bD = cnt["sB"], cnt["sC"], cnt["sD"]
                    with nc.Block(no_gpsimd_drain=True) as blk:
                        @blk.tensor
                        def _(e, bD=bD, swE=swE[0]):
                            for p in range(NP):
                                ps = pa if p % 2 == 0 else pb
                                if p == 0:
                                    e.wait_ge(s_w, swE)
                                if p >= 2:
                                    e.wait_ge(sD, bD + p - 1)
                                last = None
                                for gq in range(4):
                                    last = e.matmul(
                                        ps[:, gq*512:(gq+1)*512],
                                        ebc[:, p*128:(p+1)*128],
                                        rdst[0:16, gq*512:(gq+1)*512],
                                        start=True, stop=True)
                                INC(last, "sC")

                        @blk.vector
                        def _(e, bC=bC, bXs=bXs, swB=swB[0]):
                            for p in range(NP):
                                ps = pa if p % 2 == 0 else pb
                                if p == 0:
                                    e.wait_ge(s_x, bXs + 32)
                                    e.wait_ge(s_tr, strA[0])
                                    e.wait_ge(s_w, swB)
                                e.wait_ge(sC, bC + p + 1)
                                INC(e.tensor_mul(
                                    asb[:, p*2048:(p+1)*2048],
                                    asb[:, p*2048:(p+1)*2048], ps[:, :]), "sD")

                    if debug:
                        with nc.Block(no_gpsimd_drain=True) as blk:
                            @blk.gpsimd
                            def _(e):
                                ostore(e, d_anorm, asb[:])

                    bE, bF, bG = cnt["sE"], cnt["sF"], cnt["sG"]
                    sw_wo = cnt["s_w"]
                    with nc.Block(no_gpsimd_drain=True) as blk:
                        @blk.tensor
                        def _(e, bE=bE, bG=bG, sw_wo=sw_wo):
                            e.wait_ge(s_w, sw_wo)
                            for qt in range(NT):
                                half = (qt % 2) * 1024
                                if qt >= 2:
                                    e.wait_ge(sE, bE + qt - 1)
                                last = None
                                for p in range(NP):
                                    for gq in range(2):
                                        last = e.matmul(
                                            pb[:, half + gq*512:
                                               half + (gq+1)*512],
                                            asb[:, p*2048 + qt*128:
                                                p*2048 + (qt+1)*128],
                                            wo[:, p*1024 + gq*512:
                                               p*1024 + (gq+1)*512],
                                            start=(p == 0), stop=(p == NP-1))
                                INC(last, "sF")

                        @blk.scalar
                        def _(e, bF=bF, bG=cnt["s_st"]):
                            for qt in range(NT):
                                half = (qt % 2) * 1024
                                e.wait_ge(sF, bF + qt + 1)
                                if qt >= 2:
                                    e.wait_ge(s_st, bG + 16 * (qt - 1))
                                INC(e.copy(
                                    ostage[:, half:half + 1024],
                                    pb[:, half:half + 1024]), "sE")

                        @blk.sync
                        def _(e, bG=cnt["s_st"]):
                            bE2 = cnt["sE"] - NT
                            for qt in range(NT):
                                half = (qt % 2) * 1024
                                e.wait_ge(sE, bE2 + qt + 1)
                                if qt > 0:
                                    e.wait_ge(s_st, bG + 16 * qt)
                                INC(e.dma_start(
                                    out=attn_hbm[qt*128:(qt+1)*128, :],
                                    in_=ostage[:, half:half + 1024]), "s_st",
                                    16)

                with (
                    nc.sbuf_tensor("w1b", [128, 8192], BF16,
                                   side="right") as w1b,
                    nc.sbuf_tensor("w2b", [128, 8192], BF16,
                                   side="right") as w2b,
                    nc.sbuf_tensor("hid", [128, NP * 2048], BF16,
                                   side="right") as hid,
                    nc.sbuf_tensor("self32r", [16, S], F32R,
                                   side="right") as self32r,
                    nc.sbuf_tensor("maskm1", [16, S], BF16,
                                   side="right") as maskm1,
                    nc.sbuf_tensor("fstage", [128, 2 * 2048], BF16,
                                   side="right") as fstage,
                ):
                    with nc.Block(no_gpsimd_drain=True) as blk:
                        @blk.gpsimd
                        def _(e):
                            wload(e, w1b[:], hin["w1_k"])
                            wload(e, w2b[:], hin["w2_k"])

                    for fi, f in enumerate("qkv"):
                        w1x = (w1a, w1b, w1a)[fi]
                        w2x = (w2a, w2b, w2a)[fi]
                        dsth = (q2_hbm, k2_hbm, v2_hbm)[fi]

                        bA, bB, bC, bD, bE, bG = (
                            cnt["sA"], cnt["sB"], cnt["sC"], cnt["sD"],
                            cnt["sE"], cnt["s_st"])
                        sw_need = sw_after_rsb[0] if fi == 0 else cnt["s_w"]
                        with nc.Block(no_gpsimd_drain=True) as blk:
                            @blk.tensor
                            def _(e, fi=fi, w1x=w1x, w2x=w2x, bA=bA, bB=bB,
                                  bC=bC, bD=bD, bE=bE, bG=bG,
                                  sw_need=sw_need):
                                e.wait_ge(s_w, sw_need)
                                last = None
                                for gq in range(4):
                                    last = e.matmul(
                                        pb[0:16, gq*512:(gq+1)*512], aroute[:],
                                        rsb2[:, fi*S + gq*512:
                                             fi*S + (gq+1)*512],
                                        start=True, stop=True)
                                INC(last, "sA")
                                for p in range(NP):
                                    ps = pa if p % 2 == 0 else pb
                                    if p == 1:
                                        e.wait_ge(sB, bB + 1)
                                    if p >= 2:
                                        e.wait_ge(sC, bC + p - 1)
                                    for dc in range(NC):
                                        for gq in range(4):
                                            e.matmul(
                                                ps[:, gq*512:(gq+1)*512],
                                                w1x[:, dc*1024 + p*128:
                                                    dc*1024 + (p+1)*128],
                                                xnfm[:, dc*2048 + gq*512:
                                                     dc*2048 + (gq+1)*512],
                                                start=(dc == 0), stop=False)
                                    e.wait_ge(sB, bB + 1)
                                    last = None
                                    for gq in range(4):
                                        last = e.matmul(
                                            ps[:, gq*512:(gq+1)*512],
                                            bige[:, p*128:(p+1)*128],
                                            maskm1[0:16, gq*512:(gq+1)*512],
                                            start=False, stop=True)
                                    INC(last, "sD")
                                if fi < 2:
                                    for oc in range(NC):
                                        ps = pa if oc % 2 == 0 else pb
                                        e.wait_ge(sC, bC + NP)
                                        if oc >= 2:
                                            e.wait_ge(sE, bE + oc - 1)
                                        for p in range(NP):
                                            for gq in range(4):
                                                e.matmul(
                                                    ps[:, gq*512:(gq+1)*512],
                                                    w2x[:, p*1024 + oc*128:
                                                        p*1024 + (oc+1)*128],
                                                    hid[:, p*2048 + gq*512:
                                                        p*2048 + (gq+1)*512],
                                                    start=(p == 0), stop=False)
                                        last = None
                                        for gq in range(4):
                                            last = e.matmul(
                                                ps[:, gq*512:(gq+1)*512],
                                                b2r[0:16,
                                                    (fi % 2)*1024 + oc*128:
                                                    (fi % 2)*1024
                                                    + (oc+1)*128],
                                                self32r[0:16,
                                                        gq*512:(gq+1)*512],
                                                start=False, stop=True)
                                        INC(last, "sF")
                                else:
                                    for tt in range(NT):
                                        ps = pa if tt % 2 == 0 else pb
                                        half = 0
                                        e.wait_ge(sC, bC + NP)
                                        if tt >= 2:
                                            e.wait_ge(sE, bE + tt - 1)
                                        for p in range(NP):
                                            for gq in range(2):
                                                e.matmul(
                                                    ps[:, gq*512:(gq+1)*512],
                                                    hid[:, p*2048 + tt*128:
                                                        p*2048 + (tt+1)*128],
                                                    w2x[:, p*1024 + gq*512:
                                                        p*1024 + (gq+1)*512],
                                                    start=(p == 0), stop=False)
                                        last = None
                                        for gq in range(2):
                                            last = e.matmul(
                                                ps[:, gq*512:(gq+1)*512],
                                                self32r[0:16,
                                                        tt*128:(tt+1)*128],
                                                b2r[0:16,
                                                    (fi % 2)*1024 + gq*512:
                                                    (fi % 2)*1024
                                                    + (gq+1)*512],
                                                start=False, stop=True)
                                        INC(last, "sF")

                            @blk.vector
                            def _(e, fi=fi, bA=bA):
                                e.wait_ge(sA, bA + 1)
                                e.tensor_scalar(
                                    self32r[0:16, :], pb[0:16, :],
                                    thr[:, 0:1], None, ALU.is_ge)
                                INC(e.tensor_scalar(
                                    maskm1[0:16, :], pb[0:16, :],
                                    thr[:, 0:1], -1.0, ALU.is_ge, ALU.add),
                                    "sB")

                            @blk.scalar
                            def _(e, fi=fi, bD=bD):
                                for p in range(NP):
                                    ps = pa if p % 2 == 0 else pb
                                    e.wait_ge(sD, bD + p + 1)
                                    INC(e.activation(
                                        hid[:, p*2048:(p+1)*2048], ps[:, :],
                                        AF.Relu,
                                        bias=b1sb[:, fi*8+p:fi*8+p+1]), "sC")
                                nsub = NC if fi < 2 else NT
                                w = 2048 if fi < 2 else 1024
                                bF = cnt["sF"] - nsub
                                for u in range(nsub):
                                    ps = pa if u % 2 == 0 else pb
                                    e.wait_ge(sF, bF + u + 1)
                                    if fi == 0:
                                        INC(e.copy(
                                            q2sb[:, u*2048:(u+1)*2048],
                                            ps[:, 0:w]), "sE")
                                        continue
                                    if u >= 2:
                                        e.wait_ge(s_st, bG + 16 * (u - 1))
                                    INC(e.copy(
                                        fstage[:, (u % 2)*2048:
                                               (u % 2)*2048 + w],
                                        ps[:, 0:w]), "sE")

                            @blk.sync
                            def _(e, fi=fi, dsth=dsth, bG=bG):
                                if fi == 0:
                                    return
                                nsub = NC if fi < 2 else NT
                                w = 2048 if fi < 2 else 1024
                                bE2 = cnt["sE"] - nsub
                                for u in range(nsub):
                                    e.wait_ge(sE, bE2 + u + 1)
                                    if u > 0:
                                        e.wait_ge(s_st, bG + 16 * u)
                                    INC(e.dma_start(
                                        out=dsth[:, u*w:(u+1)*w],
                                        in_=fstage[:, (u % 2)*2048:
                                                   (u % 2)*2048 + w]),
                                        "s_st", 16)

                            if fi == 1:
                                @blk.gpsimd
                                def _(e):
                                    wload(e, w1a[:], hin["w1_v"])
                                    wload(e, w2a[:], hin["w2_v"])
                                    wload(e, b2r[:, 0:1024], hin["b2_v"])

                        if debug and f == "q":
                            with nc.Block(no_gpsimd_drain=True) as blk:
                                @blk.gpsimd
                                def _(e, fi=fi):
                                    ostore(e, d_hid, hid[:])
                                    ostore(e, d_sel.rearrange(
                                        "p (f s) -> p f s", f=3)[:, fi, :],
                                        self32r[0:16, :])
                                    ostore(e, q2_hbm, q2sb[:])

        with (
            nc.sbuf_tensor("k2sb", [128, NC * 2048], BF16, side="right") as k2sb,
            nc.sbuf_tensor("v2sb", [128, NT * 1024], BF16, side="right") as v2sb,
        ):
            b_ld = {}
            with nc.Block(no_gpsimd_drain=True) as blk:
                @blk.scalar
                def _(e):
                    e.wait_ge(s_st, cnt["s_st"])
                    INC(e.dma_start(out=k2sb[:, 0:8192],
                                    in_=k2_hbm[:, 0:8192]), "s_tr", 16)
                    b_ld["k2"] = cnt["s_tr"]

                @blk.sync
                def _(e):
                    e.wait_ge(s_st, cnt["s_st"])
                    INC(e.dma_start(out=k2sb[:, 8192:16384],
                                    in_=k2_hbm[:, 8192:16384]), "s_x", 16)
                    b_ld["k2x"] = cnt["s_x"]

                @blk.gpsimd
                def _(e):
                    e.wait_ge(s_st, cnt["s_st"])
                    INC(e.dma_start(out=v2sb[:], in_=v2_hbm), "s_w", 16)
                    b_ld["v2"] = cnt["s_w"]

            with nc.sbuf_tensor("sq2", [128, 2 * 2048], BF16,
                                side="right") as sq2:
                bA, bB = cnt["sA"], cnt["sB"]
                with nc.Block(no_gpsimd_drain=True) as blk:
                    @blk.vector
                    def _(e, bA=bA, bB=bB):
                        for i, src in enumerate((q2sb, k2sb)):
                            for dc in range(NC):
                                j = i * NC + dc
                                if i == 1 and dc == 0:
                                    e.wait_ge(s_tr, b_ld["k2"])
                                if i == 1 and dc == 4:
                                    e.wait_ge(s_x, b_ld["k2x"])
                                if j >= 2:
                                    e.wait_ge(sB, bB + j - 1)
                                INC(e.tensor_mul(
                                    sq2[:, (j % 2)*2048:(j % 2 + 1)*2048],
                                    src[:, dc*2048:(dc+1)*2048],
                                    src[:, dc*2048:(dc+1)*2048]), "sA")

                    @blk.tensor
                    def _(e, bA=bA):
                        for i in range(2):
                            ps = pa if i == 0 else pb
                            for dc in range(NC):
                                j = i * NC + dc
                                e.wait_ge(sA, bA + j + 1)
                                last = None
                                for gq in range(4):
                                    last = e.matmul(
                                        ps[0:1, gq*512:(gq+1)*512], onescol[:],
                                        sq2[:, (j % 2)*2048 + gq*512:
                                            (j % 2)*2048 + (gq+1)*512],
                                        start=(dc == 0), stop=(dc == NC - 1))
                                INC(last, "sB")

                b2A, b2B, b2C = cnt["sA"], cnt["sB"], cnt["sC"]
                with nc.Block(no_gpsimd_drain=True) as blk:
                    @blk.vector
                    def _(e, b2B=b2B):
                        e.reduce_max(tiny[0:1, 6:7], pa[0:1, :], axis=AX.X)
                        INC(e.reduce_max(tiny[0:1, 7:8], pb[0:1, :],
                                         axis=AX.X), "sA")
                        e.wait_ge(sB, b2B + 1)
                        INC(e.tensor_mul(tiny[0:1, 10:11], tiny[0:1, 8:9],
                                         tiny[0:1, 9:10]), "sB")

                    @blk.scalar
                    def _(e, b2A=b2A, b2B=b2B, b2C=b2C):
                        e.wait_ge(sA, b2A + 1)
                        e.activation(tiny[0:1, 8:9], tiny[0:1, 6:7], AF.Sqrt)
                        INC(e.activation(tiny[0:1, 9:10], tiny[0:1, 7:8],
                                         AF.Sqrt), "sB")
                        e.wait_ge(sB, b2B + 2)
                        INC(e.copy(tinyr[0:1, 2:3], tiny[0:1, 10:11]), "sB")
                        e.wait_ge(sC, b2C + 1)
                        e.activation(bias2[:], pa[:, 16:17], AF.Copy,
                                     scale=-1.0 / 32.0)

                    @blk.tensor
                    def _(e, b2B=b2B):
                        e.wait_ge(sB, b2B + 3)
                        INC(e.matmul(pa[:, 16:17], onesrow[:],
                                     tinyr[0:1, 2:3],
                                     start=True, stop=True), "sC")
                if debug:
                    with nc.Block(no_gpsimd_drain=True) as blk:
                        @blk.gpsimd
                        def _(e):
                            ostore(e, d_bias2, bias2[:])

            with (
                nc.sbuf_tensor("p2sb", [128, 2 * 16 * 512], BF16,
                               side="right") as p2sb,
                nc.sbuf_tensor("o2sb", [128, 2 * NC * 512], F32,
                               side="right") as o2sb,
                nc.sbuf_tensor("r2row", [1, 512], F32, side="right") as r2row,
                nc.sbuf_tensor("r2col", [128, 2], F32, side="right") as r2col,
                nc.sbuf_tensor("xstg", [128, 2 * 1024], F32,
                               side="right") as xstg,
                nc.sbuf_tensor("astg", [128, 2 * 1024], F32,
                               side="right") as astg,
                nc.sbuf_tensor("ysb", [128, 2 * 1024], F32,
                               side="right") as ysb,
            ):
                bA, bB, bC, bD, bE, bF, bG, bH = (
                    cnt["sA"], cnt["sB"], cnt["sC"], cnt["sD"], cnt["sE"],
                    cnt["sF"], cnt["sG"], cnt["sH"])
                bX, bO, bP = cnt["s_x"], cnt["s_out"], cnt.get("s_p", 0)
                with nc.Block(no_gpsimd_drain=True) as blk:
                    @blk.tensor
                    def _(e):
                        for qc in range(4):
                            pslot = (qc % 2) * 16 * 512
                            oslot = (qc % 2) * NC * 512

                            def emit_sc(kt, qc=qc):
                                buf = kt % 2
                                if qc * 16 + kt >= 2:
                                    e.wait_ge(sB, bB + qc * 16 + kt - 1)
                                last = None
                                for dc in range(NC):
                                    last = e.matmul(
                                        pa[:, buf*512:(buf+1)*512],
                                        k2sb[:, dc*2048 + kt*128:
                                             dc*2048 + (kt+1)*128],
                                        q2sb[:, dc*2048 + qc*512:
                                             dc*2048 + (qc+1)*512],
                                        start=(dc == 0), stop=(dc == NC - 1))
                                INC(last, "sA")

                            def emit_sum(kt, qc=qc, pslot=pslot):
                                e.wait_ge(sB, bB + qc * 16 + kt + 1)
                                if kt == 0 and qc > 0:
                                    e.wait_ge(sG, bG + qc)
                                last = e.matmul(
                                    pa[0:1, 1024:1536], onescol[:],
                                    p2sb[:, pslot + kt*512:
                                         pslot + (kt+1)*512],
                                    start=(kt == 0), stop=(kt == NT - 1))
                                if kt == NT - 1:
                                    INC(last, "sC")

                            for kt in range(NT + 1):
                                if kt < NT:
                                    emit_sc(kt)
                                if kt >= 1:
                                    emit_sum(kt - 1)

                            for dc in range(NC):
                                ps2 = (pa if dc % 2 == 0 else pb)
                                if qc == 0 and dc == 0:
                                    e.wait_ge(s_w, b_ld["v2"])
                                if dc >= 2:
                                    e.wait_ge(sD, bD + qc * 8 + dc - 1)
                                last = None
                                for kt in range(NT):
                                    last = e.matmul(
                                        ps2[:, 1536:2048],
                                        v2sb[:, kt*1024 + dc*128:
                                             kt*1024 + (dc+1)*128],
                                        p2sb[:, pslot + kt*512:
                                             pslot + (kt+1)*512],
                                        start=(kt == 0), stop=(kt == NT - 1))
                                INC(last, "sE")

                            for qs in range(4):
                                if qc * 4 + qs >= 1:
                                    e.wait_ge(s_p, bP + qc * 4 + qs)
                                e.wait_ge(sD, bD + qc * 8 + 8)
                                for dc in range(NC):
                                    e.transpose(
                                        pb[:, dc*128:(dc+1)*128],
                                        o2sb[:, oslot + dc*512 + qs*128:
                                             oslot + dc*512 + (qs+1)*128],
                                        ident32[:])
                                e.wait_ge(sG, bG + qc + 1)
                                last = e.matmul(
                                    pb[:, 1024:1025],
                                    r2row[0:1, qs*128:(qs+1)*128],
                                    one11[:], start=True, stop=True)
                                INC(last, "sF")

                    @blk.scalar
                    def _(e):
                        for qc in range(4):
                            pslot = (qc % 2) * 16 * 512
                            oslot = (qc % 2) * NC * 512
                            for kt in range(NT):
                                buf = kt % 2
                                e.wait_ge(sA, bA + qc * 16 + kt + 1)
                                if qc >= 2:
                                    e.wait_ge(sE, bE + 8 * (qc - 1))
                                INC(e.activation(
                                    p2sb[:, pslot + kt*512:
                                         pslot + (kt+1)*512],
                                    pa[:, buf*512:(buf+1)*512],
                                    AF.Exp, bias=bias2[:, 0:1],
                                    scale=1.0 / 32.0), "sB")
                            for dc in range(NC):
                                ps2 = (pa if dc % 2 == 0 else pb)
                                e.wait_ge(sE, bE + qc * 8 + dc + 1)
                                INC(e.copy(
                                    o2sb[:, oslot + dc*512:
                                         oslot + (dc+1)*512],
                                    ps2[:, 1536:2048]), "sD")
                            for qs in range(4):
                                if qc * 4 + qs >= 2:
                                    e.wait_ge(s_p, bP + qc * 4 + qs - 1)
                                e.wait_ge(sF, bF + qc * 4 + qs + 1)
                                INC(e.copy(r2col[:, qs % 2:qs % 2 + 1],
                                           pb[:, 1024:1025]), "sC")

                    @blk.vector
                    def _(e):
                        for qc in range(4):
                            e.wait_ge(sC, bC + qc * 5 + 1)
                            INC(e.reciprocal(r2row[:], pa[0:1, 1024:1536]),
                                "sG")
                            for qs in range(4):
                                slot = (qs % 2) * 1024
                                if qc * 4 + qs >= 2:
                                    e.wait_ge(s_out,
                                              bO + 16 * (qc * 4 + qs - 1))
                                e.wait_ge(sC, bC + qc * 5 + qs + 2)
                                e.wait_ge(s_x, bX + 32 * (qc * 4 + qs + 1))
                                INC(e.scalar_tensor_tensor(
                                    ysb[:, slot:slot+1024], pb[:, 0:1024],
                                    r2col[:, qs % 2:qs % 2 + 1],
                                    astg[:, slot:slot+1024],
                                    ALU.mult, ALU.add), "s_p")
                                INC(e.tensor_add(
                                    ysb[:, slot:slot+1024],
                                    ysb[:, slot:slot+1024],
                                    xstg[:, slot:slot+1024]), "sH")

                    @blk.sync
                    def _(e):
                        for qc in range(4):
                            for qs in range(4):
                                qg = qc * 4 + qs
                                slot = (qs % 2) * 1024
                                if qg >= 2:
                                    e.wait_ge(sH, bH + qg - 1)
                                if qg > 0:
                                    e.wait_ge(s_x, bX + 32 * qg)
                                INC(e.dma_start(
                                    out=xstg[:, slot:slot+1024],
                                    in_=x_h[qg*128:(qg+1)*128, :]), "s_x", 16)
                                INC(e.dma_start(
                                    out=astg[:, slot:slot+1024],
                                    in_=attn_hbm[qg*128:(qg+1)*128, :]),
                                    "s_x", 16)

                    @blk.gpsimd
                    def _(e):
                        for qc in range(4):
                            for qs in range(4):
                                qg = qc * 4 + qs
                                slot = (qs % 2) * 1024
                                e.wait_ge(sH, bH + qg + 1)
                                if qg > 0:
                                    e.wait_ge(s_out, bO + 16 * qg)
                                INC(e.dma_start(
                                    out=y_h[qg*128:(qg+1)*128, :],
                                    in_=ysb[:, slot:slot+1024]), "s_out", 16)

        with nc.Block(no_gpsimd_drain=True) as blk:
            @blk.gpsimd
            def _(e):
                e.wait_ge(s_out, cnt["s_out"])

    return nc


def kernel(**inputs):
    hp = host_prep(inputs)
    x = np.ascontiguousarray(np.asarray(inputs["L0_attn_hidden_states"],
                                        dtype=np.float32))
    nc = build_program(debug=False)
    in_maps = []
    for c in range(B):
        m = {"x": np.ascontiguousarray(x[c])}
        m.update(hp)
        in_maps.append(m)
    res = run_bass_kernel_spmd(nc, in_maps, core_ids=list(range(8)))
    out = np.stack([np.asarray(res.results[c]["y"]) for c in range(B)], axis=0)
    return out.astype(np.float32)
